# version 05
# baseline (speedup 1.0000x reference)
import contextlib
import numpy as np
import ml_dtypes
import concourse.bass as bass
import concourse.mybir as mybir
from concourse.bass_utils import run_bass_kernel_spmd

F32 = mybir.dt.float32
BF16 = mybir.dt.bfloat16
AF = mybir.ActivationFunctionType
ALU = mybir.AluOpType
AX = mybir.AxisListType

B, S, D = 2, 16384, 2048
H, G, HD = 16, 4, 128
DH = 8192
T = S // 4
TC = 512
NCH = T // TC
NSUB = TC // 128
NSUBT = T // 128
NHC = DH // 128
EPS_LN = 1e-5
EPS_A = 1e-6
VW = 136

ENGINES = ("sync", "scalar", "vector", "tensor", "gpsimd")


class Plan:

    def __init__(self, nc, sems):
        self.nc = nc
        self.sems = sems
        self.cnt = {k: 0 for k in sems}
        self.ops = {e: [] for e in ENGINES}
        self.lastw = {e: {} for e in ENGINES}

    def add(self, eng, fn, waits=(), inc=None, inc_by=1):
        ws = []
        for s, v in waits:
            if v <= 0:
                continue
            if v > self.lastw[eng].get(s, -1):
                ws.append((s, v))
                self.lastw[eng][s] = v
        post = None
        if inc is not None:
            self.cnt[inc] += inc_by
            post = self.cnt[inc]
        self.ops[eng].append((fn, tuple(ws), inc, inc_by))
        return post

    def wait_only(self, eng, waits):
        self.add(eng, None, waits=waits)

    def emit(self):
        with self.nc.Block() as block:
            for eng in ENGINES:
                lst = self.ops[eng]
                if not lst:
                    continue

                def body(e, lst=lst):
                    for fn, ws, inc, by in lst:
                        for s, v in ws:
                            e.wait_ge(self.sems[s], v)
                        if fn is None:
                            continue
                        inst = fn(e)
                        if inc is not None:
                            inst.then_inc(self.sems[inc], by)

                getattr(block, eng)(body)
        self.ops = {e: [] for e in ENGINES}

    def barrier(self):
        for eng, q, m in (("sync", "qS", "sC"), ("scalar", "qA", "sS"),
                          ("gpsimd", "qG", "sP")):
            if self.cnt[q] > 0:
                self.cnt[m] += 1
                self.add(eng, (lambda m: lambda e: e.sem_inc(self.sems[m], 1))(m),
                         waits=[(q, self.cnt[q])])
        snap = [(k, v) for k, v in self.cnt.items()
                if v > 0 and k in ("sT", "sV", "sS", "sP", "sC")]
        for e in ENGINES:
            self.wait_only(e, snap)


def build_program(stop_after="P4", debug=False):
    nc = bass.Bass("TRN2", target_bir_lowering=False, debug=False)
    ctx = contextlib.ExitStack()

    x = nc.dram_tensor("x", [T, D], F32, kind="ExternalInput").ap()
    wq = nc.dram_tensor("wq", [128, 16, 16, 128], F32, kind="ExternalInput").ap()
    wk = nc.dram_tensor("wk", [128, 16, 4, 128], F32, kind="ExternalInput").ap()
    wv = nc.dram_tensor("wv", [128, 16, 512], F32, kind="ExternalInput").ap()
    wo = nc.dram_tensor("wo", [128, 16, 2048], F32, kind="ExternalInput").ap()
    wgu = nc.dram_tensor("wgu", [NHC, 128, 2, 16, 128], F32, kind="ExternalInput").ap()
    wd = nc.dram_tensor("wd", [2, NHC, 128, 1024], F32, kind="ExternalInput").ap()
    cosf = nc.dram_tensor("cosf", [128, T], F32, kind="ExternalInput").ap()
    sinf = nc.dram_tensor("sinf", [128, T], F32, kind="ExternalInput").ap()
    trilm = nc.dram_tensor("trilm", [128, 512], F32, kind="ExternalInput").ap()
    ident = nc.dram_tensor("ident", [128, 128], BF16, kind="ExternalInput").ap()
    pmask = nc.dram_tensor("pmask", [128, 8], F32, kind="ExternalInput").ap()
    y = nc.dram_tensor("y", [T, D], F32, kind="ExternalOutput").ap()

    xnfm_h = nc.dram_tensor("xnfm_h", [NCH, 128, 16, TC], BF16).ap()
    qf_h = nc.dram_tensor("qf_h", [NCH, 128, 16, TC], BF16).ap()
    kff_h = nc.dram_tensor("kff_h", [NCH, 128, 4, TC], BF16).ap()
    kft_h = nc.dram_tensor("kft_h", [NCH, 128, NSUB, 4, 128], BF16).ap()
    va_h = nc.dram_tensor("va_h", [NCH, 128, NSUB, 4, VW], BF16).ap()
    ya_h = nc.dram_tensor("ya_h", [NSUBT, 128, 2048], F32).ap()
    gat_h = nc.dram_tensor("gat_h", [NHC, NCH, 128, TC], BF16).ap()
    cc_in_t = nc.dram_tensor("cc_in", [4 * 128, VW], F32)
    cc_out_t = nc.dram_tensor("cc_out", [8 * 4 * 128, VW], F32)

    dbg = {}
    if debug:
        def dbg_out(name, shape, dt=F32):
            dbg[name] = nc.dram_tensor(name, shape, dt, kind="ExternalOutput").ap()
        if stop_after == "P1":
            dbg_out("d_xnfm", [NCH, 128, 16, TC], BF16)
            dbg_out("d_qf", [NCH, 128, 16, TC], BF16)
            dbg_out("d_kff", [NCH, 128, 4, TC], BF16)
            dbg_out("d_kft", [NCH, 128, NSUB, 4, 128], BF16)
            dbg_out("d_va", [NCH, 128, NSUB, 4, VW], BF16)
            dbg_out("d_sloc", [128, 4, VW], F32)
        if stop_after == "CC":
            dbg_out("d_sinit", [128, 4, VW], F32)
        if stop_after == "P2":
            dbg_out("d_ya", [NSUBT, 128, 2048], F32)
            dbg_out("d_sinit", [128, 4, VW], F32)
        if stop_after == "P3":
            dbg_out("d_gat", [NHC, NCH, 128, TC], BF16)

    sems = {}
    for sname in ("sT", "sV", "sS", "sP", "sC", "qS", "qA", "qG",
                  "cst", "wwq", "wwo", "wds", "glx",
                  "xl0", "xl1", "cs0", "cs1", "ls0", "ls1",
                  "qfw0", "qfw1", "wg0", "wg1", "wg2",
                  "gwa", "gwb", "gc0", "gc1", "ld40", "ld41",
                  "xnl0", "xnl1", "xnl2", "xnl3", "xnl4", "xnl5", "xnl6", "xnl7", "kfw"):
        sems[sname] = ctx.enter_context(nc.semaphore(sname))
    P = Plan(nc, sems)

    per = contextlib.ExitStack()
    ident_sb = per.enter_context(nc.sbuf_tensor("ident_sb", [128, 128], BF16))
    tril_sb = per.enter_context(nc.sbuf_tensor("tril_sb", [128, 512], F32))
    pm_sb = per.enter_context(nc.sbuf_tensor("pm_sb", [128, 8], F32))
    S_sb = per.enter_context(nc.sbuf_tensor("S_sb", [128, 4, VW], F32))
    S_bf = per.enter_context(nc.sbuf_tensor("S_bf", [128, 4, VW], BF16))

    ph = contextlib.ExitStack()
    sb = lambda n, shp, dt: ph.enter_context(nc.sbuf_tensor(n, shp, dt))
    ps = lambda n, shp, dt: ph.enter_context(nc.psum_tensor(n, shp, dt))

    xsb = sb("xsb", [128, 2, D], F32)
    sqs = sb("sqs", [128, D], F32)
    lnA = sb("lnA", [128, 2, 8], F32)
    xntm = sb("xntm", [128, 2, D], BF16)
    xnfm = sb("xnfm", [128, 2, 16, TC], BF16)
    cossb = sb("cossb", [128, 2, 2, TC], F32)
    wq_sb = sb("wq_sb", [128, 16, 16, 128], BF16)
    wk_sb = sb("wk_sb", [128, 16, 4, 128], BF16)
    wv_sb = sb("wv_sb", [128, 16, 512], BF16)
    qf_sb = sb("qf_sb", [128, 2, TC], BF16)
    kff_sb = sb("kff_sb", [128, 4, TC], BF16)
    kft_sb = sb("kft_sb", [128, NSUB, 4, 128], BF16)
    va_sb = sb("va_sb", [128, NSUB, 4, VW], BF16)
    S_loc = sb("S_loc", [128, 4, VW], F32)
    ra = sb("ra", [128, TC], F32)
    rb = sb("rb", [128, TC], F32)
    rc = sb("rc", [128, TC], F32)
    rm = sb("rm", [128, TC], F32)
    rr = sb("rr", [128, TC], F32)
    re = sb("re", [128, TC], F32)

    psq = ps("psq", [128, 2, 512], F32)
    psv = ps("psv", [128, 2, 512], F32)
    pT = ps("pT", [128, 2, 1024], BF16)
    pstA = ps("pstA", [128, 512], F32)
    pstB = ps("pstB", [128, 512], F32)

    P.add("sync", lambda e: e.dma_start(out=ident_sb[:], in_=ident[:, :]), inc="cst", inc_by=16)
    P.add("sync", lambda e: e.dma_start(out=tril_sb[:], in_=trilm[:, :]), inc="cst", inc_by=16)
    P.add("sync", lambda e: e.dma_start(out=pm_sb[:], in_=pmask[:, :]), inc="cst", inc_by=16)
    c_consts = P.cnt["cst"]
    P.add("gpsimd", lambda e: e.dma_start(out=wq_sb[:], in_=wq[:, :, :, :]), inc="wwq", inc_by=16)
    P.add("gpsimd", lambda e: e.dma_start(out=wk_sb[:], in_=wk[:, :, :, :]), inc="wwq", inc_by=16)
    wv_ld = P.add("gpsimd", lambda e: e.dma_start(out=wv_sb[:], in_=wv[:, :, :]), inc="wwq", inc_by=16)
    P.add("vector", lambda e: e.memset(S_loc[:], 0.0), inc="sV")
    vz = P.add("vector", lambda e: e.memset(va_sb[:, :, :, 128:VW], 0.0), inc="sV")
    va_ones = P.add("vector", lambda e: e.memset(va_sb[:, :, :, 128:129], 1.0),
                    waits=[("sV", vz)], inc="sV")

    xload = [0] * NSUBT
    xn_done = [0] * NSUBT
    sq_done = [0] * NSUBT
    xnc_done = [0] * NCH
    xnfm_dma = [0] * NCH
    cos_ld = [0] * NCH
    cos_sem = [""] * NCH
    tr_last_of_sub = {}
    vmm_last = [0] * NCH
    kw_q = [0] * NCH
    kff_dma = [0] * NCH
    st_addv = {}
    psv_cp = {}
    qf_w = {}
    b2_v = {}
    tcopy = {}
    tr_cnt = 0
    hg_idx = 0
    rope_state = {"sr": 0, "se": 0}

    def plan_rope_fmap(ci, slot_par, final_fn, final_waits):
        cs = cossb[:, ci % 2, 0, :]
        sn = cossb[:, ci % 2, 1, :]
        mmv = P.cnt["sT"]
        P.add("vector", lambda e: e.tensor_tensor(ra[:], psq[:, slot_par, :], cs, ALU.mult),
              waits=[("sT", mmv), (cos_sem[ci], cos_ld[ci]), ("sS", rope_state["sr"])], inc="sV")
        P.add("vector", lambda e: e.tensor_tensor(
            rb[0:64, :], psq[64:128, slot_par, :], sn[0:64, :], ALU.mult), inc="sV")
        b2 = P.add("vector", lambda e: e.tensor_tensor(
            rb[64:128, :], psq[0:64, slot_par, :], sn[64:128, :], ALU.mult), inc="sV")
        vc = P.add("vector", lambda e: e.tensor_tensor(rc[:], ra[:], rb[:], ALU.add),
                   waits=[("sS", rope_state["se"])], inc="sV")
        sr = P.add("scalar", lambda e: e.activation(rr[:], rc[:], AF.Relu),
                   waits=[("sV", vc)], inc="sS")
        vd = P.add("vector", lambda e: e.tensor_tensor(rm[:], rc[:], rr[:], ALU.subtract),
                   waits=[("sS", sr)], inc="sV")
        se = P.add("scalar", lambda e: e.activation(re[:], rm[:], AF.Exp),
                   waits=[("sV", vd)], inc="sS")
        ve = P.add("vector", final_fn, waits=[("sS", se)] + list(final_waits), inc="sV")
        rope_state["sr"], rope_state["se"] = sr, se
        return b2, ve

    for ci in range(NCH):
        cp = ci % 2
        w = []
        if ci >= 2:
            w = [("sV", b2_v.get((ci - 2) * 20 + 19, 0))]
        csem = "cs0" if ci % 2 == 0 else "cs1"
        P.add("sync", lambda e, ci=ci: e.dma_start(
            out=cossb[:, ci % 2, 0, :], in_=cosf[:, ci * TC:(ci + 1) * TC]),
            waits=w, inc=csem, inc_by=16)
        cos_ld[ci] = P.add("sync", lambda e, ci=ci: e.dma_start(
            out=cossb[:, ci % 2, 1, :], in_=sinf[:, ci * TC:(ci + 1) * TC]),
            inc=csem, inc_by=16)
        cos_sem[ci] = csem

        for si in range(NSUB):
            u = ci * NSUB + si
            p = u % 2
            w = []
            if u >= 2:
                w = [("sV", xn_done[u - 2]), ("sS", sq_done[u - 2])]
            xsem = "xl0" if p == 0 else "xl1"
            xload[u] = P.add("sync", lambda e, u=u, p=p: e.dma_start(
                out=xsb[:, p, :], in_=x[u * 128:(u + 1) * 128, :]),
                waits=w, inc=xsem, inc_by=16)
            r1 = P.add("vector", lambda e, p=p: e.reduce_sum(
                lnA[:, p, 0:1], xsb[:, p, :], axis=AX.X),
                waits=[(xsem, xload[u])], inc="sV")
            sq_done[u] = P.add("scalar", lambda e, p=p: e.activation(
                sqs[:], xsb[:, p, :], AF.Square, accum_out=lnA[:, p, 1:2]),
                waits=[(xsem, xload[u])], inc="sS")
            v1 = P.add("vector", lambda e, p=p: e.tensor_scalar(
                lnA[:, p, 2:3], lnA[:, p, 0:1], 1.0 / D, None, ALU.mult),
                waits=[("sV", r1)], inc="sV")
            v2 = P.add("vector", lambda e, p=p: e.tensor_scalar(
                lnA[:, p, 3:4], lnA[:, p, 1:2], 1.0 / D, None, ALU.mult),
                waits=[("sS", sq_done[u])], inc="sV")
            v3 = P.add("vector", lambda e, p=p: e.tensor_tensor(
                lnA[:, p, 7:8], lnA[:, p, 2:3], lnA[:, p, 2:3], ALU.mult),
                waits=[("sV", v1)], inc="sV")
            v4 = P.add("vector", lambda e, p=p: e.scalar_tensor_tensor(
                lnA[:, p, 4:5], lnA[:, p, 3:4], EPS_LN, lnA[:, p, 7:8],
                ALU.add, ALU.subtract), waits=[("sV", v3)], inc="sV")
            sl_ = P.add("scalar", lambda e, p=p: e.activation(
                lnA[:, p, 7:8], lnA[:, p, 4:5], AF.Ln),
                waits=[("sV", v4)], inc="sS")
            s2 = P.add("scalar", lambda e, p=p: e.activation(
                lnA[:, p, 5:6], lnA[:, p, 7:8], AF.Exp, scale=-0.5),
                waits=[("sS", sl_)], inc="sS")
            v5 = P.add("vector", lambda e, p=p: e.tensor_tensor(
                lnA[:, p, 6:7], lnA[:, p, 2:3], lnA[:, p, 5:6], ALU.mult),
                waits=[("sS", s2)], inc="sV")
            w = [("sV", v5)]
            if u >= 2:
                w.append(("sT", tr_last_of_sub.get(u - 2, 0)))
            xn_done[u] = P.add("vector", lambda e, p=p: e.tensor_scalar(
                xntm[:, p, :], xsb[:, p, :], lnA[:, p, 5:6], lnA[:, p, 6:7],
                ALU.mult, ALU.subtract), waits=w, inc="sV")
            for dc in range(16):
                tp = tr_cnt % 2
                w = [("sV", xn_done[u]), ("cst", c_consts)]
                if tr_cnt >= 2:
                    w.append(("sS", tcopy[tr_cnt - 2]))
                tt = P.add("tensor", lambda e, p=p, dc=dc, tp=tp: e.transpose(
                    pT[:, tp, 0:128], xntm[:, p, dc * 128:(dc + 1) * 128], ident_sb[:]),
                    waits=w, inc="sT")
                w2 = [("sT", tt)]
                if si == 0 and dc == 0 and ci >= 2:
                    w2.append(("qA", kw_q[ci - 1]))
                tcopy[tr_cnt] = P.add("scalar", lambda e, ci=ci, dc=dc, si=si, tp=tp: e.copy(
                    xnfm[:, ci % 2, dc, si * 128:(si + 1) * 128], pT[:, tp, 0:128]),
                    waits=w2, inc="sS")
                tr_cnt += 1
            tr_last_of_sub[u] = P.cnt["sT"]
        xnc_done[ci] = P.cnt["sS"]
        xnfm_dma[ci] = P.add("scalar", lambda e, ci=ci: e.dma_start(
            out=xnfm_h[ci, :, :, :], in_=xnfm[:, ci % 2, :, :]),
            waits=[("sS", xnc_done[ci])], inc="qA", inc_by=16)

        for h in range(16):
            qh_idx = ci * 16 + h
            slot = hg_idx % 2
            qslot = qh_idx % 2
            w = [("sS", xnc_done[ci]), ("wwq", wv_ld)]
            if hg_idx >= 2:
                w.append(("sV", b2_v.get(hg_idx - 2, 0)))
            for dc in range(16):
                ww = w if dc == 0 else ()
                P.add("tensor", lambda e, ci=ci, h=h, dc=dc, slot=slot: e.matmul(
                    psq[:, slot, :], wq_sb[:, dc, h, :], xnfm[:, ci % 2, dc, :],
                    start=(dc == 0), stop=(dc == 15)),
                    waits=ww, inc=("sT" if dc == 15 else None))
            fw = [(("qfw0" if qslot == 0 else "qfw1"), qf_w[qh_idx - 2])] if qh_idx >= 2 else []
            b2, ve = plan_rope_fmap(
                ci, slot,
                (lambda qslot: lambda e: e.tensor_tensor(
                    qf_sb[:, qslot, :], re[:], rr[:], ALU.add))(qslot), fw)
            b2_v[hg_idx] = b2
            qf_w[qh_idx] = P.add("scalar", lambda e, ci=ci, h=h, qslot=qslot: e.dma_start(
                out=qf_h[ci, :, h, :], in_=qf_sb[:, qslot, :]),
                waits=[("sV", ve)], inc=("qfw0" if qslot == 0 else "qfw1"), inc_by=16)
            hg_idx += 1

        for g in range(4):
            slot = hg_idx % 2
            w = [("sS", xnc_done[ci]), ("wwq", wv_ld)]
            if hg_idx >= 2:
                w.append(("sV", b2_v.get(hg_idx - 2, 0)))
            for dc in range(16):
                ww = w if dc == 0 else ()
                P.add("tensor", lambda e, ci=ci, g=g, dc=dc, slot=slot: e.matmul(
                    psq[:, slot, :], wk_sb[:, dc, g, :], xnfm[:, ci % 2, dc, :],
                    start=(dc == 0), stop=(dc == 15)),
                    waits=ww, inc=("sT" if dc == 15 else None))
            kwf = [("kfw", kff_dma[ci - 1])] if (g == 0 and ci > 0) else []
            b2, ve = plan_rope_fmap(
                ci, slot,
                (lambda g: lambda e: e.tensor_tensor(
                    kff_sb[:, g, :], re[:], rr[:], ALU.add))(g), kwf)
            b2_v[hg_idx] = b2
            hg_idx += 1
        kff_done = P.cnt["sV"]
        kff_dma[ci] = P.add("scalar", lambda e, ci=ci: e.dma_start(
            out=kff_h[ci, :, :, :], in_=kff_sb[:, :, :]),
            waits=[("sV", kff_done)], inc="kfw", inc_by=16)
        for g in range(4):
            for si in range(NSUB):
                tp = tr_cnt % 2
                w = [("sV", kff_done)]
                if tr_cnt >= 2:
                    w.append(("sS", tcopy[tr_cnt - 2]))
                tt = P.add("tensor", lambda e, g=g, si=si, tp=tp: e.transpose(
                    pT[:, tp, 0:128], kff_sb[:, g, si * 128:(si + 1) * 128], ident_sb[:]),
                    waits=w, inc="sT")
                w2 = [("sT", tt)]
                if g == 0 and si == 0 and ci > 0:
                    w2.append(("qA", P.cnt["qA"]))
                tcopy[tr_cnt] = P.add("scalar", lambda e, g=g, si=si, tp=tp: e.copy(
                    kft_sb[:, si, g, :], pT[:, tp, 0:128]),
                    waits=w2, inc="sS")
                tr_cnt += 1
        kft_done = P.cnt["sS"]
        P.add("scalar", lambda e, ci=ci: e.dma_start(
            out=kft_h[ci, :, :, :, :], in_=kft_sb[:, :, :, :]),
            waits=[("sS", kft_done)], inc="qA", inc_by=16)

        for si in range(NSUB):
            gv = ci * NSUB + si
            vs = gv % 2
            w = [("sS", xnc_done[ci]), ("wwq", wv_ld), ("sV", va_ones)]
            if gv >= 2:
                w.append(("sS", psv_cp.get(gv - 2, 0)))
            for dc in range(16):
                ww = w if dc == 0 else ()
                P.add("tensor", lambda e, ci=ci, si=si, dc=dc, vs=vs: e.matmul(
                    psv[:, vs, :], xnfm[:, ci % 2, dc, si * 128:(si + 1) * 128],
                    wv_sb[:, dc, :], start=(dc == 0), stop=(dc == 15)),
                    waits=ww, inc=("sT" if dc == 15 else None))
            mv = P.cnt["sT"]
            last = 0
            for g in range(4):
                w2 = [("sT", mv)]
                if si == 0 and g == 0 and ci > 0:
                    w2.append(("qA", P.cnt["qA"]))
                last = P.add("scalar", lambda e, si=si, g=g, vs=vs: e.copy(
                    va_sb[:, si, g, 0:128], psv[:, vs, g * 128:(g + 1) * 128]),
                    waits=w2, inc="sS")
            psv_cp[gv] = last
        va_all = P.cnt["sS"]
        P.add("scalar", lambda e, ci=ci: e.dma_start(
            out=va_h[ci, :, :, :, :], in_=va_sb[:, :, :, :]),
            waits=[("sS", va_all)], inc="qA", inc_by=16)
        kw_q[ci] = P.cnt["qA"]

        for si in range(NSUB):
            for g in range(4):
                gi = ci * 16 + si * 4 + g
                pst = pstA if gi % 2 == 0 else pstB
                w = [("sS", va_all), ("sS", kft_done)]
                if gi >= 2:
                    w.append(("sV", st_addv.get(gi - 2, 0)))
                ms = P.add("tensor", lambda e, si=si, g=g, pst=pst: e.matmul(
                    pst[:, 0:129], kft_sb[:, si, g, :], va_sb[:, si, g, 0:129],
                    start=True, stop=True), waits=w, inc="sT")
                st_addv[gi] = P.add("vector", lambda e, g=g, pst=pst: e.tensor_tensor(
                    S_loc[:, g, 0:129], S_loc[:, g, 0:129], pst[:, 0:129], ALU.add),
                    waits=[("sT", ms)], inc="sV")
        vmm_last[ci] = P.cnt["sT"]

    p1_V = P.cnt["sV"]

    if debug and stop_after == "P1":
        w = [("sV", p1_V), ("sS", P.cnt["sS"]), ("qA", P.cnt["qA"])]
        for name, src in (("d_sloc", None),):
            pass
        P.add("gpsimd", lambda e: e.dma_start(out=dbg["d_sloc"][:, :, :], in_=S_loc[:]),
              waits=w, inc="qG", inc_by=16)
        for name, src in (("d_xnfm", xnfm_h), ("d_qf", qf_h), ("d_kff", kff_h)):
            P.add("gpsimd", lambda e, name=name, src=src: e.dma_start(
                out=dbg[name][:, :, :, :], in_=src[:, :, :, :]), inc="qG", inc_by=16)
        for name, src in (("d_kft", kft_h), ("d_va", va_h)):
            P.add("gpsimd", lambda e, name=name, src=src: e.dma_start(
                out=dbg[name][:, :, :, :, :], in_=src[:, :, :, :, :]), inc="qG", inc_by=16)
        P.wait_only("gpsimd", [("qG", P.cnt["qG"])])
        P.emit()
        ph.close(); per.close(); ctx.close()
        return nc, dbg

    P.add("gpsimd", lambda e: e.dma_start(
        out=cc_in_t.ap().rearrange("(g p) f -> p g f", p=128), in_=S_loc[:, :, :]),
        waits=[("sV", p1_V)], inc="qG", inc_by=16)
    cw = P.cnt["qG"]
    P.add("gpsimd", lambda e: e.collective_compute(
        "AllGather", ALU.bypass, replica_groups=[list(range(8))],
        ins=[cc_in_t.ap().opt()], outs=[cc_out_t.ap().opt()]),
        waits=[("qG", cw)], inc="sC")

    P.emit()
    ph.close()

    ph = contextlib.ExitStack()
    sb = lambda n, shp, dt: ph.enter_context(nc.sbuf_tensor(n, shp, dt))
    ps = lambda n, shp, dt: ph.enter_context(nc.psum_tensor(n, shp, dt))
    gath_sb = sb("gath_sb", [128, 8, 4, VW], F32)

    P.barrier()
    gl = P.add("gpsimd", lambda e: e.dma_start(
        out=gath_sb[:, :, :, :],
        in_=cc_out_t.ap().rearrange("(r g p) f -> p r g f", p=128, g=4)),
        waits=[("sC", P.cnt["sC"])], inc="glx", inc_by=16)
    pv = P.add("vector", lambda e: e.memset(S_sb[:], 0.0), waits=[("glx", gl)], inc="sV")
    for r in range(8):
        pv = P.add("vector", lambda e, r=r: e.scalar_tensor_tensor(
            S_sb.ap().rearrange("p g f -> p (g f)"),
            gath_sb[:, r, :, :].rearrange("p g f -> p (g f)"),
            pm_sb[:, r:r + 1],
            S_sb.ap().rearrange("p g f -> p (g f)"),
            ALU.mult, ALU.add),
            waits=[("cst", c_consts), ("sV", pv)], inc="sV")
    sbf_init = P.add("vector", lambda e: e.tensor_copy(S_bf[:], S_sb[:]),
                     waits=[("sV", pv)], inc="sV")

    if debug and stop_after == "CC":
        P.add("gpsimd", lambda e: e.dma_start(out=dbg["d_sinit"][:, :, :], in_=S_sb[:]),
              waits=[("sV", sbf_init)], inc="qG", inc_by=16)
        P.wait_only("gpsimd", [("qG", P.cnt["qG"])])
        P.emit()
        ph.close(); per.close(); ctx.close()
        return nc, dbg

    qbuf = sb("qbuf", [128, 2, 16, TC], BF16)
    kfb = sb("kfb", [128, 2, 4, TC], BF16)
    ktb = sb("ktb", [128, 2, NSUB, 4, 128], BF16)
    vab = sb("vab", [128, 2, NSUB, 4, VW], BF16)
    wo_sb = sb("wo_sb", [128, 16, 2048], BF16)
    abf = sb("abf", [128, 2, 4, 128], BF16)
    att_tm = sb("att_tm", [128, 2, 128], BF16)
    afm = sb("afm", [128, 2, 16, 128], BF16)
    st2 = sb("st2", [128, 2, 2], F32)
    y_sb = sb("y_sb", [128, 2, 2048], F32)

    pscr = ps("pscr", [128, 512], F32)
    pRp = ps("pRp", [128, 2, 512], F32)
    pTr = ps("pTr", [128, 1024], BF16)
    pW = ps("pW", [128, 2048], F32)

    wo_ld = P.add("gpsimd", lambda e: e.dma_start(out=wo_sb[:], in_=wo[:, :, :]),
                  inc="wwo", inc_by=16)

    ld2 = [0] * NCH
    ld2_sem = [""] * NCH
    msk_v = {}
    nrm_v = {}
    afc_s = {}
    yev_s = [0] * NSUBT
    yaw_q = [0] * NSUBT
    stadd_v = [0] * (NSUBT + 1)
    stadd_v[0] = sbf_init
    sbf_u = [0] * (NSUBT + 1)
    sbf_u[0] = sbf_init
    mmst_t = [0] * NSUBT

    for ci in range(NCH):
        w = []
        if ci >= 2:
            w.append(("sT", mmst_t[(ci - 2) * NSUB + NSUB - 1]))
        lsem = "ls0" if ci % 2 == 0 else "ls1"
        for fn in (
            lambda e, ci=ci: e.dma_start(out=qbuf[:, ci % 2], in_=qf_h[ci]),
            lambda e, ci=ci: e.dma_start(out=kfb[:, ci % 2], in_=kff_h[ci]),
            lambda e, ci=ci: e.dma_start(out=ktb[:, ci % 2], in_=kft_h[ci]),
            lambda e, ci=ci: e.dma_start(out=vab[:, ci % 2], in_=va_h[ci]),
        ):
            ld2[ci] = P.add("sync", fn, waits=w, inc=lsem, inc_by=16)
            w = []
        ld2_sem[ci] = lsem

        for si in range(NSUB):
            u = ci * NSUB + si
            up = u % 2
            sl = slice(si * 128, (si + 1) * 128)
            mmW = 0

            def plan_tr_afc(h, u=u, up=up):
                hu = u * 16 + h
                hp = h % 2
                w = [("sV", nrm_v[hu])]
                if hu >= 1:
                    w.append(("sS", afc_s.get(hu - 1, 0)))
                tt = P.add("tensor", lambda e, hp=hp: e.transpose(
                    pTr[:, 0:128], att_tm[:, hp, :], ident_sb[:]),
                    waits=w, inc="sT")
                afc_s[hu] = P.add("scalar", lambda e, up=up, h=h: e.copy(
                    afm[:, up, h, :], pTr[:, 0:128]),
                    waits=[("sT", tt)], inc="sS")

            def plan_wo(h, u=u, up=up):
                nonlocal mmW
                w = [("wwo", wo_ld), ("sS", afc_s[u * 16 + h]), ("sV", stadd_v[u])]
                if h == 0 and u >= 1:
                    w.append(("sS", yev_s[u - 1]))
                first = True
                for os_ in range(4):
                    ww = w if first else ()
                    first = False
                    mmW = P.add("tensor", lambda e, up=up, h=h, os_=os_: e.matmul(
                        pW[:, os_ * 512:(os_ + 1) * 512], afm[:, up, h, :],
                        wo_sb[:, h, os_ * 512:(os_ + 1) * 512],
                        start=(h == 0), stop=(h == 15), skip_group_check=True),
                        waits=ww, inc=("sT" if (h == 15 and os_ == 3) else None))

            for h in range(16):
                g = h // 4
                hp = h % 2
                hu = u * 16 + h
                if h % 4 == 0:
                    w = [(ld2_sem[ci], ld2[ci])]
                    qk = u * 4 + g
                    if qk >= 1:
                        w.append(("sV", msk_v.get(qk - 1, 0)))
                    sc = P.add("tensor", lambda e, ci=ci, g=g, sl=sl: e.matmul(
                        pscr[:, 0:512], kfb[:, ci % 2, g, sl],
                        qbuf[:, ci % 2, 4 * g:4 * g + 4, sl],
                        start=True, stop=True), waits=w, inc="sT")
                    msk_v[qk] = P.add("vector", lambda e, hp2=(g % 2): e.tensor_tensor(
                        abf[:, hp2, :, :], pscr.ap().rearrange("p (a f) -> p a f", a=4),
                        tril_sb.ap().rearrange("p (a f) -> p a f", a=4), ALU.mult),
                        waits=[("sT", sc)], inc="sV")
                w = [("sV", msk_v[u * 4 + g]), ("sV", sbf_u[u])]
                if hu >= 2:
                    w.append(("sV", nrm_v.get(hu - 2, 0)))
                P.add("tensor", lambda e, ci=ci, si=si, g=g, h=h, hp=hp: e.matmul(
                    pRp[:, hp, 0:129], abf[:, g % 2, h % 4, :],
                    vab[:, ci % 2, si, g, 0:129],
                    start=True, stop=False), waits=w)
                au = P.add("tensor", lambda e, ci=ci, h=h, g=g, hp=hp, sl=sl: e.matmul(
                    pRp[:, hp, 0:129], qbuf[:, ci % 2, h, sl], S_bf[:, g, 0:129],
                    start=False, stop=True), inc="sT")
                n1 = P.add("vector", lambda e, hp=hp: e.tensor_scalar(
                    st2[:, hp, 0:1], pRp[:, hp, 128:129], EPS_A, None, ALU.add),
                    waits=[("sT", au)], inc="sV")
                n2 = P.add("vector", lambda e, hp=hp: e.reciprocal(
                    st2[:, hp, 1:2], st2[:, hp, 0:1]), waits=[("sV", n1)], inc="sV")
                nrm_v[hu] = P.add("vector", lambda e, hp=hp: e.tensor_scalar(
                    att_tm[:, hp, :], pRp[:, hp, 0:128], st2[:, hp, 1:2], None, ALU.mult),
                    waits=[("sV", n2)], inc="sV")
                if h >= 1:
                    plan_tr_afc(h - 1)
                if h >= 2:
                    plan_wo(h - 2)
            plan_tr_afc(15)
            plan_wo(14)
            plan_wo(15)
            w = [("sT", mmW)]
            if u >= 2:
                w.append(("qA", P.cnt["qA"]))
            yev_s[u] = P.add("scalar", lambda e, up=up: e.copy(
                y_sb[:, up, :], pW[:]), waits=w, inc="sS")
            yaw_q[u] = P.add("scalar", lambda e, u=u, up=up: e.dma_start(
                out=ya_h[u], in_=y_sb[:, up, :]),
                waits=[("sS", yev_s[u])], inc="qA", inc_by=16)
            w = [("sS", yev_s[u]), (ld2_sem[ci], ld2[ci])]
            mmst = 0
            for g in range(4):
                ww = w if g == 0 else ()
                mmst = P.add("tensor", lambda e, ci=ci, si=si, g=g: e.matmul(
                    pW[:, g * 512:g * 512 + 129], ktb[:, ci % 2, si, g, :],
                    vab[:, ci % 2, si, g, 0:129], start=True, stop=True),
                    waits=ww, inc=("sT" if g == 3 else None))
            mmst_t[u] = mmst
            for g in range(4):
                P.add("vector", lambda e, g=g: e.tensor_tensor(
                    S_sb[:, g, 0:129], S_sb[:, g, 0:129],
                    pW[:, g * 512:g * 512 + 129], ALU.add),
                    waits=[("sT", mmst)], inc="sV")
            stadd_v[u + 1] = P.cnt["sV"]
            sbf_u[u + 1] = P.add("vector", lambda e: e.tensor_copy(S_bf[:], S_sb[:]),
                                 waits=[("sV", stadd_v[u + 1])], inc="sV")

    if debug and stop_after == "P2":
        w = [("qA", P.cnt["qA"]), ("sV", P.cnt["sV"])]
        P.add("gpsimd", lambda e: e.dma_start(out=dbg["d_ya"][:, :, :], in_=ya_h[:, :, :]),
              waits=w, inc="qG", inc_by=16)
        P.add("gpsimd", lambda e: e.dma_start(out=dbg["d_sinit"][:, :, :], in_=S_sb[:]),
              inc="qG", inc_by=16)
        P.wait_only("gpsimd", [("qG", P.cnt["qG"])])
        P.emit()
        ph.close(); per.close(); ctx.close()
        return nc, dbg

    P.barrier()
    P.emit()
    ph.close()
    ph = contextlib.ExitStack()
    sb = lambda n, shp, dt: ph.enter_context(nc.sbuf_tensor(n, shp, dt))
    ps = lambda n, shp, dt: ph.enter_context(nc.psum_tensor(n, shp, dt))

    xnr = sb("xnr", [128, 16, NCH, TC], BF16)
    wgub = sb("wgub", [128, 3, 2, 16, 128], BF16)
    gsb = sb("gsb", [128, 2, NCH, TC], BF16)
    ssb3 = sb("ssb3", [128, 2, 512], F32)
    pg = ps("pg", [128, 2, 512], F32)
    pu = ps("pu", [128, 2, 512], F32)

    xnl = [0] * NCH
    for ci in range(NCH):
        xnl[ci] = P.add("sync", lambda e, ci=ci: e.dma_start(
            out=xnr[:, :, ci, :], in_=xnfm_h[ci]), inc=f"xnl{ci}", inc_by=16)

    wgl = [0] * NHC
    mmU_t = {}
    gmul_v = {}
    gw_q = [0] * NHC
    for hc in range(NHC):
        w = []
        if hc >= 3:
            w.append(("sT", mmU_t[(hc - 3, NCH - 1)]))
        wgl[hc] = P.add("gpsimd", lambda e, hc=hc: e.dma_start(
            out=wgub[:, hc % 3], in_=wgu[hc]), waits=w, inc=f"wg{hc % 3}", inc_by=16)
        for tt in range(NCH):
            tp = tt % 2
            pidx = (hc, tt - 2) if tt >= 2 else (hc - 1, tt - 2 + NCH)
            w = [(f"wg{hc % 3}", wgl[hc]), (f"xnl{tt}", xnl[tt])]
            if (hc, tt) != (0, 0):
                w.append(("sV", gmul_v.get(pidx, 0)))
            for dc in range(16):
                ww = w if dc == 0 else ()
                P.add("tensor", lambda e, hc=hc, tt=tt, dc=dc, tp=tp: e.matmul(
                    pg[:, tp, :], wgub[:, hc % 3, 0, dc, :], xnr[:, dc, tt, :],
                    start=(dc == 0), stop=(dc == 15)),
                    waits=ww, inc=("sT" if dc == 15 else None))
            mg = P.cnt["sT"]
            for dc in range(16):
                P.add("tensor", lambda e, hc=hc, tt=tt, dc=dc, tp=tp: e.matmul(
                    pu[:, tp, :], wgub[:, hc % 3, 1, dc, :], xnr[:, dc, tt, :],
                    start=(dc == 0), stop=(dc == 15)),
                    inc=("sT" if dc == 15 else None))
            mmU_t[(hc, tt)] = P.cnt["sT"]
            w2 = [("sT", mg)]
            if (hc, tt) != (0, 0):
                w2.append(("sV", gmul_v.get(pidx, 0)))
            sl3 = P.add("scalar", lambda e, tp=tp: e.activation(
                ssb3[:, tp, :], pg[:, tp, :], AF.Silu), waits=w2, inc="sS")
            w2 = [("sS", sl3), ("sT", mmU_t[(hc, tt)])]
            if hc >= 2 and tt == 0:
                w2.append((("gwa" if hc % 2 == 0 else "gwb"), gw_q[hc - 2]))
            gmul_v[(hc, tt)] = P.add("vector", lambda e, hc=hc, tt=tt, tp=tp: e.tensor_tensor(
                gsb[:, hc % 2, tt, :], ssb3[:, tp, :], pu[:, tp, :], ALU.mult),
                waits=w2, inc="sV")
        gw_q[hc] = P.add("scalar", lambda e, hc=hc: e.dma_start(
            out=gat_h[hc].rearrange("a p f -> p a f"), in_=gsb[:, hc % 2]),
            waits=[("sV", gmul_v[(hc, NCH - 1)])],
            inc=("gwa" if hc % 2 == 0 else "gwb"), inc_by=16)

    if debug and stop_after == "P3":
        P.add("gpsimd", lambda e: e.dma_start(
            out=dbg["d_gat"][:, :, :, :], in_=gat_h[:, :, :, :]),
            waits=[("qA", P.cnt["qA"])], inc="qG", inc_by=16)
        P.wait_only("gpsimd", [("qG", P.cnt["qG"])])
        P.emit()
        ph.close(); per.close(); ctx.close()
        return nc, dbg

    P.barrier()
    P.emit()
    ph.close()
    ph = contextlib.ExitStack()
    sb = lambda n, shp, dt: ph.enter_context(nc.sbuf_tensor(n, shp, dt))
    ps = lambda n, shp, dt: ph.enter_context(nc.psum_tensor(n, shp, dt))

    wdb = sb("wdb", [128, NHC, 1024], BF16)
    gcol = sb("gcol", [128, 2, NHC, 128], BF16)
    xb4 = sb("xb4", [128, 2, 1024], F32)
    yab4 = sb("yab4", [128, 2, 1024], F32)
    obA = sb("obA", [128, 2, 1024], F32)
    ob = sb("ob", [128, 2, 1024], F32)
    pY = ps("pY", [128, 2, 1024], F32)

    cmb_v = {}
    out_q = {}
    mm_last = {}
    for half in range(2):
        wdl = [0] * NHC
        for hc in range(NHC):
            w = []
            if half == 1 and hc == 0:
                w.append(("sT", mm_last[(0, NSUBT - 1)]))
            wdl[hc] = P.add("gpsimd", lambda e, half=half, hc=hc: e.dma_start(
                out=wdb[:, hc, :], in_=wd[half, hc]), waits=w, inc="wds", inc_by=16)
        for u4 in range(NSUBT):
            up = u4 % 2
            w = []
            pidx = (half, u4 - 2) if u4 >= 2 else (half - 1, u4 - 2 + NSUBT)
            if (half, u4) != (0, 0) and (half, u4) != (0, 1):
                w.append(("sT", mm_last[pidx]))
            gl = 0
            gsem = "gc0" if up == 0 else "gc1"
            for part in range(8):
                gl = P.add("sync", lambda e, u4=u4, up=up, part=part: e.dma_start(
                    out=gcol[:, up, part * 8:(part + 1) * 8, :],
                    in_=gat_h[part * 8:(part + 1) * 8, u4 // NSUB, :,
                              (u4 % NSUB) * 128:(u4 % NSUB + 1) * 128].rearrange(
                                  "a p f -> p a f")),
                    waits=w, inc=gsem, inc_by=16)
                w = []
            w = []
            if (half, u4) != (0, 0) and (half, u4) != (0, 1):
                w.append(("sV", cmb_v.get(pidx, 0)))
                w.append(("qS", out_q.get(pidx, 0)))
            l4sem = "ld40" if up == 0 else "ld41"
            P.add("sync", lambda e, half=half, u4=u4, up=up: e.dma_start(
                out=xb4[:, up, :],
                in_=x[u4 * 128:(u4 + 1) * 128, half * 1024:(half + 1) * 1024]),
                waits=w, inc=l4sem, inc_by=16)
            yl = P.add("sync", lambda e, half=half, u4=u4, up=up: e.dma_start(
                out=yab4[:, up, :],
                in_=ya_h[u4, :, half * 1024:(half + 1) * 1024]),
                inc=l4sem, inc_by=16)
            first = True
            mmY = 0
            for os_ in range(2):
                for hc in range(NHC):
                    ww = [("wds", wdl[NHC - 1])]
                    if first:
                        ww.append((gsem, gl))
                        if (half, u4) != (0, 0) and (half, u4) != (0, 1):
                            ww.append(("sV", cmb_v.get(pidx, 0)))
                        first = False
                    mmY = P.add("tensor", lambda e, up=up, hc=hc, os_=os_: e.matmul(
                        pY[:, up, os_ * 512:(os_ + 1) * 512],
                        gcol[:, up, hc, :],
                        wdb[:, hc, os_ * 512:(os_ + 1) * 512],
                        start=(hc == 0), stop=(hc == NHC - 1)),
                        waits=ww, inc=("sT" if (hc == NHC - 1 and os_ == 1) else None))
            mm_last[(half, u4)] = mmY
            P.add("vector", lambda e, up=up: e.tensor_tensor(
                obA[:, up, :], pY[:, up, :], xb4[:, up, :], ALU.add),
                waits=[("sT", mmY), (l4sem, yl)], inc="sV")
            cmb_v[(half, u4)] = P.add("vector", lambda e, up=up: e.tensor_tensor(
                ob[:, up, :], obA[:, up, :], yab4[:, up, :], ALU.add), inc="sV")
            out_q[(half, u4)] = P.add("sync", lambda e, half=half, u4=u4, up=up: e.dma_start(
                out=y[u4 * 128:(u4 + 1) * 128, half * 1024:(half + 1) * 1024],
                in_=ob[:, up, :]),
                waits=[("sV", cmb_v[(half, u4)])], inc="qS", inc_by=16)

    P.wait_only("sync", [("qS", out_q[(1, NSUBT - 1)])])
    P.emit()
    ph.close(); per.close(); ctx.close()
    return nc, dbg


def _pack_inputs(L0_attn_x, Wq, Wk, Wv, Wo, Wg, Wu, Wd):
    f32 = np.float32
    xs = np.ascontiguousarray(np.asarray(L0_attn_x, dtype=f32).reshape(B, 4, T, D))
    wq_p = np.ascontiguousarray(
        np.asarray(Wq, f32).reshape(16, 128, 16, 128).transpose(3, 2, 0, 1))
    wk_p = np.ascontiguousarray(
        np.asarray(Wk, f32).reshape(4, 128, 16, 128).transpose(3, 2, 0, 1))
    wv_p = np.ascontiguousarray(
        np.asarray(Wv, f32).T.reshape(16, 128, 512).transpose(1, 0, 2))
    wo_p = np.ascontiguousarray(
        np.asarray(Wo, f32).T.reshape(16, 128, 2048).transpose(1, 0, 2))
    wg_t = np.asarray(Wg, f32).reshape(NHC, 128, 16, 128).transpose(0, 3, 2, 1)
    wu_t = np.asarray(Wu, f32).reshape(NHC, 128, 16, 128).transpose(0, 3, 2, 1)
    wgu_p = np.ascontiguousarray(np.stack([wg_t, wu_t], axis=2))
    wd_p = np.ascontiguousarray(
        np.asarray(Wd, f32).reshape(2, 1024, NHC, 128).transpose(0, 2, 3, 1))

    inv = 1.0 / (10000.0 ** (np.arange(0, HD, 2, dtype=np.float64) / HD))
    tril = np.ascontiguousarray(
        np.tile(np.triu(np.ones((128, 128), f32)), (1, 4)))
    identity = np.ascontiguousarray(np.eye(128, dtype=ml_dtypes.bfloat16))

    in_maps = []
    for core in range(8):
        b, c = divmod(core, 4)
        pos = np.arange(c * T, (c + 1) * T, dtype=np.float64)
        ang = np.outer(pos, inv)
        cosv = np.cos(ang).astype(f32)
        sinv = np.sin(ang).astype(f32)
        cos_fm = np.ascontiguousarray(np.concatenate([cosv, cosv], axis=1).T)
        sin_fm = np.ascontiguousarray(np.concatenate([-sinv, sinv], axis=1).T)
        pm = np.zeros((128, 8), f32)
        for r in range(8):
            rb_, rc_ = divmod(r, 4)
            if rb_ == b and rc_ < c:
                pm[:, r] = 1.0
        in_maps.append({
            "x": np.ascontiguousarray(xs[b, c]),
            "wq": wq_p, "wk": wk_p, "wv": wv_p, "wo": wo_p,
            "wgu": wgu_p, "wd": wd_p,
            "cosf": cos_fm, "sinf": sin_fm,
            "trilm": tril, "ident": identity,
            "pmask": pm,
        })
    return in_maps


def kernel(L0_attn_x, L0_attn_Wq, L0_attn_Wk, L0_attn_Wv, L0_attn_Wo,
           L0_ffn_w_gate, L0_ffn_w_up, L0_ffn_w_down):
    in_maps = _pack_inputs(L0_attn_x, L0_attn_Wq, L0_attn_Wk, L0_attn_Wv,
                           L0_attn_Wo, L0_ffn_w_gate, L0_ffn_w_up, L0_ffn_w_down)
    nc, _ = build_program(stop_after="P4", debug=False)
    res = run_bass_kernel_spmd(nc, in_maps, core_ids=list(range(8)))
    out = np.empty((B, S, D), dtype=np.float32)
    for core in range(8):
        b, c = divmod(core, 4)
        out[b, c * T:(c + 1) * T, :] = res.results[core]["y"]
    return out


# version 09
# speedup vs baseline: 1.0158x; 1.0158x over previous
import contextlib
import numpy as np
import ml_dtypes
import concourse.bass as bass
import concourse.mybir as mybir
from concourse.bass_utils import run_bass_kernel_spmd

F32 = mybir.dt.float32
BF16 = mybir.dt.bfloat16
AF = mybir.ActivationFunctionType
ALU = mybir.AluOpType
AX = mybir.AxisListType

B, S, D = 2, 16384, 2048
H, G, HD = 16, 4, 128
DH = 8192
T = S // 4
TC = 512
NCH = T // TC
NSUB = TC // 128
NSUBT = T // 128
NHC = DH // 128
EPS_LN = 1e-5
EPS_A = 1e-6
VW = 136

ENGINES = ("sync", "scalar", "vector", "tensor", "gpsimd")


class Plan:

    def __init__(self, nc, sems):
        self.nc = nc
        self.sems = sems
        self.cnt = {k: 0 for k in sems}
        self.ops = {e: [] for e in ENGINES}
        self.lastw = {e: {} for e in ENGINES}

    def add(self, eng, fn, waits=(), inc=None, inc_by=1):
        ws = []
        for s, v in waits:
            if v <= 0:
                continue
            if v > self.lastw[eng].get(s, -1):
                ws.append((s, v))
                self.lastw[eng][s] = v
        post = None
        if inc is not None:
            self.cnt[inc] += inc_by
            post = self.cnt[inc]
        self.ops[eng].append((fn, tuple(ws), inc, inc_by))
        return post

    def wait_only(self, eng, waits):
        self.add(eng, None, waits=waits)

    def emit(self):
        with self.nc.Block() as block:
            for eng in ENGINES:
                lst = self.ops[eng]
                if not lst:
                    continue

                def body(e, lst=lst):
                    for fn, ws, inc, by in lst:
                        for s, v in ws:
                            e.wait_ge(self.sems[s], v)
                        if fn is None:
                            continue
                        inst = fn(e)
                        if inc is not None:
                            inst.then_inc(self.sems[inc], by)

                getattr(block, eng)(body)
        self.ops = {e: [] for e in ENGINES}

    def barrier(self):
        for eng, q, m in (("sync", "qS", "sC"), ("scalar", "qA", "sS"),
                          ("gpsimd", "qG", "sP")):
            if self.cnt[q] > 0:
                self.cnt[m] += 1
                self.add(eng, (lambda m: lambda e: e.sem_inc(self.sems[m], 1))(m),
                         waits=[(q, self.cnt[q])])
        snap = [(k, v) for k, v in self.cnt.items()
                if v > 0 and k in ("sT", "sV", "sS", "sP", "sC")]
        for e in ENGINES:
            self.wait_only(e, snap)


def build_program(stop_after="P4", debug=False):
    nc = bass.Bass("TRN2", target_bir_lowering=False, debug=False)
    ctx = contextlib.ExitStack()

    x = nc.dram_tensor("x", [T, D], F32, kind="ExternalInput").ap()
    wq = nc.dram_tensor("wq", [128, 16, 16, 128], F32, kind="ExternalInput").ap()
    wk = nc.dram_tensor("wk", [128, 16, 4, 128], F32, kind="ExternalInput").ap()
    wv = nc.dram_tensor("wv", [128, 16, 512], F32, kind="ExternalInput").ap()
    wo = nc.dram_tensor("wo", [128, 16, 2048], F32, kind="ExternalInput").ap()
    wgu = nc.dram_tensor("wgu", [NHC, 128, 2, 16, 128], F32, kind="ExternalInput").ap()
    wd = nc.dram_tensor("wd", [2, NHC, 128, 1024], F32, kind="ExternalInput").ap()
    cosf = nc.dram_tensor("cosf", [128, T], F32, kind="ExternalInput").ap()
    sinf = nc.dram_tensor("sinf", [128, T], F32, kind="ExternalInput").ap()
    trilm = nc.dram_tensor("trilm", [128, 512], F32, kind="ExternalInput").ap()
    ident = nc.dram_tensor("ident", [128, 128], BF16, kind="ExternalInput").ap()
    pmask = nc.dram_tensor("pmask", [128, 8], F32, kind="ExternalInput").ap()
    y = nc.dram_tensor("y", [T, D], F32, kind="ExternalOutput").ap()

    xnfm_h = nc.dram_tensor("xnfm_h", [NCH, 128, 16, TC], BF16).ap()
    qf_h = nc.dram_tensor("qf_h", [NCH, 128, 16, TC], BF16).ap()
    kff_h = nc.dram_tensor("kff_h", [NCH, 128, 4, TC], BF16).ap()
    kft_h = nc.dram_tensor("kft_h", [NCH, 128, NSUB, 4, 128], BF16).ap()
    va_h = nc.dram_tensor("va_h", [NCH, 128, NSUB, 4, VW], BF16).ap()
    ya_h = nc.dram_tensor("ya_h", [NSUBT, 128, 2048], F32).ap()
    gat_h = nc.dram_tensor("gat_h", [NHC, NCH, 128, TC], BF16).ap()
    cc_in_t = nc.dram_tensor("cc_in", [4 * 128, VW], F32)
    cc_out_t = nc.dram_tensor("cc_out", [8 * 4 * 128, VW], F32)

    dbg = {}
    if debug:
        def dbg_out(name, shape, dt=F32):
            dbg[name] = nc.dram_tensor(name, shape, dt, kind="ExternalOutput").ap()
        if stop_after == "P1":
            dbg_out("d_xnfm", [NCH, 128, 16, TC], BF16)
            dbg_out("d_qf", [NCH, 128, 16, TC], BF16)
            dbg_out("d_kff", [NCH, 128, 4, TC], BF16)
            dbg_out("d_kft", [NCH, 128, NSUB, 4, 128], BF16)
            dbg_out("d_va", [NCH, 128, NSUB, 4, VW], BF16)
            dbg_out("d_sloc", [128, 4, VW], F32)
        if stop_after == "CC":
            dbg_out("d_sinit", [128, 4, VW], F32)
        if stop_after == "P2":
            dbg_out("d_ya", [NSUBT, 128, 2048], F32)
            dbg_out("d_sinit", [128, 4, VW], F32)
        if stop_after == "P3":
            dbg_out("d_gat", [NHC, NCH, 128, TC], BF16)

    sems = {}
    for sname in ("sT", "sV", "sS", "sP", "sC", "qS", "qA", "qG",
                  "cst", "wwq", "wwo", "wds", "glx",
                  "xl0", "xl1", "cs0", "cs1", "ls0", "ls1",
                  "qfw0", "qfw1", "wg0", "wg1", "wg2",
                  "gwa", "gwb", "gc0", "gc1", "ld40", "ld41",
                  "xnl0", "xnl1", "xnl2", "xnl3", "xnl4", "xnl5", "xnl6", "xnl7", "kfw"):
        sems[sname] = ctx.enter_context(nc.semaphore(sname))
    P = Plan(nc, sems)

    per = contextlib.ExitStack()
    ident_sb = per.enter_context(nc.sbuf_tensor("ident_sb", [128, 128], BF16))
    tril_sb = per.enter_context(nc.sbuf_tensor("tril_sb", [128, 512], F32))
    pm_sb = per.enter_context(nc.sbuf_tensor("pm_sb", [128, 8], F32))
    S_sb = per.enter_context(nc.sbuf_tensor("S_sb", [128, 4, VW], F32))
    S_bf = per.enter_context(nc.sbuf_tensor("S_bf", [128, 4, VW], BF16))

    ph = contextlib.ExitStack()
    sb = lambda n, shp, dt: ph.enter_context(nc.sbuf_tensor(n, shp, dt))
    ps = lambda n, shp, dt: ph.enter_context(nc.psum_tensor(n, shp, dt))

    xsb = sb("xsb", [128, 2, D], F32)
    sqs = sb("sqs", [128, D], F32)
    lnA = sb("lnA", [128, 2, 8], F32)
    xntm = sb("xntm", [128, 2, D], BF16)
    xnfm = sb("xnfm", [128, 2, 16, TC], BF16)
    cossb = sb("cossb", [128, 2, 2, TC], F32)
    wq_sb = sb("wq_sb", [128, 16, 16, 128], BF16)
    wk_sb = sb("wk_sb", [128, 16, 4, 128], BF16)
    wv_sb = sb("wv_sb", [128, 16, 512], BF16)
    qf_sb = sb("qf_sb", [128, 2, TC], BF16)
    kff_sb = sb("kff_sb", [128, 4, TC], BF16)
    kft_sb = sb("kft_sb", [128, NSUB, 4, 128], BF16)
    va_sb = sb("va_sb", [128, NSUB, 4, VW], BF16)
    S_loc = sb("S_loc", [128, 4, VW], F32)
    ra = sb("ra", [128, 2, TC], F32)
    rb = sb("rb", [128, TC], F32)
    rc = sb("rc", [128, TC], F32)
    rm = sb("rm", [128, TC], F32)
    re = sb("re", [128, TC], F32)
    qev = sb("qev", [128, 2, TC], F32)

    psq = ps("psq", [128, 2, 512], F32)
    psv = ps("psv", [128, 2, 512], F32)
    pT = ps("pT", [128, 2, 1024], BF16)
    pstA = ps("pstA", [128, 512], F32)
    pstB = ps("pstB", [128, 512], F32)

    P.add("sync", lambda e: e.dma_start(out=ident_sb[:], in_=ident[:, :]), inc="cst", inc_by=16)
    P.add("sync", lambda e: e.dma_start(out=tril_sb[:], in_=trilm[:, :]), inc="cst", inc_by=16)
    P.add("sync", lambda e: e.dma_start(out=pm_sb[:], in_=pmask[:, :]), inc="cst", inc_by=16)
    c_consts = P.cnt["cst"]
    P.add("gpsimd", lambda e: e.dma_start(out=wq_sb[:], in_=wq[:, :, :, :]), inc="wwq", inc_by=16)
    P.add("gpsimd", lambda e: e.dma_start(out=wk_sb[:], in_=wk[:, :, :, :]), inc="wwq", inc_by=16)
    wv_ld = P.add("gpsimd", lambda e: e.dma_start(out=wv_sb[:], in_=wv[:, :, :]), inc="wwq", inc_by=16)
    P.add("vector", lambda e: e.memset(S_loc[:], 0.0), inc="sV")
    vz = P.add("vector", lambda e: e.memset(va_sb[:, :, :, 128:VW], 0.0), inc="sV")
    va_ones = P.add("vector", lambda e: e.memset(va_sb[:, :, :, 128:129], 1.0),
                    waits=[("sV", vz)], inc="sV")

    xload = [0] * NSUBT
    xn_done = [0] * NSUBT
    sq_done = [0] * NSUBT
    xnc_done = [0] * NCH
    xnfm_dma = [0] * NCH
    cos_ld = [0] * NCH
    cos_sem = [""] * NCH
    tr_last_of_sub = {}
    rope_prev = {}
    vmm_last = [0] * NCH
    kw_q = [0] * NCH
    kff_dma = [0] * NCH
    st_addv = {}
    psv_cp = {}
    qf_w = {}
    b2_v = {}
    tcopy = {}
    tr_cnt = 0
    hg_idx = 0
    rope_state = {"se": 0, "vc1": 0, "vc2": 0, "vb2": 0}

    def plan_rope_fmap(ci, slot_par, hg, final_fn, final_waits):
        cs = cossb[:, ci % 2, 0, :]
        sn = cossb[:, ci % 2, 1, :]
        mmv = P.cnt["sT"]
        pa2, pb2, _ = rope_prev.get(hg - 2, (0, 0, 0))
        s_ev = P.add("scalar", lambda e: e.copy(qev[:, slot_par, :], psq[:, slot_par, :]),
                     waits=[("sT", mmv), ("sP", pb2)], inc="sS")
        pw = [("sS", s_ev), (cos_sem[ci], cos_ld[ci])]
        if hg >= 2:
            pw.append(("sV", rope_state["vc2"]))
        va = P.add("gpsimd", lambda e: e.tensor_tensor(
            ra[:, slot_par, :], qev[:, slot_par, :], cs, ALU.mult),
            waits=pw, inc="sP")
        P.add("vector", lambda e: e.tensor_tensor(
            rb[0:64, :], psq[64:128, slot_par, :], sn[0:64, :], ALU.mult),
            waits=[("sS", s_ev), (cos_sem[ci], cos_ld[ci]), ("sV", rope_state["vb2"])], inc="sV")
        b2 = P.add("vector", lambda e: e.tensor_tensor(
            rb[64:128, :], psq[0:64, slot_par, :], sn[64:128, :], ALU.mult),
            inc="sV")
        vc = P.add("vector", lambda e: e.tensor_tensor(
            rc[:], ra[:, slot_par, :], rb[:, :], ALU.add),
            waits=[("sP", va)], inc="sV")
        vm = P.add("vector", lambda e: e.tensor_scalar(
            rm[:], rc[:], 0.0, None, ALU.min),
            waits=[("sS", rope_state["se"])], inc="sV")
        se = P.add("scalar", lambda e: e.activation(re[:], rm[:], AF.Exp),
                   waits=[("sV", vm)], inc="sS")
        ve = P.add("vector", lambda e_=None: None, waits=()) if False else None
        ve = P.add("vector", final_fn, waits=[("sS", se)] + list(final_waits), inc="sV")
        rope_state["se"] = se
        rope_state["vc2"] = rope_state.get("vc1", 0)
        rope_state["vc1"] = vc
        rope_state["vb2"] = b2
        rope_prev[hg] = (b2, va, s_ev)
        return s_ev, ve

    for ci in range(NCH):
        cp = ci % 2
        w = []
        if ci >= 2:
            pa, pb, _ = rope_prev.get((ci - 2) * 20 + 19, (0, 0, 0))
            w = [("sV", pa), ("sP", pb)]
        csem = "cs0" if ci % 2 == 0 else "cs1"
        P.add("sync", lambda e, ci=ci: e.dma_start(
            out=cossb[:, ci % 2, 0, :], in_=cosf[:, ci * TC:(ci + 1) * TC]),
            waits=w, inc=csem, inc_by=16)
        cos_ld[ci] = P.add("sync", lambda e, ci=ci: e.dma_start(
            out=cossb[:, ci % 2, 1, :], in_=sinf[:, ci * TC:(ci + 1) * TC]),
            inc=csem, inc_by=16)
        cos_sem[ci] = csem

        for si in range(NSUB):
            u = ci * NSUB + si
            p = u % 2
            w = []
            if u >= 2:
                w = [("sV", xn_done[u - 2]), ("sS", sq_done[u - 2])]
            xsem = "xl0" if p == 0 else "xl1"
            xload[u] = P.add("sync", lambda e, u=u, p=p: e.dma_start(
                out=xsb[:, p, :], in_=x[u * 128:(u + 1) * 128, :]),
                waits=w, inc=xsem, inc_by=16)
            P.add("scalar", lambda e, p=p: e.activation(
                sqs[:], xsb[:, p, :], AF.Identity, accum_out=lnA[:, p, 0:1]),
                waits=[(xsem, xload[u])], inc="sS")
            sq_done[u] = P.add("scalar", lambda e, p=p: e.activation(
                sqs[:], xsb[:, p, :], AF.Square, accum_out=lnA[:, p, 1:2]),
                waits=[(xsem, xload[u])], inc="sS")
            v1 = P.add("vector", lambda e, p=p: e.tensor_scalar(
                lnA[:, p, 2:3], lnA[:, p, 0:1], 1.0 / D, None, ALU.mult),
                waits=[("sS", sq_done[u])], inc="sV")
            v2 = P.add("vector", lambda e, p=p: e.tensor_scalar(
                lnA[:, p, 3:4], lnA[:, p, 1:2], 1.0 / D, None, ALU.mult),
                waits=[("sS", sq_done[u])], inc="sV")
            v3 = P.add("vector", lambda e, p=p: e.tensor_tensor(
                lnA[:, p, 7:8], lnA[:, p, 2:3], lnA[:, p, 2:3], ALU.mult),
                waits=[("sV", v1)], inc="sV")
            v4 = P.add("vector", lambda e, p=p: e.scalar_tensor_tensor(
                lnA[:, p, 4:5], lnA[:, p, 3:4], EPS_LN, lnA[:, p, 7:8],
                ALU.add, ALU.subtract), waits=[("sV", v3)], inc="sV")
            sl_ = P.add("scalar", lambda e, p=p: e.activation(
                lnA[:, p, 7:8], lnA[:, p, 4:5], AF.Ln),
                waits=[("sV", v4)], inc="sS")
            s2 = P.add("scalar", lambda e, p=p: e.activation(
                lnA[:, p, 5:6], lnA[:, p, 7:8], AF.Exp, scale=-0.5),
                waits=[("sS", sl_)], inc="sS")
            v5 = P.add("vector", lambda e, p=p: e.tensor_tensor(
                lnA[:, p, 6:7], lnA[:, p, 2:3], lnA[:, p, 5:6], ALU.mult),
                waits=[("sS", s2)], inc="sV")
            w = [("sV", v5)]
            if u >= 2:
                w.append(("sT", tr_last_of_sub.get(u - 2, 0)))
            xn_done[u] = P.add("vector", lambda e, p=p: e.tensor_scalar(
                xntm[:, p, :], xsb[:, p, :], lnA[:, p, 5:6], lnA[:, p, 6:7],
                ALU.mult, ALU.subtract), waits=w, inc="sV")
            for dc in range(16):
                tp = tr_cnt % 2
                w = [("sV", xn_done[u]), ("cst", c_consts)]
                if tr_cnt >= 2:
                    w.append(("sS", tcopy[tr_cnt - 2]))
                tt = P.add("tensor", lambda e, p=p, dc=dc, tp=tp: e.transpose(
                    pT[:, tp, 0:128], xntm[:, p, dc * 128:(dc + 1) * 128], ident_sb[:]),
                    waits=w, inc="sT")
                w2 = [("sT", tt)]
                if si == 0 and dc == 0 and ci >= 2:
                    w2.append(("qA", kw_q[ci - 1]))
                tcopy[tr_cnt] = P.add("scalar", lambda e, ci=ci, dc=dc, si=si, tp=tp: e.copy(
                    xnfm[:, ci % 2, dc, si * 128:(si + 1) * 128], pT[:, tp, 0:128]),
                    waits=w2, inc="sS")
                tr_cnt += 1
            tr_last_of_sub[u] = P.cnt["sT"]
        xnc_done[ci] = P.cnt["sS"]
        xnfm_dma[ci] = P.add("scalar", lambda e, ci=ci: e.dma_start(
            out=xnfm_h[ci, :, :, :], in_=xnfm[:, ci % 2, :, :]),
            waits=[("sS", xnc_done[ci])], inc="qA", inc_by=16)

        for h in range(16):
            qh_idx = ci * 16 + h
            slot = hg_idx % 2
            qslot = qh_idx % 2
            w = [("sS", xnc_done[ci]), ("wwq", wv_ld)]
            if hg_idx >= 2:
                sv2, pb2v, se2 = rope_prev.get(hg_idx - 2, (0, 0, 0))
                w.append(("sS", se2))
                w.append(("sV", sv2))
            for dc in range(16):
                ww = w if dc == 0 else ()
                P.add("tensor", lambda e, ci=ci, h=h, dc=dc, slot=slot: e.matmul(
                    psq[:, slot, :], wq_sb[:, dc, h, :], xnfm[:, ci % 2, dc, :],
                    start=(dc == 0), stop=(dc == 15)),
                    waits=ww, inc=("sT" if dc == 15 else None))
            fw = [(("qfw0" if qslot == 0 else "qfw1"), qf_w[qh_idx - 2])] if qh_idx >= 2 else []
            b2, ve = plan_rope_fmap(
                ci, slot, hg_idx,
                (lambda qslot: lambda e: e.scalar_tensor_tensor(
                    qf_sb[:, qslot, :], rc[:], 0.0, re[:], ALU.max, ALU.add))(qslot), fw)
            b2_v[hg_idx] = b2
            qf_w[qh_idx] = P.add("scalar", lambda e, ci=ci, h=h, qslot=qslot: e.dma_start(
                out=qf_h[ci, :, h, :], in_=qf_sb[:, qslot, :]),
                waits=[("sV", ve)], inc=("qfw0" if qslot == 0 else "qfw1"), inc_by=16)
            hg_idx += 1

        for g in range(4):
            slot = hg_idx % 2
            w = [("sS", xnc_done[ci]), ("wwq", wv_ld)]
            if hg_idx >= 2:
                sv2, pb2v, se2 = rope_prev.get(hg_idx - 2, (0, 0, 0))
                w.append(("sS", se2))
                w.append(("sV", sv2))
            for dc in range(16):
                ww = w if dc == 0 else ()
                P.add("tensor", lambda e, ci=ci, g=g, dc=dc, slot=slot: e.matmul(
                    psq[:, slot, :], wk_sb[:, dc, g, :], xnfm[:, ci % 2, dc, :],
                    start=(dc == 0), stop=(dc == 15)),
                    waits=ww, inc=("sT" if dc == 15 else None))
            kwf = [("kfw", kff_dma[ci - 1])] if (g == 0 and ci > 0) else []
            b2, ve = plan_rope_fmap(
                ci, slot, hg_idx,
                (lambda g: lambda e: e.scalar_tensor_tensor(
                    kff_sb[:, g, :], rc[:], 0.0, re[:], ALU.max, ALU.add))(g), kwf)
            b2_v[hg_idx] = b2
            hg_idx += 1
        kff_done = P.cnt["sV"]
        kff_dma[ci] = P.add("scalar", lambda e, ci=ci: e.dma_start(
            out=kff_h[ci, :, :, :], in_=kff_sb[:, :, :]),
            waits=[("sV", kff_done)], inc="kfw", inc_by=16)
        for g in range(4):
            for si in range(NSUB):
                tp = tr_cnt % 2
                w = [("sV", kff_done)]
                if tr_cnt >= 2:
                    w.append(("sS", tcopy[tr_cnt - 2]))
                tt = P.add("tensor", lambda e, g=g, si=si, tp=tp: e.transpose(
                    pT[:, tp, 0:128], kff_sb[:, g, si * 128:(si + 1) * 128], ident_sb[:]),
                    waits=w, inc="sT")
                w2 = [("sT", tt)]
                if g == 0 and si == 0 and ci > 0:
                    w2.append(("qA", P.cnt["qA"]))
                tcopy[tr_cnt] = P.add("scalar", lambda e, g=g, si=si, tp=tp: e.copy(
                    kft_sb[:, si, g, :], pT[:, tp, 0:128]),
                    waits=w2, inc="sS")
                tr_cnt += 1
        kft_done = P.cnt["sS"]
        P.add("scalar", lambda e, ci=ci: e.dma_start(
            out=kft_h[ci, :, :, :, :], in_=kft_sb[:, :, :, :]),
            waits=[("sS", kft_done)], inc="qA", inc_by=16)

        for si in range(NSUB):
            gv = ci * NSUB + si
            vs = gv % 2
            w = [("sS", xnc_done[ci]), ("wwq", wv_ld), ("sV", va_ones)]
            if gv >= 2:
                w.append(("sS", psv_cp.get(gv - 2, 0)))
            for dc in range(16):
                ww = w if dc == 0 else ()
                P.add("tensor", lambda e, ci=ci, si=si, dc=dc, vs=vs: e.matmul(
                    psv[:, vs, :], xnfm[:, ci % 2, dc, si * 128:(si + 1) * 128],
                    wv_sb[:, dc, :], start=(dc == 0), stop=(dc == 15)),
                    waits=ww, inc=("sT" if dc == 15 else None))
            mv = P.cnt["sT"]
            last = 0
            for g in range(4):
                w2 = [("sT", mv)]
                if si == 0 and g == 0 and ci > 0:
                    w2.append(("qA", P.cnt["qA"]))
                last = P.add("scalar", lambda e, si=si, g=g, vs=vs: e.copy(
                    va_sb[:, si, g, 0:128], psv[:, vs, g * 128:(g + 1) * 128]),
                    waits=w2, inc="sS")
            psv_cp[gv] = last
        va_all = P.cnt["sS"]
        P.add("scalar", lambda e, ci=ci: e.dma_start(
            out=va_h[ci, :, :, :, :], in_=va_sb[:, :, :, :]),
            waits=[("sS", va_all)], inc="qA", inc_by=16)
        kw_q[ci] = P.cnt["qA"]

        for si in range(NSUB):
            for g in range(4):
                gi = ci * 16 + si * 4 + g
                pst = pstA if gi % 2 == 0 else pstB
                w = [("sS", va_all), ("sS", kft_done)]
                if gi >= 2:
                    w.append(("sV", st_addv.get(gi - 2, 0)))
                ms = P.add("tensor", lambda e, si=si, g=g, pst=pst: e.matmul(
                    pst[:, 0:129], kft_sb[:, si, g, :], va_sb[:, si, g, 0:129],
                    start=True, stop=True), waits=w, inc="sT")
                st_addv[gi] = P.add("vector", lambda e, g=g, pst=pst: e.tensor_tensor(
                    S_loc[:, g, 0:129], S_loc[:, g, 0:129], pst[:, 0:129], ALU.add),
                    waits=[("sT", ms)], inc="sV")
        vmm_last[ci] = P.cnt["sT"]

    p1_V = P.cnt["sV"]

    if debug and stop_after == "P1":
        w = [("sV", p1_V), ("sS", P.cnt["sS"]), ("qA", P.cnt["qA"])]
        for name, src in (("d_sloc", None),):
            pass
        P.add("gpsimd", lambda e: e.dma_start(out=dbg["d_sloc"][:, :, :], in_=S_loc[:]),
              waits=w, inc="qG", inc_by=16)
        for name, src in (("d_xnfm", xnfm_h), ("d_qf", qf_h), ("d_kff", kff_h)):
            P.add("gpsimd", lambda e, name=name, src=src: e.dma_start(
                out=dbg[name][:, :, :, :], in_=src[:, :, :, :]), inc="qG", inc_by=16)
        for name, src in (("d_kft", kft_h), ("d_va", va_h)):
            P.add("gpsimd", lambda e, name=name, src=src: e.dma_start(
                out=dbg[name][:, :, :, :, :], in_=src[:, :, :, :, :]), inc="qG", inc_by=16)
        P.wait_only("gpsimd", [("qG", P.cnt["qG"])])
        P.emit()
        ph.close(); per.close(); ctx.close()
        return nc, dbg

    P.add("gpsimd", lambda e: e.dma_start(
        out=cc_in_t.ap().rearrange("(g p) f -> p g f", p=128), in_=S_loc[:, :, :]),
        waits=[("sV", p1_V)], inc="qG", inc_by=16)
    cw = P.cnt["qG"]
    P.add("gpsimd", lambda e: e.collective_compute(
        "AllGather", ALU.bypass, replica_groups=[list(range(8))],
        ins=[cc_in_t.ap().opt()], outs=[cc_out_t.ap().opt()]),
        waits=[("qG", cw)], inc="sC")

    P.emit()
    ph.close()

    ph = contextlib.ExitStack()
    sb = lambda n, shp, dt: ph.enter_context(nc.sbuf_tensor(n, shp, dt))
    ps = lambda n, shp, dt: ph.enter_context(nc.psum_tensor(n, shp, dt))
    gath_sb = sb("gath_sb", [128, 8, 4, VW], F32)

    P.barrier()
    gl = P.add("gpsimd", lambda e: e.dma_start(
        out=gath_sb[:, :, :, :],
        in_=cc_out_t.ap().rearrange("(r g p) f -> p r g f", p=128, g=4)),
        waits=[("sC", P.cnt["sC"])], inc="glx", inc_by=16)
    pv = P.add("vector", lambda e: e.memset(S_sb[:], 0.0), waits=[("glx", gl)], inc="sV")
    for r in range(8):
        pv = P.add("vector", lambda e, r=r: e.scalar_tensor_tensor(
            S_sb.ap().rearrange("p g f -> p (g f)"),
            gath_sb[:, r, :, :].rearrange("p g f -> p (g f)"),
            pm_sb[:, r:r + 1],
            S_sb.ap().rearrange("p g f -> p (g f)"),
            ALU.mult, ALU.add),
            waits=[("cst", c_consts), ("sV", pv)], inc="sV")
    sbf_init = P.add("vector", lambda e: e.tensor_copy(S_bf[:], S_sb[:]),
                     waits=[("sV", pv)], inc="sV")

    if debug and stop_after == "CC":
        P.add("gpsimd", lambda e: e.dma_start(out=dbg["d_sinit"][:, :, :], in_=S_sb[:]),
              waits=[("sV", sbf_init)], inc="qG", inc_by=16)
        P.wait_only("gpsimd", [("qG", P.cnt["qG"])])
        P.emit()
        ph.close(); per.close(); ctx.close()
        return nc, dbg

    qbuf = sb("qbuf", [128, 2, 16, TC], BF16)
    kfb = sb("kfb", [128, 2, 4, TC], BF16)
    ktb = sb("ktb", [128, 2, NSUB, 4, 128], BF16)
    vab = sb("vab", [128, 2, NSUB, 4, VW], BF16)
    wo_sb = sb("wo_sb", [128, 16, 2048], BF16)
    abf = sb("abf", [128, 2, 4, 128], BF16)
    att_tm = sb("att_tm", [128, 2, 128], BF16)
    afm = sb("afm", [128, 2, 16, 128], BF16)
    st2 = sb("st2", [128, 2, 2], F32)
    y_sb = sb("y_sb", [128, 2, 2048], F32)

    pscr = ps("pscr", [128, 512], F32)
    pRp = ps("pRp", [128, 2, 512], F32)
    pTr = ps("pTr", [128, 1024], BF16)
    pW = ps("pW", [128, 2048], F32)

    wo_ld = P.add("gpsimd", lambda e: e.dma_start(out=wo_sb[:], in_=wo[:, :, :]),
                  inc="wwo", inc_by=16)

    ld2 = [0] * NCH
    ld2_sem = [""] * NCH
    msk_v = {}
    nrm_v = {}
    afc_s = {}
    yev_s = [0] * NSUBT
    yaw_q = [0] * NSUBT
    stadd_v = [0] * (NSUBT + 1)
    stadd_v[0] = sbf_init
    sbf_u = [0] * (NSUBT + 1)
    sbf_u[0] = sbf_init
    mmst_t = [0] * NSUBT

    for ci in range(NCH):
        w = []
        if ci >= 2:
            w.append(("sT", mmst_t[(ci - 2) * NSUB + NSUB - 1]))
        lsem = "ls0" if ci % 2 == 0 else "ls1"
        for fn in (
            lambda e, ci=ci: e.dma_start(out=qbuf[:, ci % 2], in_=qf_h[ci]),
            lambda e, ci=ci: e.dma_start(out=kfb[:, ci % 2], in_=kff_h[ci]),
            lambda e, ci=ci: e.dma_start(out=ktb[:, ci % 2], in_=kft_h[ci]),
            lambda e, ci=ci: e.dma_start(out=vab[:, ci % 2], in_=va_h[ci]),
        ):
            ld2[ci] = P.add("sync", fn, waits=w, inc=lsem, inc_by=16)
            w = []
        ld2_sem[ci] = lsem

        for si in range(NSUB):
            u = ci * NSUB + si
            up = u % 2
            sl = slice(si * 128, (si + 1) * 128)
            mmW = 0

            def plan_tr_afc(h, u=u, up=up):
                hu = u * 16 + h
                hp = h % 2
                w = [("sV", nrm_v[hu])]
                if hu >= 1:
                    w.append(("sS", afc_s.get(hu - 1, 0)))
                tt = P.add("tensor", lambda e, hp=hp: e.transpose(
                    pTr[:, 0:128], att_tm[:, hp, :], ident_sb[:]),
                    waits=w, inc="sT")
                afc_s[hu] = P.add("scalar", lambda e, up=up, h=h: e.copy(
                    afm[:, up, h, :], pTr[:, 0:128]),
                    waits=[("sT", tt)], inc="sS")

            def plan_wo(h, u=u, up=up):
                nonlocal mmW
                w = [("wwo", wo_ld), ("sS", afc_s[u * 16 + h]), ("sV", stadd_v[u])]
                if h == 0 and u >= 1:
                    w.append(("sS", yev_s[u - 1]))
                first = True
                for os_ in range(4):
                    ww = w if first else ()
                    first = False
                    mmW = P.add("tensor", lambda e, up=up, h=h, os_=os_: e.matmul(
                        pW[:, os_ * 512:(os_ + 1) * 512], afm[:, up, h, :],
                        wo_sb[:, h, os_ * 512:(os_ + 1) * 512],
                        start=(h == 0), stop=(h == 15), skip_group_check=True),
                        waits=ww, inc=("sT" if (h == 15 and os_ == 3) else None))

            for h in range(16):
                g = h // 4
                hp = h % 2
                hu = u * 16 + h
                if h % 4 == 0:
                    w = [(ld2_sem[ci], ld2[ci])]
                    qk = u * 4 + g
                    if qk >= 1:
                        w.append(("sV", msk_v.get(qk - 1, 0)))
                    sc = P.add("tensor", lambda e, ci=ci, g=g, sl=sl: e.matmul(
                        pscr[:, 0:512], kfb[:, ci % 2, g, sl],
                        qbuf[:, ci % 2, 4 * g:4 * g + 4, sl],
                        start=True, stop=True), waits=w, inc="sT")
                    msk_v[qk] = P.add("vector", lambda e, hp2=(g % 2): e.tensor_tensor(
                        abf[:, hp2, :, :], pscr.ap().rearrange("p (a f) -> p a f", a=4),
                        tril_sb.ap().rearrange("p (a f) -> p a f", a=4), ALU.mult),
                        waits=[("sT", sc)], inc="sV")
                w = [("sV", msk_v[u * 4 + g]), ("sV", sbf_u[u])]
                if hu >= 2:
                    w.append(("sV", nrm_v.get(hu - 2, 0)))
                P.add("tensor", lambda e, ci=ci, si=si, g=g, h=h, hp=hp: e.matmul(
                    pRp[:, hp, 0:129], abf[:, g % 2, h % 4, :],
                    vab[:, ci % 2, si, g, 0:129],
                    start=True, stop=False), waits=w)
                au = P.add("tensor", lambda e, ci=ci, h=h, g=g, hp=hp, sl=sl: e.matmul(
                    pRp[:, hp, 0:129], qbuf[:, ci % 2, h, sl], S_bf[:, g, 0:129],
                    start=False, stop=True), inc="sT")
                n1 = P.add("vector", lambda e, hp=hp: e.tensor_scalar(
                    st2[:, hp, 0:1], pRp[:, hp, 128:129], EPS_A, None, ALU.add),
                    waits=[("sT", au)], inc="sV")
                n2 = P.add("vector", lambda e, hp=hp: e.reciprocal(
                    st2[:, hp, 1:2], st2[:, hp, 0:1]), waits=[("sV", n1)], inc="sV")
                nrm_v[hu] = P.add("vector", lambda e, hp=hp: e.tensor_scalar(
                    att_tm[:, hp, :], pRp[:, hp, 0:128], st2[:, hp, 1:2], None, ALU.mult),
                    waits=[("sV", n2)], inc="sV")
                if h >= 1:
                    plan_tr_afc(h - 1)
                if h >= 2:
                    plan_wo(h - 2)
            plan_tr_afc(15)
            plan_wo(14)
            plan_wo(15)
            w = [("sT", mmW)]
            if u >= 2:
                w.append(("qA", P.cnt["qA"]))
            yev_s[u] = P.add("scalar", lambda e, up=up: e.copy(
                y_sb[:, up, :], pW[:]), waits=w, inc="sS")
            yaw_q[u] = P.add("scalar", lambda e, u=u, up=up: e.dma_start(
                out=ya_h[u], in_=y_sb[:, up, :]),
                waits=[("sS", yev_s[u])], inc="qA", inc_by=16)
            w = [("sS", yev_s[u]), (ld2_sem[ci], ld2[ci])]
            mmst = 0
            for g in range(4):
                ww = w if g == 0 else ()
                mmst = P.add("tensor", lambda e, ci=ci, si=si, g=g: e.matmul(
                    pW[:, g * 512:g * 512 + 129], ktb[:, ci % 2, si, g, :],
                    vab[:, ci % 2, si, g, 0:129], start=True, stop=True),
                    waits=ww, inc=("sT" if g == 3 else None))
            mmst_t[u] = mmst
            for g in range(4):
                P.add("vector", lambda e, g=g: e.tensor_tensor(
                    S_sb[:, g, 0:129], S_sb[:, g, 0:129],
                    pW[:, g * 512:g * 512 + 129], ALU.add),
                    waits=[("sT", mmst)], inc="sV")
            stadd_v[u + 1] = P.cnt["sV"]
            sbf_u[u + 1] = P.add("vector", lambda e: e.tensor_copy(S_bf[:], S_sb[:]),
                                 waits=[("sV", stadd_v[u + 1])], inc="sV")

    if debug and stop_after == "P2":
        w = [("qA", P.cnt["qA"]), ("sV", P.cnt["sV"])]
        P.add("gpsimd", lambda e: e.dma_start(out=dbg["d_ya"][:, :, :], in_=ya_h[:, :, :]),
              waits=w, inc="qG", inc_by=16)
        P.add("gpsimd", lambda e: e.dma_start(out=dbg["d_sinit"][:, :, :], in_=S_sb[:]),
              inc="qG", inc_by=16)
        P.wait_only("gpsimd", [("qG", P.cnt["qG"])])
        P.emit()
        ph.close(); per.close(); ctx.close()
        return nc, dbg

    P.barrier()
    P.emit()
    ph.close()
    ph = contextlib.ExitStack()
    sb = lambda n, shp, dt: ph.enter_context(nc.sbuf_tensor(n, shp, dt))
    ps = lambda n, shp, dt: ph.enter_context(nc.psum_tensor(n, shp, dt))

    xnr = sb("xnr", [128, 16, NCH, TC], BF16)
    wgub = sb("wgub", [128, 3, 2, 16, 128], BF16)
    gsb = sb("gsb", [128, 2, NCH, TC], BF16)
    ssb3 = sb("ssb3", [128, 2, 512], F32)
    pg = ps("pg", [128, 2, 512], F32)
    pu = ps("pu", [128, 2, 512], F32)

    xnl = [0] * NCH
    for ci in range(NCH):
        xnl[ci] = P.add("sync", lambda e, ci=ci: e.dma_start(
            out=xnr[:, :, ci, :], in_=xnfm_h[ci]), inc=f"xnl{ci}", inc_by=16)

    wgl = [0] * NHC
    mmU_t = {}
    gmul_v = {}
    gw_q = [0] * NHC
    for hc in range(NHC):
        w = []
        if hc >= 3:
            w.append(("sT", mmU_t[(hc - 3, NCH - 1)]))
        wgl[hc] = P.add("gpsimd", lambda e, hc=hc: e.dma_start(
            out=wgub[:, hc % 3], in_=wgu[hc]), waits=w, inc=f"wg{hc % 3}", inc_by=16)
        for tt in range(NCH):
            tp = tt % 2
            pidx = (hc, tt - 2) if tt >= 2 else (hc - 1, tt - 2 + NCH)
            w = [(f"wg{hc % 3}", wgl[hc]), (f"xnl{tt}", xnl[tt])]
            if (hc, tt) != (0, 0):
                w.append(("sV", gmul_v.get(pidx, 0)))
            for dc in range(16):
                ww = w if dc == 0 else ()
                P.add("tensor", lambda e, hc=hc, tt=tt, dc=dc, tp=tp: e.matmul(
                    pg[:, tp, :], wgub[:, hc % 3, 0, dc, :], xnr[:, dc, tt, :],
                    start=(dc == 0), stop=(dc == 15)),
                    waits=ww, inc=("sT" if dc == 15 else None))
            mg = P.cnt["sT"]
            for dc in range(16):
                P.add("tensor", lambda e, hc=hc, tt=tt, dc=dc, tp=tp: e.matmul(
                    pu[:, tp, :], wgub[:, hc % 3, 1, dc, :], xnr[:, dc, tt, :],
                    start=(dc == 0), stop=(dc == 15)),
                    inc=("sT" if dc == 15 else None))
            mmU_t[(hc, tt)] = P.cnt["sT"]
            w2 = [("sT", mg)]
            if (hc, tt) != (0, 0):
                w2.append(("sV", gmul_v.get(pidx, 0)))
            sl3 = P.add("scalar", lambda e, tp=tp: e.activation(
                ssb3[:, tp, :], pg[:, tp, :], AF.Silu), waits=w2, inc="sS")
            w2 = [("sS", sl3), ("sT", mmU_t[(hc, tt)])]
            if hc >= 2 and tt == 0:
                w2.append((("gwa" if hc % 2 == 0 else "gwb"), gw_q[hc - 2]))
            gmul_v[(hc, tt)] = P.add("vector", lambda e, hc=hc, tt=tt, tp=tp: e.tensor_tensor(
                gsb[:, hc % 2, tt, :], ssb3[:, tp, :], pu[:, tp, :], ALU.mult),
                waits=w2, inc="sV")
        gw_q[hc] = P.add("scalar", lambda e, hc=hc: e.dma_start(
            out=gat_h[hc].rearrange("a p f -> p a f"), in_=gsb[:, hc % 2]),
            waits=[("sV", gmul_v[(hc, NCH - 1)])],
            inc=("gwa" if hc % 2 == 0 else "gwb"), inc_by=16)

    if debug and stop_after == "P3":
        P.add("gpsimd", lambda e: e.dma_start(
            out=dbg["d_gat"][:, :, :, :], in_=gat_h[:, :, :, :]),
            waits=[("qA", P.cnt["qA"])], inc="qG", inc_by=16)
        P.wait_only("gpsimd", [("qG", P.cnt["qG"])])
        P.emit()
        ph.close(); per.close(); ctx.close()
        return nc, dbg

    P.barrier()
    P.emit()
    ph.close()
    ph = contextlib.ExitStack()
    sb = lambda n, shp, dt: ph.enter_context(nc.sbuf_tensor(n, shp, dt))
    ps = lambda n, shp, dt: ph.enter_context(nc.psum_tensor(n, shp, dt))

    wdb = sb("wdb", [128, NHC, 1024], BF16)
    gcol = sb("gcol", [128, 2, NHC, 128], BF16)
    xb4 = sb("xb4", [128, 2, 1024], F32)
    yab4 = sb("yab4", [128, 2, 1024], F32)
    obA = sb("obA", [128, 2, 1024], F32)
    ob = sb("ob", [128, 2, 1024], F32)
    pY = ps("pY", [128, 2, 1024], F32)

    cmb_v = {}
    out_q = {}
    mm_last = {}
    for half in range(2):
        wdl = [0] * NHC
        for hc in range(NHC):
            w = []
            if half == 1 and hc == 0:
                w.append(("sT", mm_last[(0, NSUBT - 1)]))
            wdl[hc] = P.add("gpsimd", lambda e, half=half, hc=hc: e.dma_start(
                out=wdb[:, hc, :], in_=wd[half, hc]), waits=w, inc="wds", inc_by=16)
        for u4 in range(NSUBT):
            up = u4 % 2
            w = []
            pidx = (half, u4 - 2) if u4 >= 2 else (half - 1, u4 - 2 + NSUBT)
            if (half, u4) != (0, 0) and (half, u4) != (0, 1):
                w.append(("sT", mm_last[pidx]))
            gl = 0
            gsem = "gc0" if up == 0 else "gc1"
            w2 = list(w)
            for part in range(8):
                eng = "sync" if part < 4 else "scalar"
                gl = P.add(eng, lambda e, u4=u4, up=up, part=part: e.dma_start(
                    out=gcol[:, up, part * 8:(part + 1) * 8, :],
                    in_=gat_h[part * 8:(part + 1) * 8, u4 // NSUB, :,
                              (u4 % NSUB) * 128:(u4 % NSUB + 1) * 128].rearrange(
                                  "a p f -> p a f")),
                    waits=(w if part < 4 else w2), inc=gsem, inc_by=16)
                if part < 4:
                    w = []
                else:
                    w2 = []
            w = []
            if (half, u4) != (0, 0) and (half, u4) != (0, 1):
                w.append(("sV", cmb_v.get(pidx, 0)))
                w.append(("qS", out_q.get(pidx, 0)))
            l4sem = "ld40" if up == 0 else "ld41"
            P.add("sync", lambda e, half=half, u4=u4, up=up: e.dma_start(
                out=xb4[:, up, :],
                in_=x[u4 * 128:(u4 + 1) * 128, half * 1024:(half + 1) * 1024]),
                waits=w, inc=l4sem, inc_by=16)
            yl = P.add("sync", lambda e, half=half, u4=u4, up=up: e.dma_start(
                out=yab4[:, up, :],
                in_=ya_h[u4, :, half * 1024:(half + 1) * 1024]),
                inc=l4sem, inc_by=16)
            first = True
            mmY = 0
            for os_ in range(2):
                for hc in range(NHC):
                    ww = [("wds", wdl[NHC - 1])]
                    if first:
                        ww.append((gsem, gl))
                        if (half, u4) != (0, 0) and (half, u4) != (0, 1):
                            ww.append(("sV", cmb_v.get(pidx, 0)))
                        first = False
                    mmY = P.add("tensor", lambda e, up=up, hc=hc, os_=os_: e.matmul(
                        pY[:, up, os_ * 512:(os_ + 1) * 512],
                        gcol[:, up, hc, :],
                        wdb[:, hc, os_ * 512:(os_ + 1) * 512],
                        start=(hc == 0), stop=(hc == NHC - 1)),
                        waits=ww, inc=("sT" if (hc == NHC - 1 and os_ == 1) else None))
            mm_last[(half, u4)] = mmY
            P.add("vector", lambda e, up=up: e.tensor_tensor(
                obA[:, up, :], pY[:, up, :], xb4[:, up, :], ALU.add),
                waits=[("sT", mmY), (l4sem, yl)], inc="sV")
            cmb_v[(half, u4)] = P.add("vector", lambda e, up=up: e.tensor_tensor(
                ob[:, up, :], obA[:, up, :], yab4[:, up, :], ALU.add), inc="sV")
            out_q[(half, u4)] = P.add("sync", lambda e, half=half, u4=u4, up=up: e.dma_start(
                out=y[u4 * 128:(u4 + 1) * 128, half * 1024:(half + 1) * 1024],
                in_=ob[:, up, :]),
                waits=[("sV", cmb_v[(half, u4)])], inc="qS", inc_by=16)

    P.wait_only("sync", [("qS", out_q[(1, NSUBT - 1)])])
    P.emit()
    ph.close(); per.close(); ctx.close()
    return nc, dbg


def _pack_inputs(L0_attn_x, Wq, Wk, Wv, Wo, Wg, Wu, Wd):
    f32 = np.float32
    xs = np.ascontiguousarray(np.asarray(L0_attn_x, dtype=f32).reshape(B, 4, T, D))
    wq_p = np.ascontiguousarray(
        np.asarray(Wq, f32).reshape(16, 128, 16, 128).transpose(3, 2, 0, 1))
    wk_p = np.ascontiguousarray(
        np.asarray(Wk, f32).reshape(4, 128, 16, 128).transpose(3, 2, 0, 1))
    wv_p = np.ascontiguousarray(
        np.asarray(Wv, f32).T.reshape(16, 128, 512).transpose(1, 0, 2))
    wo_p = np.ascontiguousarray(
        np.asarray(Wo, f32).T.reshape(16, 128, 2048).transpose(1, 0, 2))
    wg_t = np.asarray(Wg, f32).reshape(NHC, 128, 16, 128).transpose(0, 3, 2, 1)
    wu_t = np.asarray(Wu, f32).reshape(NHC, 128, 16, 128).transpose(0, 3, 2, 1)
    wgu_p = np.ascontiguousarray(np.stack([wg_t, wu_t], axis=2))
    wd_p = np.ascontiguousarray(
        np.asarray(Wd, f32).reshape(2, 1024, NHC, 128).transpose(0, 2, 3, 1))

    inv = 1.0 / (10000.0 ** (np.arange(0, HD, 2, dtype=np.float64) / HD))
    tril = np.ascontiguousarray(
        np.tile(np.triu(np.ones((128, 128), f32)), (1, 4)))
    identity = np.ascontiguousarray(np.eye(128, dtype=ml_dtypes.bfloat16))

    in_maps = []
    for core in range(8):
        b, c = divmod(core, 4)
        pos = np.arange(c * T, (c + 1) * T, dtype=np.float64)
        ang = np.outer(pos, inv)
        cosv = np.cos(ang).astype(f32)
        sinv = np.sin(ang).astype(f32)
        cos_fm = np.ascontiguousarray(np.concatenate([cosv, cosv], axis=1).T)
        sin_fm = np.ascontiguousarray(np.concatenate([-sinv, sinv], axis=1).T)
        pm = np.zeros((128, 8), f32)
        for r in range(8):
            rb_, rc_ = divmod(r, 4)
            if rb_ == b and rc_ < c:
                pm[:, r] = 1.0
        in_maps.append({
            "x": np.ascontiguousarray(xs[b, c]),
            "wq": wq_p, "wk": wk_p, "wv": wv_p, "wo": wo_p,
            "wgu": wgu_p, "wd": wd_p,
            "cosf": cos_fm, "sinf": sin_fm,
            "trilm": tril, "ident": identity,
            "pmask": pm,
        })
    return in_maps


def kernel(L0_attn_x, L0_attn_Wq, L0_attn_Wk, L0_attn_Wv, L0_attn_Wo,
           L0_ffn_w_gate, L0_ffn_w_up, L0_ffn_w_down):
    in_maps = _pack_inputs(L0_attn_x, L0_attn_Wq, L0_attn_Wk, L0_attn_Wv,
                           L0_attn_Wo, L0_ffn_w_gate, L0_ffn_w_up, L0_ffn_w_down)
    nc, _ = build_program(stop_after="P4", debug=False)
    res = run_bass_kernel_spmd(nc, in_maps, core_ids=list(range(8)))
    out = np.empty((B, S, D), dtype=np.float32)
    for core in range(8):
        b, c = divmod(core, 4)
        out[b, c * T:(c + 1) * T, :] = res.results[core]["y"]
    return out


# version 14
# speedup vs baseline: 1.0502x; 1.0338x over previous
import contextlib
import numpy as np
import ml_dtypes
import concourse.bass as bass
import concourse.mybir as mybir
from concourse.bass_utils import run_bass_kernel_spmd

F32 = mybir.dt.float32
BF16 = mybir.dt.bfloat16
AF = mybir.ActivationFunctionType
ALU = mybir.AluOpType
AX = mybir.AxisListType

B, S, D = 2, 16384, 2048
H, G, HD = 16, 4, 128
DH = 8192
T = S // 4
TC = 512
NCH = T // TC
NSUB = TC // 128
NSUBT = T // 128
NHC = DH // 128
EPS_LN = 1e-5
EPS_A = 1e-6
VW = 136

ENGINES = ("sync", "scalar", "vector", "tensor", "gpsimd")


class Plan:

    def __init__(self, nc, sems):
        self.nc = nc
        self.sems = sems
        self.cnt = {k: 0 for k in sems}
        self.ops = {e: [] for e in ENGINES}
        self.lastw = {e: {} for e in ENGINES}

    def add(self, eng, fn, waits=(), inc=None, inc_by=1):
        ws = []
        for s, v in waits:
            if v <= 0:
                continue
            if v > self.lastw[eng].get(s, -1):
                ws.append((s, v))
                self.lastw[eng][s] = v
        post = None
        if inc is not None:
            self.cnt[inc] += inc_by
            post = self.cnt[inc]
        self.ops[eng].append((fn, tuple(ws), inc, inc_by))
        return post

    def wait_only(self, eng, waits):
        self.add(eng, None, waits=waits)

    def emit(self):
        with self.nc.Block() as block:
            for eng in ENGINES:
                lst = self.ops[eng]
                if not lst:
                    continue

                def body(e, lst=lst):
                    for fn, ws, inc, by in lst:
                        for s, v in ws:
                            e.wait_ge(self.sems[s], v)
                        if fn is None:
                            continue
                        inst = fn(e)
                        if inc is not None:
                            inst.then_inc(self.sems[inc], by)

                getattr(block, eng)(body)
        self.ops = {e: [] for e in ENGINES}

    def barrier(self):
        for eng, q, m in (("sync", "qS", "sC"), ("scalar", "qA", "sS"),
                          ("gpsimd", "qG", "sP")):
            if self.cnt[q] > 0:
                self.cnt[m] += 1
                self.add(eng, (lambda m: lambda e: e.sem_inc(self.sems[m], 1))(m),
                         waits=[(q, self.cnt[q])])
        snap = [(k, v) for k, v in self.cnt.items()
                if v > 0 and k in ("sT", "sV", "sS", "sP", "sC")]
        for e in ENGINES:
            self.wait_only(e, snap)


def build_program(stop_after="P4", debug=False):
    nc = bass.Bass("TRN2", target_bir_lowering=False, debug=False)
    ctx = contextlib.ExitStack()

    x = nc.dram_tensor("x", [T, D], F32, kind="ExternalInput").ap()
    wq = nc.dram_tensor("wq", [128, 16, 16, 128], F32, kind="ExternalInput").ap()
    wk = nc.dram_tensor("wk", [128, 16, 4, 128], F32, kind="ExternalInput").ap()
    wv = nc.dram_tensor("wv", [128, 16, 512], F32, kind="ExternalInput").ap()
    wo = nc.dram_tensor("wo", [128, 16, 2048], F32, kind="ExternalInput").ap()
    wgu = nc.dram_tensor("wgu", [NHC, 128, 2, 16, 128], F32, kind="ExternalInput").ap()
    wd = nc.dram_tensor("wd", [2, NHC, 128, 1024], F32, kind="ExternalInput").ap()
    cosf = nc.dram_tensor("cosf", [128, T], F32, kind="ExternalInput").ap()
    sinf = nc.dram_tensor("sinf", [128, T], F32, kind="ExternalInput").ap()
    trilm = nc.dram_tensor("trilm", [128, 512], F32, kind="ExternalInput").ap()
    ident = nc.dram_tensor("ident", [128, 128], BF16, kind="ExternalInput").ap()
    pmask = nc.dram_tensor("pmask", [128, 8], F32, kind="ExternalInput").ap()
    y = nc.dram_tensor("y", [T, D], F32, kind="ExternalOutput").ap()

    xnfm_h = nc.dram_tensor("xnfm_h", [NCH, 128, 16, TC], BF16).ap()
    qf_h = nc.dram_tensor("qf_h", [NCH, 128, 16, TC], BF16).ap()
    kff_h = nc.dram_tensor("kff_h", [NCH, 128, 4, TC], BF16).ap()
    kft_h = nc.dram_tensor("kft_h", [NCH, 128, NSUB, 4, 128], BF16).ap()
    va_h = nc.dram_tensor("va_h", [NCH, 128, NSUB, 4, VW], BF16).ap()
    ya_h = nc.dram_tensor("ya_h", [NSUBT, 128, 2048], F32).ap()
    gat_h = nc.dram_tensor("gat_h", [NHC, NCH, 128, TC], BF16).ap()
    cc_in_t = nc.dram_tensor("cc_in", [4 * 128, VW], F32)
    cc_out_t = nc.dram_tensor("cc_out", [8 * 4 * 128, VW], F32)

    dbg = {}
    if debug:
        def dbg_out(name, shape, dt=F32):
            dbg[name] = nc.dram_tensor(name, shape, dt, kind="ExternalOutput").ap()
        if stop_after == "P1":
            dbg_out("d_xnfm", [NCH, 128, 16, TC], BF16)
            dbg_out("d_qf", [NCH, 128, 16, TC], BF16)
            dbg_out("d_kff", [NCH, 128, 4, TC], BF16)
            dbg_out("d_kft", [NCH, 128, NSUB, 4, 128], BF16)
            dbg_out("d_va", [NCH, 128, NSUB, 4, VW], BF16)
            dbg_out("d_sloc", [128, 4, VW], F32)
        if stop_after == "CC":
            dbg_out("d_sinit", [128, 4, VW], F32)
        if stop_after == "P2":
            dbg_out("d_ya", [NSUBT, 128, 2048], F32)
            dbg_out("d_sinit", [128, 4, VW], F32)
        if stop_after == "P3":
            dbg_out("d_gat", [NHC, NCH, 128, TC], BF16)

    sems = {}
    for sname in ("sT", "sV", "sS", "sP", "sC", "qS", "qA", "qG",
                  "cst", "wwq", "wwo", "wds", "glx",
                  "xl0", "xl1", "cs0", "cs1", "ls0", "ls1",
                  "qfw0", "qfw1", "wg0", "wg1", "wg2",
                  "gwa", "gwb", "gc0", "gc1", "ld40", "ld41",
                  "xnl0", "xnl1", "xnl2", "xnl3", "xnl4", "xnl5", "xnl6", "xnl7", "kfw"):
        sems[sname] = ctx.enter_context(nc.semaphore(sname))
    P = Plan(nc, sems)

    per = contextlib.ExitStack()
    ident_sb = per.enter_context(nc.sbuf_tensor("ident_sb", [128, 128], BF16))
    tril_sb = per.enter_context(nc.sbuf_tensor("tril_sb", [128, 512], F32))
    pm_sb = per.enter_context(nc.sbuf_tensor("pm_sb", [128, 8], F32))
    S_sb = per.enter_context(nc.sbuf_tensor("S_sb", [128, 4, VW], F32))
    S_bf = per.enter_context(nc.sbuf_tensor("S_bf", [128, 4, VW], BF16))

    ph = contextlib.ExitStack()
    sb = lambda n, shp, dt: ph.enter_context(nc.sbuf_tensor(n, shp, dt))
    ps = lambda n, shp, dt: ph.enter_context(nc.psum_tensor(n, shp, dt))

    xsb = sb("xsb", [128, 2, D], F32)
    sqs = sb("sqs", [128, D], BF16)
    lnA = sb("lnA", [128, 2, 8], F32)
    xntm = sb("xntm", [128, 4, D], BF16)
    xnfm = sb("xnfm", [128, 2, 16, TC], BF16)
    cossb = sb("cossb", [128, 2, 2, TC], F32)
    wq_sb = sb("wq_sb", [128, 16, 16, 128], BF16)
    wk_sb = sb("wk_sb", [128, 16, 4, 128], BF16)
    wv_sb = sb("wv_sb", [128, 16, 512], BF16)
    qf_sb = sb("qf_sb", [128, 2, TC], BF16)
    kff_sb = sb("kff_sb", [128, 4, TC], BF16)
    kft_sb = sb("kft_sb", [128, NSUB, 4, 128], BF16)
    va_sb = sb("va_sb", [128, NSUB, 4, VW], BF16)
    S_loc = sb("S_loc", [128, 4, VW], F32)
    ra = sb("ra", [128, TC], F32)
    rb = sb("rb", [128, TC], F32)
    rc = sb("rc", [128, TC], F32)
    re = sb("re", [128, TC], F32)
    qev = sb("qev", [128, 2, TC], F32)

    psq = ps("psq", [128, 2, 512], F32)
    psv = ps("psv", [128, 2, 512], F32)
    pT = ps("pT", [128, 2, 1024], BF16)
    pstA = ps("pstA", [128, 512], F32)
    pstB = ps("pstB", [128, 512], F32)

    P.add("sync", lambda e: e.dma_start(out=ident_sb[:], in_=ident[:, :]), inc="cst", inc_by=16)
    P.add("sync", lambda e: e.dma_start(out=tril_sb[:], in_=trilm[:, :]), inc="cst", inc_by=16)
    P.add("sync", lambda e: e.dma_start(out=pm_sb[:], in_=pmask[:, :]), inc="cst", inc_by=16)
    c_consts = P.cnt["cst"]
    P.add("gpsimd", lambda e: e.dma_start(out=wq_sb[:], in_=wq[:, :, :, :]), inc="wwq", inc_by=16)
    P.add("gpsimd", lambda e: e.dma_start(out=wk_sb[:], in_=wk[:, :, :, :]), inc="wwq", inc_by=16)
    wv_ld = P.add("gpsimd", lambda e: e.dma_start(out=wv_sb[:], in_=wv[:, :, :]), inc="wwq", inc_by=16)
    P.add("vector", lambda e: e.memset(S_loc[:], 0.0), inc="sV")
    vz = P.add("vector", lambda e: e.memset(va_sb[:, :, :, 128:VW], 0.0), inc="sV")
    va_ones = P.add("vector", lambda e: e.memset(va_sb[:, :, :, 128:129], 1.0),
                    waits=[("sV", vz)], inc="sV")

    xload = [0] * NSUBT
    xn_done = [0] * NSUBT
    sq_done = [0] * NSUBT
    xnc_done = [0] * NCH
    xnfm_dma = [0] * NCH
    cos_ld = [0] * NCH
    cos_sem = [""] * NCH
    tr_last_of_sub = {}
    rope_prev = {}
    vmm_last = [0] * NCH
    kw_q = [0] * NCH
    kff_dma = [0] * NCH
    st_addv = {}
    psv_cp = {}
    qf_w = {}
    b2_v = {}
    tcopy = {}
    tr_cnt = 0
    hg_idx = 0
    rope_state = {"se": 0, "vc1": 0, "vc2": 0, "vb2": 0}

    def plan_rope_fmap(ci, slot_par, hg, final_fn, final_waits):
        cs = cossb[:, ci % 2, 0, :]
        sn = cossb[:, ci % 2, 1, :]
        mmv = P.cnt["sT"]
        pa2, pb2, _ = rope_prev.get(hg - 2, (0, 0, 0))
        s_ev = P.add("scalar", lambda e: e.copy(qev[:, slot_par, :], psq[:, slot_par, :]),
                     waits=[("sT", mmv), ("sP", pb2)], inc="sS")
        pw = [("sS", s_ev), (cos_sem[ci], cos_ld[ci])]
        if hg >= 1:
            pw.append(("sV", rope_state["vc1"]))
        va = P.add("gpsimd", lambda e: e.tensor_tensor(
            ra[:], qev[:, slot_par, :], cs, ALU.mult),
            waits=pw, inc="sP")
        P.add("vector", lambda e: e.tensor_tensor(
            rb[0:64, :], psq[64:128, slot_par, :], sn[0:64, :], ALU.mult),
            waits=[("sS", s_ev), (cos_sem[ci], cos_ld[ci]), ("sV", rope_state["vb2"])], inc="sV")
        b2 = P.add("vector", lambda e: e.tensor_tensor(
            rb[64:128, :], psq[0:64, slot_par, :], sn[64:128, :], ALU.mult),
            inc="sV")
        vc = P.add("vector", lambda e: e.tensor_tensor(
            rc[:], ra[:], rb[:, :], ALU.add),
            waits=[("sP", va)], inc="sV")
        vm = P.add("vector", lambda e: e.tensor_scalar(
            rb[:], rc[:], 0.0, None, ALU.min),
            waits=[("sS", rope_state["se"])], inc="sV")
        se = P.add("scalar", lambda e: e.activation(re[:], rb[:], AF.Exp),
                   waits=[("sV", vm)], inc="sS")
        ve = P.add("vector", lambda e_=None: None, waits=()) if False else None
        ve = P.add("vector", final_fn, waits=[("sS", se)] + list(final_waits), inc="sV")
        rope_state["se"] = se
        rope_state["vc2"] = rope_state.get("vc1", 0)
        rope_state["vc1"] = vc
        rope_state["vb2"] = b2
        rope_prev[hg] = (b2, va, s_ev)
        return s_ev, ve

    def plan_ln_sub(u):
        ci_, si_ = divmod(u, NSUB)
        p = u % 2
        w = []
        if u >= 2:
            w = [("sV", xn_done[u - 2]), ("sS", sq_done[u - 2])]
        xsem = "xl0" if p == 0 else "xl1"
        xload[u] = P.add("sync", lambda e, u=u, p=p: e.dma_start(
            out=xsb[:, p, :], in_=x[u * 128:(u + 1) * 128, :]),
            waits=w, inc=xsem, inc_by=16)
        P.add("scalar", lambda e, p=p: e.activation(
            sqs[:], xsb[:, p, :], AF.Identity, accum_out=lnA[:, p, 0:1]),
            waits=[(xsem, xload[u])], inc="sS")
        sq_done[u] = P.add("scalar", lambda e, p=p: e.activation(
            sqs[:], xsb[:, p, :], AF.Square, accum_out=lnA[:, p, 1:2]),
            waits=[(xsem, xload[u])], inc="sS")
        v1 = P.add("vector", lambda e, p=p: e.tensor_scalar(
            lnA[:, p, 2:3], lnA[:, p, 0:1], 1.0 / D, None, ALU.mult),
            waits=[("sS", sq_done[u])], inc="sV")
        v2 = P.add("vector", lambda e, p=p: e.tensor_scalar(
            lnA[:, p, 3:4], lnA[:, p, 1:2], 1.0 / D, None, ALU.mult),
            waits=[("sS", sq_done[u])], inc="sV")
        v3 = P.add("vector", lambda e, p=p: e.tensor_tensor(
            lnA[:, p, 7:8], lnA[:, p, 2:3], lnA[:, p, 2:3], ALU.mult),
            waits=[("sV", v1)], inc="sV")
        v4 = P.add("vector", lambda e, p=p: e.scalar_tensor_tensor(
            lnA[:, p, 4:5], lnA[:, p, 3:4], EPS_LN, lnA[:, p, 7:8],
            ALU.add, ALU.subtract), waits=[("sV", v3)], inc="sV")
        sl_ = P.add("scalar", lambda e, p=p: e.activation(
            lnA[:, p, 7:8], lnA[:, p, 4:5], AF.Ln),
            waits=[("sV", v4)], inc="sS")
        s2 = P.add("scalar", lambda e, p=p: e.activation(
            lnA[:, p, 5:6], lnA[:, p, 7:8], AF.Exp, scale=-0.5),
            waits=[("sS", sl_)], inc="sS")
        v5 = P.add("vector", lambda e, p=p: e.tensor_tensor(
            lnA[:, p, 6:7], lnA[:, p, 2:3], lnA[:, p, 5:6], ALU.mult),
            waits=[("sS", s2)], inc="sV")
        w = [("sV", v5)]
        if u >= 4:
            assert (u - 4) in tr_last_of_sub
            w.append(("sT", tr_last_of_sub[u - 4]))
        xn_done[u] = P.add("vector", lambda e, p=p, p4=u % 4: e.tensor_scalar(
            xntm[:, p4, :], xsb[:, p, :], lnA[:, p, 5:6], lnA[:, p, 6:7],
            ALU.mult, ALU.subtract), waits=w, inc="sV")

    pending_qf = []

    def flush_qf(n=None):
        while pending_qf and (n is None or len(pending_qf) > n):
            ci_, h_, qslot_, ve_ = pending_qf.pop(0)
            qf_w[ci_ * 16 + h_] = P.add(
                "scalar", lambda e, ci=ci_, h=h_, qslot=qslot_: e.dma_start(
                    out=qf_h[ci, :, h, :], in_=qf_sb[:, qslot, :]),
                waits=[("sV", ve_)], inc=("qfw0" if qslot_ == 0 else "qfw1"),
                inc_by=16)

    for si in range(NSUB):
        plan_ln_sub(si)

    for ci in range(NCH):
        cp = ci % 2
        w = []
        if ci >= 2:
            pa, pb, _ = rope_prev.get((ci - 2) * 20 + 19, (0, 0, 0))
            w = [("sV", pa), ("sP", pb)]
        csem = "cs0" if ci % 2 == 0 else "cs1"
        P.add("sync", lambda e, ci=ci: e.dma_start(
            out=cossb[:, ci % 2, 0, :], in_=cosf[:, ci * TC:(ci + 1) * TC]),
            waits=w, inc=csem, inc_by=16)
        cos_ld[ci] = P.add("sync", lambda e, ci=ci: e.dma_start(
            out=cossb[:, ci % 2, 1, :], in_=sinf[:, ci * TC:(ci + 1) * TC]),
            inc=csem, inc_by=16)
        cos_sem[ci] = csem

        for si in range(NSUB):
            u = ci * NSUB + si
            p = u % 2
            for dc in range(16):
                tp = tr_cnt % 2
                w = [("sV", xn_done[u]), ("cst", c_consts)]
                if tr_cnt >= 2:
                    w.append(("sS", tcopy[tr_cnt - 2]))
                tt = P.add("tensor", lambda e, p4=u % 4, dc=dc, tp=tp: e.transpose(
                    pT[:, tp, 0:128], xntm[:, p4, dc * 128:(dc + 1) * 128], ident_sb[:]),
                    waits=w, inc="sT")
                w2 = [("sT", tt)]
                if si == 0 and dc == 0 and ci >= 2:
                    w2.append(("qA", kw_q[ci - 1]))
                tcopy[tr_cnt] = P.add("scalar", lambda e, ci=ci, dc=dc, si=si, tp=tp: e.copy(
                    xnfm[:, ci % 2, dc, si * 128:(si + 1) * 128], pT[:, tp, 0:128]),
                    waits=w2, inc="sS")
                tr_cnt += 1
            tr_last_of_sub[u] = P.cnt["sT"]
        xnc_done[ci] = P.cnt["sS"]
        xnfm_dma[ci] = P.add("scalar", lambda e, ci=ci: e.dma_start(
            out=xnfm_h[ci, :, :, :], in_=xnfm[:, ci % 2, :, :]),
            waits=[("sS", xnc_done[ci])], inc="qA", inc_by=16)

        for h in range(16):
            qh_idx = ci * 16 + h
            slot = hg_idx % 2
            qslot = qh_idx % 2
            flush_qf(1)
            w = [("sS", xnc_done[ci]), ("wwq", wv_ld)]
            if hg_idx >= 2:
                sv2, pb2v, se2 = rope_prev.get(hg_idx - 2, (0, 0, 0))
                w.append(("sS", se2))
                w.append(("sV", sv2))
            for dc in range(16):
                ww = w if dc == 0 else ()
                P.add("tensor", lambda e, ci=ci, h=h, dc=dc, slot=slot: e.matmul(
                    psq[:, slot, :], wq_sb[:, dc, h, :], xnfm[:, ci % 2, dc, :],
                    start=(dc == 0), stop=(dc == 15)),
                    waits=ww, inc=("sT" if dc == 15 else None))
            fw = [(("qfw0" if qslot == 0 else "qfw1"), qf_w.get(qh_idx - 2, 0))] \
                if qh_idx >= 2 else []
            b2, ve = plan_rope_fmap(
                ci, slot, hg_idx,
                (lambda qslot: lambda e: e.scalar_tensor_tensor(
                    qf_sb[:, qslot, :], rc[:], 0.0, re[:], ALU.max, ALU.add))(qslot), fw)
            b2_v[hg_idx] = b2
            pending_qf.append((ci, h, qslot, ve))
            hg_idx += 1
            if ci + 1 < NCH and h in (4, 6, 8, 10):
                plan_ln_sub((ci + 1) * NSUB + (h - 4) // 2)

        for g in range(4):
            slot = hg_idx % 2
            flush_qf(1)
            w = [("sS", xnc_done[ci]), ("wwq", wv_ld)]
            if hg_idx >= 2:
                sv2, pb2v, se2 = rope_prev.get(hg_idx - 2, (0, 0, 0))
                w.append(("sS", se2))
                w.append(("sV", sv2))
            for dc in range(16):
                ww = w if dc == 0 else ()
                P.add("tensor", lambda e, ci=ci, g=g, dc=dc, slot=slot: e.matmul(
                    psq[:, slot, :], wk_sb[:, dc, g, :], xnfm[:, ci % 2, dc, :],
                    start=(dc == 0), stop=(dc == 15)),
                    waits=ww, inc=("sT" if dc == 15 else None))
            kwf = [("kfw", kff_dma[ci - 1])] if (g == 0 and ci > 0) else []
            b2, ve = plan_rope_fmap(
                ci, slot, hg_idx,
                (lambda g: lambda e: e.scalar_tensor_tensor(
                    kff_sb[:, g, :], rc[:], 0.0, re[:], ALU.max, ALU.add))(g), kwf)
            b2_v[hg_idx] = b2
            hg_idx += 1
        kff_done = P.cnt["sV"]
        flush_qf(0)
        for g in range(4):
            for si in range(NSUB):
                tp = tr_cnt % 2
                w = [("sV", kff_done)]
                if tr_cnt >= 2:
                    w.append(("sS", tcopy[tr_cnt - 2]))
                tt = P.add("tensor", lambda e, g=g, si=si, tp=tp: e.transpose(
                    pT[:, tp, 0:128], kff_sb[:, g, si * 128:(si + 1) * 128], ident_sb[:]),
                    waits=w, inc="sT")
                w2 = [("sT", tt)]
                if g == 0 and si == 0 and ci > 0:
                    w2.append(("qA", P.cnt["qA"]))
                tcopy[tr_cnt] = P.add("scalar", lambda e, g=g, si=si, tp=tp: e.copy(
                    kft_sb[:, si, g, :], pT[:, tp, 0:128]),
                    waits=w2, inc="sS")
                tr_cnt += 1
        kft_done = P.cnt["sS"]
        kff_dma[ci] = P.add("scalar", lambda e, ci=ci: e.dma_start(
            out=kff_h[ci, :, :, :], in_=kff_sb[:, :, :]),
            waits=[("sV", kff_done)], inc="kfw", inc_by=16)
        P.add("scalar", lambda e, ci=ci: e.dma_start(
            out=kft_h[ci, :, :, :, :], in_=kft_sb[:, :, :, :]),
            waits=[("sS", kft_done)], inc="qA", inc_by=16)

        for si in range(NSUB):
            gv = ci * NSUB + si
            vs = gv % 2
            w = [("sS", xnc_done[ci]), ("wwq", wv_ld), ("sV", va_ones)]
            if gv >= 2:
                w.append(("sS", psv_cp.get(gv - 2, 0)))
            for dc in range(16):
                ww = w if dc == 0 else ()
                P.add("tensor", lambda e, ci=ci, si=si, dc=dc, vs=vs: e.matmul(
                    psv[:, vs, :], xnfm[:, ci % 2, dc, si * 128:(si + 1) * 128],
                    wv_sb[:, dc, :], start=(dc == 0), stop=(dc == 15)),
                    waits=ww, inc=("sT" if dc == 15 else None))
            mv = P.cnt["sT"]
            last = 0
            for g in range(4):
                w2 = [("sT", mv)]
                if si == 0 and g == 0 and ci > 0:
                    w2.append(("qA", P.cnt["qA"]))
                last = P.add("scalar", lambda e, si=si, g=g, vs=vs: e.copy(
                    va_sb[:, si, g, 0:128], psv[:, vs, g * 128:(g + 1) * 128]),
                    waits=w2, inc="sS")
            psv_cp[gv] = last
        va_all = P.cnt["sS"]
        P.add("scalar", lambda e, ci=ci: e.dma_start(
            out=va_h[ci, :, :, :, :], in_=va_sb[:, :, :, :]),
            waits=[("sS", va_all)], inc="qA", inc_by=16)
        kw_q[ci] = P.cnt["qA"]

        for si in range(NSUB):
            for g in range(4):
                gi = ci * 16 + si * 4 + g
                pst = pstA if gi % 2 == 0 else pstB
                w = [("sS", va_all), ("sS", kft_done)]
                if gi >= 2:
                    w.append(("sV", st_addv.get(gi - 2, 0)))
                ms = P.add("tensor", lambda e, si=si, g=g, pst=pst: e.matmul(
                    pst[:, 0:129], kft_sb[:, si, g, :], va_sb[:, si, g, 0:129],
                    start=True, stop=True), waits=w, inc="sT")
                st_addv[gi] = P.add("vector", lambda e, g=g, pst=pst: e.tensor_tensor(
                    S_loc[:, g, 0:129], S_loc[:, g, 0:129], pst[:, 0:129], ALU.add),
                    waits=[("sT", ms)], inc="sV")
        vmm_last[ci] = P.cnt["sT"]

    p1_V = P.cnt["sV"]

    if debug and stop_after == "P1":
        w = [("sV", p1_V), ("sS", P.cnt["sS"]), ("qA", P.cnt["qA"])]
        for name, src in (("d_sloc", None),):
            pass
        P.add("gpsimd", lambda e: e.dma_start(out=dbg["d_sloc"][:, :, :], in_=S_loc[:]),
              waits=w, inc="qG", inc_by=16)
        for name, src in (("d_xnfm", xnfm_h), ("d_qf", qf_h), ("d_kff", kff_h)):
            P.add("gpsimd", lambda e, name=name, src=src: e.dma_start(
                out=dbg[name][:, :, :, :], in_=src[:, :, :, :]), inc="qG", inc_by=16)
        for name, src in (("d_kft", kft_h), ("d_va", va_h)):
            P.add("gpsimd", lambda e, name=name, src=src: e.dma_start(
                out=dbg[name][:, :, :, :, :], in_=src[:, :, :, :, :]), inc="qG", inc_by=16)
        P.wait_only("gpsimd", [("qG", P.cnt["qG"])])
        P.emit()
        ph.close(); per.close(); ctx.close()
        return nc, dbg

    P.add("gpsimd", lambda e: e.dma_start(
        out=cc_in_t.ap().rearrange("(g p) f -> p g f", p=128), in_=S_loc[:, :, :]),
        waits=[("sV", p1_V)], inc="qG", inc_by=16)
    cw = P.cnt["qG"]
    P.add("gpsimd", lambda e: e.collective_compute(
        "AllGather", ALU.bypass, replica_groups=[list(range(8))],
        ins=[cc_in_t.ap().opt()], outs=[cc_out_t.ap().opt()]),
        waits=[("qG", cw)], inc="sC")

    P.emit()
    ph.close()

    ph = contextlib.ExitStack()
    sb = lambda n, shp, dt: ph.enter_context(nc.sbuf_tensor(n, shp, dt))
    ps = lambda n, shp, dt: ph.enter_context(nc.psum_tensor(n, shp, dt))
    gath_sb = sb("gath_sb", [128, 8, 4, VW], F32)

    P.barrier()
    gl = P.add("gpsimd", lambda e: e.dma_start(
        out=gath_sb[:, :, :, :],
        in_=cc_out_t.ap().rearrange("(r g p) f -> p r g f", p=128, g=4)),
        waits=[("sC", P.cnt["sC"])], inc="glx", inc_by=16)
    pv = P.add("vector", lambda e: e.memset(S_sb[:], 0.0), waits=[("glx", gl)], inc="sV")
    for r in range(8):
        pv = P.add("vector", lambda e, r=r: e.scalar_tensor_tensor(
            S_sb.ap().rearrange("p g f -> p (g f)"),
            gath_sb[:, r, :, :].rearrange("p g f -> p (g f)"),
            pm_sb[:, r:r + 1],
            S_sb.ap().rearrange("p g f -> p (g f)"),
            ALU.mult, ALU.add),
            waits=[("cst", c_consts), ("sV", pv)], inc="sV")
    sbf_init = P.add("vector", lambda e: e.tensor_copy(S_bf[:], S_sb[:]),
                     waits=[("sV", pv)], inc="sV")

    if debug and stop_after == "CC":
        P.add("gpsimd", lambda e: e.dma_start(out=dbg["d_sinit"][:, :, :], in_=S_sb[:]),
              waits=[("sV", sbf_init)], inc="qG", inc_by=16)
        P.wait_only("gpsimd", [("qG", P.cnt["qG"])])
        P.emit()
        ph.close(); per.close(); ctx.close()
        return nc, dbg

    qbuf = sb("qbuf", [128, 2, 16, TC], BF16)
    kfb = sb("kfb", [128, 2, 4, TC], BF16)
    ktb = sb("ktb", [128, 2, NSUB, 4, 128], BF16)
    vab = sb("vab", [128, 2, NSUB, 4, VW], BF16)
    wo_sb = sb("wo_sb", [128, 16, 2048], BF16)
    abf = sb("abf", [128, 2, 4, 128], BF16)
    att_tm = sb("att_tm", [128, 2, 128], BF16)
    afm = sb("afm", [128, 2, 16, 128], BF16)
    st2 = sb("st2", [128, 2, 2], F32)
    y_sb = sb("y_sb", [128, 2, 2048], F32)

    pscr = ps("pscr", [128, 512], F32)
    pRp = ps("pRp", [128, 2, 512], F32)
    pTr = ps("pTr", [128, 1024], BF16)
    pW = ps("pW", [128, 2048], F32)

    wo_ld = P.add("gpsimd", lambda e: e.dma_start(out=wo_sb[:], in_=wo[:, :, :]),
                  inc="wwo", inc_by=16)

    ld2 = [0] * NCH
    ld2_sem = [""] * NCH
    msk_v = {}
    nrm_v = {}
    afc_s = {}
    yev_s = [0] * NSUBT
    yaw_q = [0] * NSUBT
    stadd_v = [0] * (NSUBT + 1)
    stadd_v[0] = sbf_init
    sbf_u = [0] * (NSUBT + 1)
    sbf_u[0] = sbf_init
    mmst_t = [0] * NSUBT

    for ci in range(NCH):
        w = []
        if ci >= 2:
            w.append(("sT", mmst_t[(ci - 2) * NSUB + NSUB - 1]))
        lsem = "ls0" if ci % 2 == 0 else "ls1"
        for fn in (
            lambda e, ci=ci: e.dma_start(out=qbuf[:, ci % 2], in_=qf_h[ci]),
            lambda e, ci=ci: e.dma_start(out=kfb[:, ci % 2], in_=kff_h[ci]),
            lambda e, ci=ci: e.dma_start(out=ktb[:, ci % 2], in_=kft_h[ci]),
            lambda e, ci=ci: e.dma_start(out=vab[:, ci % 2], in_=va_h[ci]),
        ):
            ld2[ci] = P.add("sync", fn, waits=w, inc=lsem, inc_by=16)
            w = []
        ld2_sem[ci] = lsem

        for si in range(NSUB):
            u = ci * NSUB + si
            up = u % 2
            sl = slice(si * 128, (si + 1) * 128)
            mmW = 0

            def plan_tr_afc(h, u=u, up=up):
                hu = u * 16 + h
                hp = h % 2
                w = [("sV", nrm_v[hu])]
                if hu >= 1:
                    w.append(("sS", afc_s.get(hu - 1, 0)))
                tt = P.add("tensor", lambda e, hp=hp: e.transpose(
                    pTr[:, 0:128], att_tm[:, hp, :], ident_sb[:]),
                    waits=w, inc="sT")
                afc_s[hu] = P.add("scalar", lambda e, up=up, h=h: e.copy(
                    afm[:, up, h, :], pTr[:, 0:128]),
                    waits=[("sT", tt)], inc="sS")

            def plan_wo(h, u=u, up=up):
                nonlocal mmW
                w = [("wwo", wo_ld), ("sS", afc_s[u * 16 + h]), ("sV", stadd_v[u])]
                if h == 0 and u >= 1:
                    w.append(("sS", yev_s[u - 1]))
                first = True
                for os_ in range(4):
                    ww = w if first else ()
                    first = False
                    mmW = P.add("tensor", lambda e, up=up, h=h, os_=os_: e.matmul(
                        pW[:, os_ * 512:(os_ + 1) * 512], afm[:, up, h, :],
                        wo_sb[:, h, os_ * 512:(os_ + 1) * 512],
                        start=(h == 0), stop=(h == 15), skip_group_check=True),
                        waits=ww, inc=("sT" if (h == 15 and os_ == 3) else None))

            for h in range(16):
                g = h // 4
                hp = h % 2
                hu = u * 16 + h
                if h % 4 == 0:
                    w = [(ld2_sem[ci], ld2[ci])]
                    qk = u * 4 + g
                    if qk >= 1:
                        w.append(("sV", msk_v.get(qk - 1, 0)))
                    sc = P.add("tensor", lambda e, ci=ci, g=g, sl=sl: e.matmul(
                        pscr[:, 0:512], kfb[:, ci % 2, g, sl],
                        qbuf[:, ci % 2, 4 * g:4 * g + 4, sl],
                        start=True, stop=True), waits=w, inc="sT")
                    msk_v[qk] = P.add("vector", lambda e, hp2=(g % 2): e.tensor_tensor(
                        abf[:, hp2, :, :], pscr.ap().rearrange("p (a f) -> p a f", a=4),
                        tril_sb.ap().rearrange("p (a f) -> p a f", a=4), ALU.mult),
                        waits=[("sT", sc)], inc="sV")
                w = [("sV", msk_v[u * 4 + g]), ("sV", sbf_u[u])]
                if hu >= 2:
                    w.append(("sV", nrm_v.get(hu - 2, 0)))
                P.add("tensor", lambda e, ci=ci, si=si, g=g, h=h, hp=hp: e.matmul(
                    pRp[:, hp, 0:129], abf[:, g % 2, h % 4, :],
                    vab[:, ci % 2, si, g, 0:129],
                    start=True, stop=False), waits=w)
                au = P.add("tensor", lambda e, ci=ci, h=h, g=g, hp=hp, sl=sl: e.matmul(
                    pRp[:, hp, 0:129], qbuf[:, ci % 2, h, sl], S_bf[:, g, 0:129],
                    start=False, stop=True), inc="sT")
                n1 = P.add("vector", lambda e, hp=hp: e.tensor_scalar(
                    st2[:, hp, 0:1], pRp[:, hp, 128:129], EPS_A, None, ALU.add),
                    waits=[("sT", au)], inc="sV")
                n2 = P.add("vector", lambda e, hp=hp: e.reciprocal(
                    st2[:, hp, 1:2], st2[:, hp, 0:1]), waits=[("sV", n1)], inc="sV")
                nrm_v[hu] = P.add("vector", lambda e, hp=hp: e.tensor_scalar(
                    att_tm[:, hp, :], pRp[:, hp, 0:128], st2[:, hp, 1:2], None, ALU.mult),
                    waits=[("sV", n2)], inc="sV")
                if h >= 1:
                    plan_tr_afc(h - 1)
                if h >= 2:
                    plan_wo(h - 2)
            plan_tr_afc(15)
            plan_wo(14)
            plan_wo(15)
            w = [("sT", mmW)]
            if u >= 2:
                w.append(("qA", P.cnt["qA"]))
            yev_s[u] = P.add("scalar", lambda e, up=up: e.copy(
                y_sb[:, up, :], pW[:]), waits=w, inc="sS")
            yaw_q[u] = P.add("scalar", lambda e, u=u, up=up: e.dma_start(
                out=ya_h[u], in_=y_sb[:, up, :]),
                waits=[("sS", yev_s[u])], inc="qA", inc_by=16)
            w = [("sS", yev_s[u]), (ld2_sem[ci], ld2[ci])]
            mmst = 0
            for g in range(4):
                ww = w if g == 0 else ()
                mmst = P.add("tensor", lambda e, ci=ci, si=si, g=g: e.matmul(
                    pW[:, g * 512:g * 512 + 129], ktb[:, ci % 2, si, g, :],
                    vab[:, ci % 2, si, g, 0:129], start=True, stop=True),
                    waits=ww, inc=("sT" if g == 3 else None))
            mmst_t[u] = mmst
            for g in range(4):
                P.add("vector", lambda e, g=g: e.tensor_tensor(
                    S_sb[:, g, 0:129], S_sb[:, g, 0:129],
                    pW[:, g * 512:g * 512 + 129], ALU.add),
                    waits=[("sT", mmst)], inc="sV")
            stadd_v[u + 1] = P.cnt["sV"]
            sbf_u[u + 1] = P.add("vector", lambda e: e.tensor_copy(S_bf[:], S_sb[:]),
                                 waits=[("sV", stadd_v[u + 1])], inc="sV")

    if debug and stop_after == "P2":
        w = [("qA", P.cnt["qA"]), ("sV", P.cnt["sV"])]
        P.add("gpsimd", lambda e: e.dma_start(out=dbg["d_ya"][:, :, :], in_=ya_h[:, :, :]),
              waits=w, inc="qG", inc_by=16)
        P.add("gpsimd", lambda e: e.dma_start(out=dbg["d_sinit"][:, :, :], in_=S_sb[:]),
              inc="qG", inc_by=16)
        P.wait_only("gpsimd", [("qG", P.cnt["qG"])])
        P.emit()
        ph.close(); per.close(); ctx.close()
        return nc, dbg

    P.barrier()
    P.emit()
    ph.close()
    ph = contextlib.ExitStack()
    sb = lambda n, shp, dt: ph.enter_context(nc.sbuf_tensor(n, shp, dt))
    ps = lambda n, shp, dt: ph.enter_context(nc.psum_tensor(n, shp, dt))

    xnr = sb("xnr", [128, 16, NCH, TC], BF16)
    wgub = sb("wgub", [128, 3, 2, 16, 128], BF16)
    gsb = sb("gsb", [128, 2, NCH, TC], BF16)
    ssb3 = sb("ssb3", [128, 2, 512], F32)
    pg = ps("pg", [128, 2, 512], F32)
    pu = ps("pu", [128, 2, 512], F32)

    xnl = [0] * NCH
    for ci in range(NCH):
        xnl[ci] = P.add("sync", lambda e, ci=ci: e.dma_start(
            out=xnr[:, :, ci, :], in_=xnfm_h[ci]), inc=f"xnl{ci}", inc_by=16)

    wgl = [0] * NHC
    mmU_t = {}
    gmul_v = {}
    gw_q = [0] * NHC
    for hc in range(NHC):
        w = []
        if hc >= 3:
            w.append(("sT", mmU_t[(hc - 3, NCH - 1)]))
        wgl[hc] = P.add("gpsimd", lambda e, hc=hc: e.dma_start(
            out=wgub[:, hc % 3], in_=wgu[hc]), waits=w, inc=f"wg{hc % 3}", inc_by=16)
        for tt in range(NCH):
            tp = tt % 2
            pidx = (hc, tt - 2) if tt >= 2 else (hc - 1, tt - 2 + NCH)
            w = [(f"wg{hc % 3}", wgl[hc]), (f"xnl{tt}", xnl[tt])]
            if (hc, tt) != (0, 0):
                w.append(("sV", gmul_v.get(pidx, 0)))
            for dc in range(16):
                ww = w if dc == 0 else ()
                P.add("tensor", lambda e, hc=hc, tt=tt, dc=dc, tp=tp: e.matmul(
                    pg[:, tp, :], wgub[:, hc % 3, 0, dc, :], xnr[:, dc, tt, :],
                    start=(dc == 0), stop=(dc == 15)),
                    waits=ww, inc=("sT" if dc == 15 else None))
            mg = P.cnt["sT"]
            for dc in range(16):
                P.add("tensor", lambda e, hc=hc, tt=tt, dc=dc, tp=tp: e.matmul(
                    pu[:, tp, :], wgub[:, hc % 3, 1, dc, :], xnr[:, dc, tt, :],
                    start=(dc == 0), stop=(dc == 15)),
                    inc=("sT" if dc == 15 else None))
            mmU_t[(hc, tt)] = P.cnt["sT"]
            w2 = [("sT", mg)]
            if (hc, tt) != (0, 0):
                w2.append(("sV", gmul_v.get(pidx, 0)))
            sl3 = P.add("scalar", lambda e, tp=tp: e.activation(
                ssb3[:, tp, :], pg[:, tp, :], AF.Silu), waits=w2, inc="sS")
            w2 = [("sS", sl3), ("sT", mmU_t[(hc, tt)])]
            if hc >= 2 and tt == 0:
                w2.append((("gwa" if hc % 2 == 0 else "gwb"), gw_q[hc - 2]))
            gmul_v[(hc, tt)] = P.add("vector", lambda e, hc=hc, tt=tt, tp=tp: e.tensor_tensor(
                gsb[:, hc % 2, tt, :], ssb3[:, tp, :], pu[:, tp, :], ALU.mult),
                waits=w2, inc="sV")
        gw_q[hc] = P.add("scalar", lambda e, hc=hc: e.dma_start(
            out=gat_h[hc].rearrange("a p f -> p a f"), in_=gsb[:, hc % 2]),
            waits=[("sV", gmul_v[(hc, NCH - 1)])],
            inc=("gwa" if hc % 2 == 0 else "gwb"), inc_by=16)

    if debug and stop_after == "P3":
        P.add("gpsimd", lambda e: e.dma_start(
            out=dbg["d_gat"][:, :, :, :], in_=gat_h[:, :, :, :]),
            waits=[("qA", P.cnt["qA"])], inc="qG", inc_by=16)
        P.wait_only("gpsimd", [("qG", P.cnt["qG"])])
        P.emit()
        ph.close(); per.close(); ctx.close()
        return nc, dbg

    P.barrier()
    P.emit()
    ph.close()
    ph = contextlib.ExitStack()
    sb = lambda n, shp, dt: ph.enter_context(nc.sbuf_tensor(n, shp, dt))
    ps = lambda n, shp, dt: ph.enter_context(nc.psum_tensor(n, shp, dt))

    wdb = sb("wdb", [128, NHC, 1024], BF16)
    gcol = sb("gcol", [128, 2, NHC, 128], BF16)
    xb4 = sb("xb4", [128, 2, 1024], F32)
    yab4 = sb("yab4", [128, 2, 1024], F32)
    obA = sb("obA", [128, 2, 1024], F32)
    ob = sb("ob", [128, 2, 1024], F32)
    pY = ps("pY", [128, 2, 1024], F32)

    cmb_v = {}
    out_q = {}
    mm_last = {}
    for half in range(2):
        wdl = [0] * NHC
        for hc in range(NHC):
            w = []
            if half == 1 and hc == 0:
                w.append(("sT", mm_last[(0, NSUBT - 1)]))
            wdl[hc] = P.add("gpsimd", lambda e, half=half, hc=hc: e.dma_start(
                out=wdb[:, hc, :], in_=wd[half, hc]), waits=w, inc="wds", inc_by=16)
        for u4 in range(NSUBT):
            up = u4 % 2
            w = []
            pidx = (half, u4 - 2) if u4 >= 2 else (half - 1, u4 - 2 + NSUBT)
            if (half, u4) != (0, 0) and (half, u4) != (0, 1):
                w.append(("sT", mm_last[pidx]))
            gl = 0
            gsem = "gc0" if up == 0 else "gc1"
            w2 = list(w)
            for part in range(8):
                eng = "sync" if part < 4 else "scalar"
                gl = P.add(eng, lambda e, u4=u4, up=up, part=part: e.dma_start(
                    out=gcol[:, up, part * 8:(part + 1) * 8, :],
                    in_=gat_h[part * 8:(part + 1) * 8, u4 // NSUB, :,
                              (u4 % NSUB) * 128:(u4 % NSUB + 1) * 128].rearrange(
                                  "a p f -> p a f")),
                    waits=(w if part < 4 else w2), inc=gsem, inc_by=16)
                if part < 4:
                    w = []
                else:
                    w2 = []
            w = []
            if (half, u4) != (0, 0) and (half, u4) != (0, 1):
                w.append(("sV", cmb_v.get(pidx, 0)))
                w.append(("qS", out_q.get(pidx, 0)))
            l4sem = "ld40" if up == 0 else "ld41"
            P.add("sync", lambda e, half=half, u4=u4, up=up: e.dma_start(
                out=xb4[:, up, :],
                in_=x[u4 * 128:(u4 + 1) * 128, half * 1024:(half + 1) * 1024]),
                waits=w, inc=l4sem, inc_by=16)
            yl = P.add("sync", lambda e, half=half, u4=u4, up=up: e.dma_start(
                out=yab4[:, up, :],
                in_=ya_h[u4, :, half * 1024:(half + 1) * 1024]),
                inc=l4sem, inc_by=16)
            first = True
            mmY = 0
            for os_ in range(2):
                for hc in range(NHC):
                    ww = [("wds", wdl[NHC - 1])]
                    if first:
                        ww.append((gsem, gl))
                        if (half, u4) != (0, 0) and (half, u4) != (0, 1):
                            ww.append(("sV", cmb_v.get(pidx, 0)))
                        first = False
                    mmY = P.add("tensor", lambda e, up=up, hc=hc, os_=os_: e.matmul(
                        pY[:, up, os_ * 512:(os_ + 1) * 512],
                        gcol[:, up, hc, :],
                        wdb[:, hc, os_ * 512:(os_ + 1) * 512],
                        start=(hc == 0), stop=(hc == NHC - 1)),
                        waits=ww, inc=("sT" if (hc == NHC - 1 and os_ == 1) else None))
            mm_last[(half, u4)] = mmY
            P.add("vector", lambda e, up=up: e.tensor_tensor(
                obA[:, up, :], pY[:, up, :], xb4[:, up, :], ALU.add),
                waits=[("sT", mmY), (l4sem, yl)], inc="sV")
            cmb_v[(half, u4)] = P.add("vector", lambda e, up=up: e.tensor_tensor(
                ob[:, up, :], obA[:, up, :], yab4[:, up, :], ALU.add), inc="sV")
            out_q[(half, u4)] = P.add("sync", lambda e, half=half, u4=u4, up=up: e.dma_start(
                out=y[u4 * 128:(u4 + 1) * 128, half * 1024:(half + 1) * 1024],
                in_=ob[:, up, :]),
                waits=[("sV", cmb_v[(half, u4)])], inc="qS", inc_by=16)

    P.wait_only("sync", [("qS", out_q[(1, NSUBT - 1)])])
    P.emit()
    ph.close(); per.close(); ctx.close()
    return nc, dbg


def _pack_inputs(L0_attn_x, Wq, Wk, Wv, Wo, Wg, Wu, Wd):
    f32 = np.float32
    xs = np.ascontiguousarray(np.asarray(L0_attn_x, dtype=f32).reshape(B, 4, T, D))
    wq_p = np.ascontiguousarray(
        np.asarray(Wq, f32).reshape(16, 128, 16, 128).transpose(3, 2, 0, 1))
    wk_p = np.ascontiguousarray(
        np.asarray(Wk, f32).reshape(4, 128, 16, 128).transpose(3, 2, 0, 1))
    wv_p = np.ascontiguousarray(
        np.asarray(Wv, f32).T.reshape(16, 128, 512).transpose(1, 0, 2))
    wo_p = np.ascontiguousarray(
        np.asarray(Wo, f32).T.reshape(16, 128, 2048).transpose(1, 0, 2))
    wg_t = np.asarray(Wg, f32).reshape(NHC, 128, 16, 128).transpose(0, 3, 2, 1)
    wu_t = np.asarray(Wu, f32).reshape(NHC, 128, 16, 128).transpose(0, 3, 2, 1)
    wgu_p = np.ascontiguousarray(np.stack([wg_t, wu_t], axis=2))
    wd_p = np.ascontiguousarray(
        np.asarray(Wd, f32).reshape(2, 1024, NHC, 128).transpose(0, 2, 3, 1))

    inv = 1.0 / (10000.0 ** (np.arange(0, HD, 2, dtype=np.float64) / HD))
    tril = np.ascontiguousarray(
        np.tile(np.triu(np.ones((128, 128), f32)), (1, 4)))
    identity = np.ascontiguousarray(np.eye(128, dtype=ml_dtypes.bfloat16))

    in_maps = []
    for core in range(8):
        b, c = divmod(core, 4)
        pos = np.arange(c * T, (c + 1) * T, dtype=np.float64)
        ang = np.outer(pos, inv)
        cosv = np.cos(ang).astype(f32)
        sinv = np.sin(ang).astype(f32)
        cos_fm = np.ascontiguousarray(np.concatenate([cosv, cosv], axis=1).T)
        sin_fm = np.ascontiguousarray(np.concatenate([-sinv, sinv], axis=1).T)
        pm = np.zeros((128, 8), f32)
        for r in range(8):
            rb_, rc_ = divmod(r, 4)
            if rb_ == b and rc_ < c:
                pm[:, r] = 1.0
        in_maps.append({
            "x": np.ascontiguousarray(xs[b, c]),
            "wq": wq_p, "wk": wk_p, "wv": wv_p, "wo": wo_p,
            "wgu": wgu_p, "wd": wd_p,
            "cosf": cos_fm, "sinf": sin_fm,
            "trilm": tril, "ident": identity,
            "pmask": pm,
        })
    return in_maps


def kernel(L0_attn_x, L0_attn_Wq, L0_attn_Wk, L0_attn_Wv, L0_attn_Wo,
           L0_ffn_w_gate, L0_ffn_w_up, L0_ffn_w_down):
    in_maps = _pack_inputs(L0_attn_x, L0_attn_Wq, L0_attn_Wk, L0_attn_Wv,
                           L0_attn_Wo, L0_ffn_w_gate, L0_ffn_w_up, L0_ffn_w_down)
    nc, _ = build_program(stop_after="P4", debug=False)
    res = run_bass_kernel_spmd(nc, in_maps, core_ids=list(range(8)))
    out = np.empty((B, S, D), dtype=np.float32)
    for core in range(8):
        b, c = divmod(core, 4)
        out[b, c * T:(c + 1) * T, :] = res.results[core]["y"]
    return out


# version 15 (final)
# speedup vs baseline: 1.1607x; 1.1052x over previous
import contextlib
import numpy as np
import ml_dtypes
import concourse.bass as bass
import concourse.mybir as mybir
from concourse.bass_utils import run_bass_kernel_spmd

F32 = mybir.dt.float32
BF16 = mybir.dt.bfloat16
AF = mybir.ActivationFunctionType
ALU = mybir.AluOpType
AX = mybir.AxisListType

B, S, D = 2, 16384, 2048
H, G, HD = 16, 4, 128
DH = 8192
T = S // 4
TC = 512
NCH = T // TC
NSUB = TC // 128
NSUBT = T // 128
NHC = DH // 128
EPS_LN = 1e-5
EPS_A = 1e-6
VW = 136

ENGINES = ("sync", "scalar", "vector", "tensor", "gpsimd")


class Plan:

    def __init__(self, nc, sems):
        self.nc = nc
        self.sems = sems
        self.cnt = {k: 0 for k in sems}
        self.ops = {e: [] for e in ENGINES}
        self.lastw = {e: {} for e in ENGINES}

    def add(self, eng, fn, waits=(), inc=None, inc_by=1):
        ws = []
        for s, v in waits:
            if v <= 0:
                continue
            if v > self.lastw[eng].get(s, -1):
                ws.append((s, v))
                self.lastw[eng][s] = v
        post = None
        if inc is not None:
            self.cnt[inc] += inc_by
            post = self.cnt[inc]
        self.ops[eng].append((fn, tuple(ws), inc, inc_by))
        return post

    def wait_only(self, eng, waits):
        self.add(eng, None, waits=waits)

    def emit(self):
        with self.nc.Block() as block:
            for eng in ENGINES:
                lst = self.ops[eng]
                if not lst:
                    continue

                def body(e, lst=lst):
                    for fn, ws, inc, by in lst:
                        for s, v in ws:
                            e.wait_ge(self.sems[s], v)
                        if fn is None:
                            continue
                        inst = fn(e)
                        if inc is not None:
                            inst.then_inc(self.sems[inc], by)

                getattr(block, eng)(body)
        self.ops = {e: [] for e in ENGINES}

    def barrier(self):
        for eng, q, m in (("sync", "qS", "sC"), ("scalar", "qA", "sS"),
                          ("gpsimd", "qG", "sP")):
            if self.cnt[q] > 0:
                self.cnt[m] += 1
                self.add(eng, (lambda m: lambda e: e.sem_inc(self.sems[m], 1))(m),
                         waits=[(q, self.cnt[q])])
        snap = [(k, v) for k, v in self.cnt.items()
                if v > 0 and k in ("sT", "sV", "sS", "sP", "sC")]
        for e in ENGINES:
            self.wait_only(e, snap)


def build_program(stop_after="P4", debug=False):
    nc = bass.Bass("TRN2", target_bir_lowering=False, debug=False)
    ctx = contextlib.ExitStack()

    x = nc.dram_tensor("x", [T, D], F32, kind="ExternalInput").ap()
    wq = nc.dram_tensor("wq", [128, 16, 16, 128], F32, kind="ExternalInput").ap()
    wk = nc.dram_tensor("wk", [128, 16, 4, 128], F32, kind="ExternalInput").ap()
    wv = nc.dram_tensor("wv", [128, 16, 512], F32, kind="ExternalInput").ap()
    wo = nc.dram_tensor("wo", [128, 16, 2048], F32, kind="ExternalInput").ap()
    wgu = nc.dram_tensor("wgu", [NHC, 128, 2, 16, 128], F32, kind="ExternalInput").ap()
    wd = nc.dram_tensor("wd", [2, NHC, 128, 1024], F32, kind="ExternalInput").ap()
    cosf = nc.dram_tensor("cosf", [128, T], F32, kind="ExternalInput").ap()
    sinf = nc.dram_tensor("sinf", [128, T], F32, kind="ExternalInput").ap()
    trilm = nc.dram_tensor("trilm", [128, 512], F32, kind="ExternalInput").ap()
    ident = nc.dram_tensor("ident", [128, 128], BF16, kind="ExternalInput").ap()
    pmask = nc.dram_tensor("pmask", [128, 8], F32, kind="ExternalInput").ap()
    y = nc.dram_tensor("y", [T, D], F32, kind="ExternalOutput").ap()

    xnfm_h = nc.dram_tensor("xnfm_h", [NCH, 128, 16, TC], BF16).ap()
    qf_h = nc.dram_tensor("qf_h", [NCH, 128, 16, TC], BF16).ap()
    kff_h = nc.dram_tensor("kff_h", [NCH, 128, 4, TC], BF16).ap()
    kft_h = nc.dram_tensor("kft_h", [NCH, 128, NSUB, 4, 128], BF16).ap()
    va_h = nc.dram_tensor("va_h", [NCH, 128, NSUB, 4, VW], BF16).ap()
    ya_h = nc.dram_tensor("ya_h", [NSUBT, 128, 2048], F32).ap()
    gat_h = nc.dram_tensor("gat_h", [NHC, NCH, 128, TC], BF16).ap()
    cc_in_t = nc.dram_tensor("cc_in", [4 * 128, VW], F32)
    cc_out_t = nc.dram_tensor("cc_out", [8 * 4 * 128, VW], F32)

    dbg = {}
    if debug:
        def dbg_out(name, shape, dt=F32):
            dbg[name] = nc.dram_tensor(name, shape, dt, kind="ExternalOutput").ap()
        if stop_after == "P1":
            dbg_out("d_xnfm", [NCH, 128, 16, TC], BF16)
            dbg_out("d_qf", [NCH, 128, 16, TC], BF16)
            dbg_out("d_kff", [NCH, 128, 4, TC], BF16)
            dbg_out("d_kft", [NCH, 128, NSUB, 4, 128], BF16)
            dbg_out("d_va", [NCH, 128, NSUB, 4, VW], BF16)
            dbg_out("d_sloc", [128, 4, VW], F32)
        if stop_after == "CC":
            dbg_out("d_sinit", [128, 4, VW], F32)
        if stop_after == "P2":
            dbg_out("d_ya", [NSUBT, 128, 2048], F32)
            dbg_out("d_sinit", [128, 4, VW], F32)
        if stop_after == "P3":
            dbg_out("d_gat", [NHC, NCH, 128, TC], BF16)

    sems = {}
    for sname in ("sT", "sV", "sS", "sP", "sC", "qS", "qA", "qG",
                  "cst", "wwq", "wwo", "wds", "glx",
                  "xl0", "xl1", "cs0", "cs1", "ls0", "ls1",
                  "qfw0", "qfw1", "wg0", "wg1", "wg2",
                  "gwa", "gwb", "gc0", "gc1", "ld40", "ld41",
                  "xnl0", "xnl1", "xnl2", "xnl3", "xnl4", "xnl5", "xnl6", "xnl7", "kfw",
                  "ow0", "ow1"):
        sems[sname] = ctx.enter_context(nc.semaphore(sname))
    P = Plan(nc, sems)

    per = contextlib.ExitStack()
    ident_sb = per.enter_context(nc.sbuf_tensor("ident_sb", [128, 128], BF16))
    tril_sb = per.enter_context(nc.sbuf_tensor("tril_sb", [128, 512], F32))
    pm_sb = per.enter_context(nc.sbuf_tensor("pm_sb", [128, 8], F32))
    S_sb = per.enter_context(nc.sbuf_tensor("S_sb", [128, 4, VW], F32))
    S_bf = per.enter_context(nc.sbuf_tensor("S_bf", [128, 4, VW], BF16))

    ph = contextlib.ExitStack()
    sb = lambda n, shp, dt: ph.enter_context(nc.sbuf_tensor(n, shp, dt))
    ps = lambda n, shp, dt: ph.enter_context(nc.psum_tensor(n, shp, dt))

    xsb = sb("xsb", [128, 2, D], F32)
    sqs = sb("sqs", [128, D], BF16)
    lnA = sb("lnA", [128, 2, 8], F32)
    xntm = sb("xntm", [128, 4, D], BF16)
    xnfm = sb("xnfm", [128, 2, 16, TC], BF16)
    cossb = sb("cossb", [128, 2, 2, TC], F32)
    wq_sb = sb("wq_sb", [128, 16, 16, 128], BF16)
    wk_sb = sb("wk_sb", [128, 16, 4, 128], BF16)
    wv_sb = sb("wv_sb", [128, 16, 512], BF16)
    qf_sb = sb("qf_sb", [128, 2, TC], BF16)
    kff_sb = sb("kff_sb", [128, 4, TC], BF16)
    kft_sb = sb("kft_sb", [128, NSUB, 4, 128], BF16)
    va_sb = sb("va_sb", [128, NSUB, 4, VW], BF16)
    S_loc = sb("S_loc", [128, 4, VW], F32)
    ra = sb("ra", [128, TC], F32)
    rb = sb("rb", [128, TC], F32)
    rc = sb("rc", [128, TC], F32)
    re = sb("re", [128, TC], F32)
    qev = sb("qev", [128, 2, TC], F32)

    psq = ps("psq", [128, 2, 512], F32)
    psv = ps("psv", [128, 2, 512], F32)
    pT = ps("pT", [128, 2, 1024], BF16)
    pstA = ps("pstA", [128, 512], F32)
    pstB = ps("pstB", [128, 512], F32)

    P.add("sync", lambda e: e.dma_start(out=ident_sb[:], in_=ident[:, :]), inc="cst", inc_by=16)
    P.add("sync", lambda e: e.dma_start(out=tril_sb[:], in_=trilm[:, :]), inc="cst", inc_by=16)
    P.add("sync", lambda e: e.dma_start(out=pm_sb[:], in_=pmask[:, :]), inc="cst", inc_by=16)
    c_consts = P.cnt["cst"]
    P.add("gpsimd", lambda e: e.dma_start(out=wq_sb[:], in_=wq[:, :, :, :]), inc="wwq", inc_by=16)
    P.add("gpsimd", lambda e: e.dma_start(out=wk_sb[:], in_=wk[:, :, :, :]), inc="wwq", inc_by=16)
    wv_ld = P.add("gpsimd", lambda e: e.dma_start(out=wv_sb[:], in_=wv[:, :, :]), inc="wwq", inc_by=16)
    P.add("vector", lambda e: e.memset(S_loc[:], 0.0), inc="sV")
    vz = P.add("vector", lambda e: e.memset(va_sb[:, :, :, 128:VW], 0.0), inc="sV")
    va_ones = P.add("vector", lambda e: e.memset(va_sb[:, :, :, 128:129], 1.0),
                    waits=[("sV", vz)], inc="sV")

    xload = [0] * NSUBT
    xn_done = [0] * NSUBT
    sq_done = [0] * NSUBT
    xnc_done = [0] * NCH
    xnfm_dma = [0] * NCH
    cos_ld = [0] * NCH
    cos_sem = [""] * NCH
    tr_last_of_sub = {}
    rope_prev = {}
    vmm_last = [0] * NCH
    kw_q = [0] * NCH
    kff_dma = [0] * NCH
    st_addv = {}
    psv_cp = {}
    qf_w = {}
    b2_v = {}
    tcopy = {}
    tr_cnt = 0
    hg_idx = 0
    rope_state = {"se": 0, "vc1": 0, "vc2": 0, "vb2": 0}

    def plan_rope_fmap(ci, slot_par, hg, final_fn, final_waits):
        cs = cossb[:, ci % 2, 0, :]
        sn = cossb[:, ci % 2, 1, :]
        mmv = P.cnt["sT"]
        pa2, pb2, _ = rope_prev.get(hg - 2, (0, 0, 0))
        s_ev = P.add("scalar", lambda e: e.copy(qev[:, slot_par, :], psq[:, slot_par, :]),
                     waits=[("sT", mmv), ("sP", pb2)], inc="sS")
        pw = [("sS", s_ev), (cos_sem[ci], cos_ld[ci])]
        if hg >= 1:
            pw.append(("sV", rope_state["vc1"]))
        va = P.add("gpsimd", lambda e: e.tensor_tensor(
            ra[:], qev[:, slot_par, :], cs, ALU.mult),
            waits=pw, inc="sP")
        P.add("vector", lambda e: e.tensor_tensor(
            rb[0:64, :], psq[64:128, slot_par, :], sn[0:64, :], ALU.mult),
            waits=[("sS", s_ev), (cos_sem[ci], cos_ld[ci]), ("sV", rope_state["vb2"])], inc="sV")
        b2 = P.add("vector", lambda e: e.tensor_tensor(
            rb[64:128, :], psq[0:64, slot_par, :], sn[64:128, :], ALU.mult),
            inc="sV")
        vc = P.add("vector", lambda e: e.tensor_tensor(
            rc[:], ra[:], rb[:, :], ALU.add),
            waits=[("sP", va)], inc="sV")
        vm = P.add("vector", lambda e: e.tensor_scalar(
            rb[:], rc[:], 0.0, None, ALU.min),
            waits=[("sS", rope_state["se"])], inc="sV")
        se = P.add("scalar", lambda e: e.activation(re[:], rb[:], AF.Exp),
                   waits=[("sV", vm)], inc="sS")
        ve = P.add("vector", lambda e_=None: None, waits=()) if False else None
        ve = P.add("vector", final_fn, waits=[("sS", se)] + list(final_waits), inc="sV")
        rope_state["se"] = se
        rope_state["vc2"] = rope_state.get("vc1", 0)
        rope_state["vc1"] = vc
        rope_state["vb2"] = b2
        rope_prev[hg] = (b2, va, s_ev)
        return s_ev, ve

    def plan_ln_sub(u):
        ci_, si_ = divmod(u, NSUB)
        p = u % 2
        w = []
        if u >= 2:
            w = [("sV", xn_done[u - 2]), ("sS", sq_done[u - 2])]
        xsem = "xl0" if p == 0 else "xl1"
        xload[u] = P.add("sync", lambda e, u=u, p=p: e.dma_start(
            out=xsb[:, p, :], in_=x[u * 128:(u + 1) * 128, :]),
            waits=w, inc=xsem, inc_by=16)
        P.add("scalar", lambda e, p=p: e.activation(
            sqs[:], xsb[:, p, :], AF.Identity, accum_out=lnA[:, p, 0:1]),
            waits=[(xsem, xload[u])], inc="sS")
        sq_done[u] = P.add("scalar", lambda e, p=p: e.activation(
            sqs[:], xsb[:, p, :], AF.Square, accum_out=lnA[:, p, 1:2]),
            waits=[(xsem, xload[u])], inc="sS")
        v1 = P.add("vector", lambda e, p=p: e.tensor_scalar(
            lnA[:, p, 2:3], lnA[:, p, 0:1], 1.0 / D, None, ALU.mult),
            waits=[("sS", sq_done[u])], inc="sV")
        v2 = P.add("vector", lambda e, p=p: e.tensor_scalar(
            lnA[:, p, 3:4], lnA[:, p, 1:2], 1.0 / D, None, ALU.mult),
            waits=[("sS", sq_done[u])], inc="sV")
        v3 = P.add("vector", lambda e, p=p: e.tensor_tensor(
            lnA[:, p, 7:8], lnA[:, p, 2:3], lnA[:, p, 2:3], ALU.mult),
            waits=[("sV", v1)], inc="sV")
        v4 = P.add("vector", lambda e, p=p: e.scalar_tensor_tensor(
            lnA[:, p, 4:5], lnA[:, p, 3:4], EPS_LN, lnA[:, p, 7:8],
            ALU.add, ALU.subtract), waits=[("sV", v3)], inc="sV")
        sl_ = P.add("scalar", lambda e, p=p: e.activation(
            lnA[:, p, 7:8], lnA[:, p, 4:5], AF.Ln),
            waits=[("sV", v4)], inc="sS")
        s2 = P.add("scalar", lambda e, p=p: e.activation(
            lnA[:, p, 5:6], lnA[:, p, 7:8], AF.Exp, scale=-0.5),
            waits=[("sS", sl_)], inc="sS")
        v5 = P.add("vector", lambda e, p=p: e.tensor_tensor(
            lnA[:, p, 6:7], lnA[:, p, 2:3], lnA[:, p, 5:6], ALU.mult),
            waits=[("sS", s2)], inc="sV")
        w = [("sV", v5)]
        if u >= 4:
            assert (u - 4) in tr_last_of_sub
            w.append(("sT", tr_last_of_sub[u - 4]))
        xn_done[u] = P.add("vector", lambda e, p=p, p4=u % 4: e.tensor_scalar(
            xntm[:, p4, :], xsb[:, p, :], lnA[:, p, 5:6], lnA[:, p, 6:7],
            ALU.mult, ALU.subtract), waits=w, inc="sV")

    pending_qf = []

    def flush_qf(n=None):
        while pending_qf and (n is None or len(pending_qf) > n):
            ci_, h_, qslot_, ve_ = pending_qf.pop(0)
            qf_w[ci_ * 16 + h_] = P.add(
                "scalar", lambda e, ci=ci_, h=h_, qslot=qslot_: e.dma_start(
                    out=qf_h[ci, :, h, :], in_=qf_sb[:, qslot, :]),
                waits=[("sV", ve_)], inc=("qfw0" if qslot_ == 0 else "qfw1"),
                inc_by=16)

    for si in range(NSUB):
        plan_ln_sub(si)

    for ci in range(NCH):
        cp = ci % 2
        w = []
        if ci >= 2:
            pa, pb, _ = rope_prev.get((ci - 2) * 20 + 19, (0, 0, 0))
            w = [("sV", pa), ("sP", pb)]
        csem = "cs0" if ci % 2 == 0 else "cs1"
        P.add("sync", lambda e, ci=ci: e.dma_start(
            out=cossb[:, ci % 2, 0, :], in_=cosf[:, ci * TC:(ci + 1) * TC]),
            waits=w, inc=csem, inc_by=16)
        cos_ld[ci] = P.add("sync", lambda e, ci=ci: e.dma_start(
            out=cossb[:, ci % 2, 1, :], in_=sinf[:, ci * TC:(ci + 1) * TC]),
            inc=csem, inc_by=16)
        cos_sem[ci] = csem

        for si in range(NSUB):
            u = ci * NSUB + si
            p = u % 2
            for dc in range(16):
                tp = tr_cnt % 2
                w = [("sV", xn_done[u]), ("cst", c_consts)]
                if tr_cnt >= 2:
                    w.append(("sS", tcopy[tr_cnt - 2]))
                tt = P.add("tensor", lambda e, p4=u % 4, dc=dc, tp=tp: e.transpose(
                    pT[:, tp, 0:128], xntm[:, p4, dc * 128:(dc + 1) * 128], ident_sb[:]),
                    waits=w, inc="sT")
                w2 = [("sT", tt)]
                if si == 0 and dc == 0 and ci >= 2:
                    w2.append(("qA", kw_q[ci - 1]))
                tcopy[tr_cnt] = P.add("scalar", lambda e, ci=ci, dc=dc, si=si, tp=tp: e.copy(
                    xnfm[:, ci % 2, dc, si * 128:(si + 1) * 128], pT[:, tp, 0:128]),
                    waits=w2, inc="sS")
                tr_cnt += 1
            tr_last_of_sub[u] = P.cnt["sT"]
        xnc_done[ci] = P.cnt["sS"]
        xnfm_dma[ci] = P.add("scalar", lambda e, ci=ci: e.dma_start(
            out=xnfm_h[ci, :, :, :], in_=xnfm[:, ci % 2, :, :]),
            waits=[("sS", xnc_done[ci])], inc="qA", inc_by=16)

        for h in range(16):
            qh_idx = ci * 16 + h
            slot = hg_idx % 2
            qslot = qh_idx % 2
            flush_qf(1)
            w = [("sS", xnc_done[ci]), ("wwq", wv_ld)]
            if hg_idx >= 2:
                sv2, pb2v, se2 = rope_prev.get(hg_idx - 2, (0, 0, 0))
                w.append(("sS", se2))
                w.append(("sV", sv2))
            for dc in range(16):
                ww = w if dc == 0 else ()
                P.add("tensor", lambda e, ci=ci, h=h, dc=dc, slot=slot: e.matmul(
                    psq[:, slot, :], wq_sb[:, dc, h, :], xnfm[:, ci % 2, dc, :],
                    start=(dc == 0), stop=(dc == 15)),
                    waits=ww, inc=("sT" if dc == 15 else None))
            fw = [(("qfw0" if qslot == 0 else "qfw1"), qf_w.get(qh_idx - 2, 0))] \
                if qh_idx >= 2 else []
            b2, ve = plan_rope_fmap(
                ci, slot, hg_idx,
                (lambda qslot: lambda e: e.scalar_tensor_tensor(
                    qf_sb[:, qslot, :], rc[:], 0.0, re[:], ALU.max, ALU.add))(qslot), fw)
            b2_v[hg_idx] = b2
            pending_qf.append((ci, h, qslot, ve))
            hg_idx += 1
            if ci + 1 < NCH and h in (4, 6, 8, 10):
                plan_ln_sub((ci + 1) * NSUB + (h - 4) // 2)

        for g in range(4):
            slot = hg_idx % 2
            flush_qf(1)
            w = [("sS", xnc_done[ci]), ("wwq", wv_ld)]
            if hg_idx >= 2:
                sv2, pb2v, se2 = rope_prev.get(hg_idx - 2, (0, 0, 0))
                w.append(("sS", se2))
                w.append(("sV", sv2))
            for dc in range(16):
                ww = w if dc == 0 else ()
                P.add("tensor", lambda e, ci=ci, g=g, dc=dc, slot=slot: e.matmul(
                    psq[:, slot, :], wk_sb[:, dc, g, :], xnfm[:, ci % 2, dc, :],
                    start=(dc == 0), stop=(dc == 15)),
                    waits=ww, inc=("sT" if dc == 15 else None))
            kwf = [("kfw", kff_dma[ci - 1])] if (g == 0 and ci > 0) else []
            b2, ve = plan_rope_fmap(
                ci, slot, hg_idx,
                (lambda g: lambda e: e.scalar_tensor_tensor(
                    kff_sb[:, g, :], rc[:], 0.0, re[:], ALU.max, ALU.add))(g), kwf)
            b2_v[hg_idx] = b2
            hg_idx += 1
        kff_done = P.cnt["sV"]
        flush_qf(0)
        for g in range(4):
            for si in range(NSUB):
                tp = tr_cnt % 2
                w = [("sV", kff_done)]
                if tr_cnt >= 2:
                    w.append(("sS", tcopy[tr_cnt - 2]))
                tt = P.add("tensor", lambda e, g=g, si=si, tp=tp: e.transpose(
                    pT[:, tp, 0:128], kff_sb[:, g, si * 128:(si + 1) * 128], ident_sb[:]),
                    waits=w, inc="sT")
                w2 = [("sT", tt)]
                if g == 0 and si == 0 and ci > 0:
                    w2.append(("qA", P.cnt["qA"]))
                tcopy[tr_cnt] = P.add("scalar", lambda e, g=g, si=si, tp=tp: e.copy(
                    kft_sb[:, si, g, :], pT[:, tp, 0:128]),
                    waits=w2, inc="sS")
                tr_cnt += 1
        kft_done = P.cnt["sS"]
        kff_dma[ci] = P.add("scalar", lambda e, ci=ci: e.dma_start(
            out=kff_h[ci, :, :, :], in_=kff_sb[:, :, :]),
            waits=[("sV", kff_done)], inc="kfw", inc_by=16)
        P.add("scalar", lambda e, ci=ci: e.dma_start(
            out=kft_h[ci, :, :, :, :], in_=kft_sb[:, :, :, :]),
            waits=[("sS", kft_done)], inc="qA", inc_by=16)

        for si in range(NSUB):
            gv = ci * NSUB + si
            vs = gv % 2
            w = [("sS", xnc_done[ci]), ("wwq", wv_ld), ("sV", va_ones)]
            if gv >= 2:
                w.append(("sS", psv_cp.get(gv - 2, 0)))
            for dc in range(16):
                ww = w if dc == 0 else ()
                P.add("tensor", lambda e, ci=ci, si=si, dc=dc, vs=vs: e.matmul(
                    psv[:, vs, :], xnfm[:, ci % 2, dc, si * 128:(si + 1) * 128],
                    wv_sb[:, dc, :], start=(dc == 0), stop=(dc == 15)),
                    waits=ww, inc=("sT" if dc == 15 else None))
            mv = P.cnt["sT"]
            last = 0
            for g in range(4):
                w2 = [("sT", mv)]
                if si == 0 and g == 0 and ci > 0:
                    w2.append(("qA", P.cnt["qA"]))
                last = P.add("scalar", lambda e, si=si, g=g, vs=vs: e.copy(
                    va_sb[:, si, g, 0:128], psv[:, vs, g * 128:(g + 1) * 128]),
                    waits=w2, inc="sS")
            psv_cp[gv] = last
        va_all = P.cnt["sS"]
        P.add("scalar", lambda e, ci=ci: e.dma_start(
            out=va_h[ci, :, :, :, :], in_=va_sb[:, :, :, :]),
            waits=[("sS", va_all)], inc="qA", inc_by=16)
        kw_q[ci] = P.cnt["qA"]

        for si in range(NSUB):
            for g in range(4):
                gi = ci * 16 + si * 4 + g
                pst = pstA if gi % 2 == 0 else pstB
                w = [("sS", va_all), ("sS", kft_done)]
                if gi >= 2:
                    w.append(("sV", st_addv.get(gi - 2, 0)))
                ms = P.add("tensor", lambda e, si=si, g=g, pst=pst: e.matmul(
                    pst[:, 0:129], kft_sb[:, si, g, :], va_sb[:, si, g, 0:129],
                    start=True, stop=True), waits=w, inc="sT")
                st_addv[gi] = P.add("vector", lambda e, g=g, pst=pst: e.tensor_tensor(
                    S_loc[:, g, 0:129], S_loc[:, g, 0:129], pst[:, 0:129], ALU.add),
                    waits=[("sT", ms)], inc="sV")
        vmm_last[ci] = P.cnt["sT"]

    p1_V = P.cnt["sV"]

    if debug and stop_after == "P1":
        w = [("sV", p1_V), ("sS", P.cnt["sS"]), ("qA", P.cnt["qA"])]
        for name, src in (("d_sloc", None),):
            pass
        P.add("gpsimd", lambda e: e.dma_start(out=dbg["d_sloc"][:, :, :], in_=S_loc[:]),
              waits=w, inc="qG", inc_by=16)
        for name, src in (("d_xnfm", xnfm_h), ("d_qf", qf_h), ("d_kff", kff_h)):
            P.add("gpsimd", lambda e, name=name, src=src: e.dma_start(
                out=dbg[name][:, :, :, :], in_=src[:, :, :, :]), inc="qG", inc_by=16)
        for name, src in (("d_kft", kft_h), ("d_va", va_h)):
            P.add("gpsimd", lambda e, name=name, src=src: e.dma_start(
                out=dbg[name][:, :, :, :, :], in_=src[:, :, :, :, :]), inc="qG", inc_by=16)
        P.wait_only("gpsimd", [("qG", P.cnt["qG"])])
        P.emit()
        ph.close(); per.close(); ctx.close()
        return nc, dbg

    P.add("gpsimd", lambda e: e.dma_start(
        out=cc_in_t.ap().rearrange("(g p) f -> p g f", p=128), in_=S_loc[:, :, :]),
        waits=[("sV", p1_V)], inc="qG", inc_by=16)
    cw = P.cnt["qG"]
    P.add("gpsimd", lambda e: e.collective_compute(
        "AllGather", ALU.bypass, replica_groups=[list(range(8))],
        ins=[cc_in_t.ap().opt()], outs=[cc_out_t.ap().opt()]),
        waits=[("qG", cw)], inc="sC")

    P.emit()
    ph.close()

    ph = contextlib.ExitStack()
    sb = lambda n, shp, dt: ph.enter_context(nc.sbuf_tensor(n, shp, dt))
    ps = lambda n, shp, dt: ph.enter_context(nc.psum_tensor(n, shp, dt))
    gath_sb = sb("gath_sb", [128, 8, 4, VW], F32)

    P.barrier()
    gl = P.add("gpsimd", lambda e: e.dma_start(
        out=gath_sb[:, :, :, :],
        in_=cc_out_t.ap().rearrange("(r g p) f -> p r g f", p=128, g=4)),
        waits=[("sC", P.cnt["sC"])], inc="glx", inc_by=16)
    pv = P.add("vector", lambda e: e.memset(S_sb[:], 0.0), waits=[("glx", gl)], inc="sV")
    for r in range(8):
        pv = P.add("vector", lambda e, r=r: e.scalar_tensor_tensor(
            S_sb.ap().rearrange("p g f -> p (g f)"),
            gath_sb[:, r, :, :].rearrange("p g f -> p (g f)"),
            pm_sb[:, r:r + 1],
            S_sb.ap().rearrange("p g f -> p (g f)"),
            ALU.mult, ALU.add),
            waits=[("cst", c_consts), ("sV", pv)], inc="sV")
    sbf_init = P.add("vector", lambda e: e.tensor_copy(S_bf[:], S_sb[:]),
                     waits=[("sV", pv)], inc="sV")

    if debug and stop_after == "CC":
        P.add("gpsimd", lambda e: e.dma_start(out=dbg["d_sinit"][:, :, :], in_=S_sb[:]),
              waits=[("sV", sbf_init)], inc="qG", inc_by=16)
        P.wait_only("gpsimd", [("qG", P.cnt["qG"])])
        P.emit()
        ph.close(); per.close(); ctx.close()
        return nc, dbg

    qbuf = sb("qbuf", [128, 2, 16, TC], BF16)
    kfb = sb("kfb", [128, 2, 4, TC], BF16)
    ktb = sb("ktb", [128, 2, NSUB, 4, 128], BF16)
    vab = sb("vab", [128, 2, NSUB, 4, VW], BF16)
    wo_sb = sb("wo_sb", [128, 16, 2048], BF16)
    abf = sb("abf", [128, 2, 4, 128], BF16)
    att_tm = sb("att_tm", [128, 2, 128], BF16)
    afm = sb("afm", [128, 2, 16, 128], BF16)
    st2 = sb("st2", [128, 2, 2], F32)
    y_sb = sb("y_sb", [128, 2, 2048], F32)

    pscr = ps("pscr", [128, 512], F32)
    pRp = ps("pRp", [128, 2, 512], F32)
    pTr = ps("pTr", [128, 1024], BF16)
    pW = ps("pW", [128, 2048], F32)

    wo_ld = P.add("gpsimd", lambda e: e.dma_start(out=wo_sb[:], in_=wo[:, :, :]),
                  inc="wwo", inc_by=16)

    ld2 = [0] * NCH
    ld2_sem = [""] * NCH
    msk_v = {}
    nrm_v = {}
    afc_s = {}
    yev_s = [0] * NSUBT
    yaw_q = [0] * NSUBT
    stadd_v = [0] * (NSUBT + 1)
    stadd_v[0] = sbf_init
    sbf_u = [0] * (NSUBT + 1)
    sbf_u[0] = sbf_init
    mmst_t = [0] * NSUBT

    for ci in range(NCH):
        w = []
        if ci >= 2:
            w.append(("sT", mmst_t[(ci - 2) * NSUB + NSUB - 1]))
        lsem = "ls0" if ci % 2 == 0 else "ls1"
        for fn in (
            lambda e, ci=ci: e.dma_start(out=qbuf[:, ci % 2], in_=qf_h[ci]),
            lambda e, ci=ci: e.dma_start(out=kfb[:, ci % 2], in_=kff_h[ci]),
            lambda e, ci=ci: e.dma_start(out=ktb[:, ci % 2], in_=kft_h[ci]),
            lambda e, ci=ci: e.dma_start(out=vab[:, ci % 2], in_=va_h[ci]),
        ):
            ld2[ci] = P.add("sync", fn, waits=w, inc=lsem, inc_by=16)
            w = []
        ld2_sem[ci] = lsem

        for si in range(NSUB):
            u = ci * NSUB + si
            up = u % 2
            sl = slice(si * 128, (si + 1) * 128)
            mmW = 0

            def plan_tr_afc(h, u=u, up=up):
                hu = u * 16 + h
                hp = h % 2
                w = [("sV", nrm_v[hu])]
                if hu >= 1:
                    w.append(("sS", afc_s.get(hu - 1, 0)))
                tt = P.add("tensor", lambda e, hp=hp: e.transpose(
                    pTr[:, 0:128], att_tm[:, hp, :], ident_sb[:]),
                    waits=w, inc="sT")
                afc_s[hu] = P.add("scalar", lambda e, up=up, h=h: e.copy(
                    afm[:, up, h, :], pTr[:, 0:128]),
                    waits=[("sT", tt)], inc="sS")

            def plan_wo(h, u=u, up=up):
                nonlocal mmW
                w = [("wwo", wo_ld), ("sS", afc_s[u * 16 + h]), ("sV", stadd_v[u])]
                if h == 0 and u >= 1:
                    w.append(("sS", yev_s[u - 1]))
                first = True
                for os_ in range(4):
                    ww = w if first else ()
                    first = False
                    mmW = P.add("tensor", lambda e, up=up, h=h, os_=os_: e.matmul(
                        pW[:, os_ * 512:(os_ + 1) * 512], afm[:, up, h, :],
                        wo_sb[:, h, os_ * 512:(os_ + 1) * 512],
                        start=(h == 0), stop=(h == 15), skip_group_check=True),
                        waits=ww, inc=("sT" if (h == 15 and os_ == 3) else None))

            for h in range(16):
                g = h // 4
                hp = h % 2
                hu = u * 16 + h
                if h % 4 == 0:
                    w = [(ld2_sem[ci], ld2[ci])]
                    qk = u * 4 + g
                    if qk >= 1:
                        w.append(("sV", msk_v.get(qk - 1, 0)))
                    sc = P.add("tensor", lambda e, ci=ci, g=g, sl=sl: e.matmul(
                        pscr[:, 0:512], kfb[:, ci % 2, g, sl],
                        qbuf[:, ci % 2, 4 * g:4 * g + 4, sl],
                        start=True, stop=True), waits=w, inc="sT")
                    msk_v[qk] = P.add("vector", lambda e, hp2=(g % 2): e.tensor_tensor(
                        abf[:, hp2, :, :], pscr.ap().rearrange("p (a f) -> p a f", a=4),
                        tril_sb.ap().rearrange("p (a f) -> p a f", a=4), ALU.mult),
                        waits=[("sT", sc)], inc="sV")
                w = [("sV", msk_v[u * 4 + g]), ("sV", sbf_u[u])]
                if hu >= 2:
                    w.append(("sV", nrm_v.get(hu - 2, 0)))
                P.add("tensor", lambda e, ci=ci, si=si, g=g, h=h, hp=hp: e.matmul(
                    pRp[:, hp, 0:129], abf[:, g % 2, h % 4, :],
                    vab[:, ci % 2, si, g, 0:129],
                    start=True, stop=False), waits=w)
                au = P.add("tensor", lambda e, ci=ci, h=h, g=g, hp=hp, sl=sl: e.matmul(
                    pRp[:, hp, 0:129], qbuf[:, ci % 2, h, sl], S_bf[:, g, 0:129],
                    start=False, stop=True), inc="sT")
                n1 = P.add("vector", lambda e, hp=hp: e.tensor_scalar(
                    st2[:, hp, 0:1], pRp[:, hp, 128:129], EPS_A, None, ALU.add),
                    waits=[("sT", au)], inc="sV")
                n2 = P.add("vector", lambda e, hp=hp: e.reciprocal(
                    st2[:, hp, 1:2], st2[:, hp, 0:1]), waits=[("sV", n1)], inc="sV")
                nrm_v[hu] = P.add("vector", lambda e, hp=hp: e.tensor_scalar(
                    att_tm[:, hp, :], pRp[:, hp, 0:128], st2[:, hp, 1:2], None, ALU.mult),
                    waits=[("sV", n2)], inc="sV")
                if h >= 1:
                    plan_tr_afc(h - 1)
                if h >= 2:
                    plan_wo(h - 2)
            plan_tr_afc(15)
            plan_wo(14)
            plan_wo(15)
            w = [("sT", mmW)]
            if u >= 2:
                w.append(("qA", P.cnt["qA"]))
            yev_s[u] = P.add("scalar", lambda e, up=up: e.copy(
                y_sb[:, up, :], pW[:]), waits=w, inc="sS")
            yaw_q[u] = P.add("scalar", lambda e, u=u, up=up: e.dma_start(
                out=ya_h[u], in_=y_sb[:, up, :]),
                waits=[("sS", yev_s[u])], inc="qA", inc_by=16)
            w = [("sS", yev_s[u]), (ld2_sem[ci], ld2[ci])]
            mmst = 0
            for g in range(4):
                ww = w if g == 0 else ()
                mmst = P.add("tensor", lambda e, ci=ci, si=si, g=g: e.matmul(
                    pW[:, g * 512:g * 512 + 129], ktb[:, ci % 2, si, g, :],
                    vab[:, ci % 2, si, g, 0:129], start=True, stop=True),
                    waits=ww, inc=("sT" if g == 3 else None))
            mmst_t[u] = mmst
            for g in range(4):
                P.add("vector", lambda e, g=g: e.tensor_tensor(
                    S_sb[:, g, 0:129], S_sb[:, g, 0:129],
                    pW[:, g * 512:g * 512 + 129], ALU.add),
                    waits=[("sT", mmst)], inc="sV")
            stadd_v[u + 1] = P.cnt["sV"]
            sbf_u[u + 1] = P.add("vector", lambda e: e.tensor_copy(S_bf[:], S_sb[:]),
                                 waits=[("sV", stadd_v[u + 1])], inc="sV")

    if debug and stop_after == "P2":
        w = [("qA", P.cnt["qA"]), ("sV", P.cnt["sV"])]
        P.add("gpsimd", lambda e: e.dma_start(out=dbg["d_ya"][:, :, :], in_=ya_h[:, :, :]),
              waits=w, inc="qG", inc_by=16)
        P.add("gpsimd", lambda e: e.dma_start(out=dbg["d_sinit"][:, :, :], in_=S_sb[:]),
              inc="qG", inc_by=16)
        P.wait_only("gpsimd", [("qG", P.cnt["qG"])])
        P.emit()
        ph.close(); per.close(); ctx.close()
        return nc, dbg

    P.barrier()
    P.emit()
    ph.close()
    ph = contextlib.ExitStack()
    sb = lambda n, shp, dt: ph.enter_context(nc.sbuf_tensor(n, shp, dt))
    ps = lambda n, shp, dt: ph.enter_context(nc.psum_tensor(n, shp, dt))

    xnr = sb("xnr", [128, 16, NCH, TC], BF16)
    wgub = sb("wgub", [128, 3, 2, 16, 128], BF16)
    gsb = sb("gsb", [128, 2, NCH, TC], BF16)
    ssb3 = sb("ssb3", [128, 2, 512], F32)
    pg = ps("pg", [128, 2, 512], F32)
    pu = ps("pu", [128, 2, 512], F32)

    xnl = [0] * NCH
    for ci in range(NCH):
        xnl[ci] = P.add("sync", lambda e, ci=ci: e.dma_start(
            out=xnr[:, :, ci, :], in_=xnfm_h[ci]), inc=f"xnl{ci}", inc_by=16)

    wgl = [0] * NHC
    mmU_t = {}
    gmul_v = {}
    gw_q = [0] * NHC
    for hc in range(NHC):
        w = []
        if hc >= 3:
            w.append(("sT", mmU_t[(hc - 3, NCH - 1)]))
        wgl[hc] = P.add("gpsimd", lambda e, hc=hc: e.dma_start(
            out=wgub[:, hc % 3], in_=wgu[hc]), waits=w, inc=f"wg{hc % 3}", inc_by=16)
        for tt in range(NCH):
            tp = tt % 2
            pidx = (hc, tt - 2) if tt >= 2 else (hc - 1, tt - 2 + NCH)
            w = [(f"wg{hc % 3}", wgl[hc]), (f"xnl{tt}", xnl[tt])]
            if (hc, tt) != (0, 0):
                w.append(("sV", gmul_v.get(pidx, 0)))
            for dc in range(16):
                ww = w if dc == 0 else ()
                P.add("tensor", lambda e, hc=hc, tt=tt, dc=dc, tp=tp: e.matmul(
                    pg[:, tp, :], wgub[:, hc % 3, 0, dc, :], xnr[:, dc, tt, :],
                    start=(dc == 0), stop=(dc == 15)),
                    waits=ww, inc=("sT" if dc == 15 else None))
            mg = P.cnt["sT"]
            for dc in range(16):
                P.add("tensor", lambda e, hc=hc, tt=tt, dc=dc, tp=tp: e.matmul(
                    pu[:, tp, :], wgub[:, hc % 3, 1, dc, :], xnr[:, dc, tt, :],
                    start=(dc == 0), stop=(dc == 15)),
                    inc=("sT" if dc == 15 else None))
            mmU_t[(hc, tt)] = P.cnt["sT"]
            w2 = [("sT", mg)]
            if (hc, tt) != (0, 0):
                w2.append(("sV", gmul_v.get(pidx, 0)))
            sl3 = P.add("scalar", lambda e, tp=tp: e.activation(
                ssb3[:, tp, :], pg[:, tp, :], AF.Silu), waits=w2, inc="sS")
            w2 = [("sS", sl3), ("sT", mmU_t[(hc, tt)])]
            if hc >= 2 and tt == 0:
                w2.append((("gwa" if hc % 2 == 0 else "gwb"), gw_q[hc - 2]))
            gmul_v[(hc, tt)] = P.add("vector", lambda e, hc=hc, tt=tt, tp=tp: e.tensor_tensor(
                gsb[:, hc % 2, tt, :], ssb3[:, tp, :], pu[:, tp, :], ALU.mult),
                waits=w2, inc="sV")
        gw_q[hc] = P.add("scalar", lambda e, hc=hc: e.dma_start(
            out=gat_h[hc].rearrange("a p f -> p a f"), in_=gsb[:, hc % 2]),
            waits=[("sV", gmul_v[(hc, NCH - 1)])],
            inc=("gwa" if hc % 2 == 0 else "gwb"), inc_by=16)

    if debug and stop_after == "P3":
        P.add("gpsimd", lambda e: e.dma_start(
            out=dbg["d_gat"][:, :, :, :], in_=gat_h[:, :, :, :]),
            waits=[("qA", P.cnt["qA"])], inc="qG", inc_by=16)
        P.wait_only("gpsimd", [("qG", P.cnt["qG"])])
        P.emit()
        ph.close(); per.close(); ctx.close()
        return nc, dbg

    P.barrier()
    P.emit()
    ph.close()
    ph = contextlib.ExitStack()
    sb = lambda n, shp, dt: ph.enter_context(nc.sbuf_tensor(n, shp, dt))
    ps = lambda n, shp, dt: ph.enter_context(nc.psum_tensor(n, shp, dt))

    wdb = sb("wdb", [128, NHC, 1024], BF16)
    gcol = sb("gcol", [128, 2, NHC, 128], BF16)
    xb4 = sb("xb4", [128, 2, 1024], F32)
    yab4 = sb("yab4", [128, 2, 1024], F32)
    obA = sb("obA", [128, 2, 1024], F32)
    ob = sb("ob", [128, 2, 1024], F32)
    pY = ps("pY", [128, 2, 1024], F32)

    cmb_v = {}
    out_q = {}
    mm_last = {}
    for half in range(2):
        wdl = [0] * NHC
        for hc in range(NHC):
            w = []
            if half == 1 and hc == 0:
                w.append(("sT", mm_last[(0, NSUBT - 1)]))
            wdl[hc] = P.add("gpsimd", lambda e, half=half, hc=hc: e.dma_start(
                out=wdb[:, hc, :], in_=wd[half, hc]), waits=w, inc="wds", inc_by=16)
        for u4 in range(NSUBT):
            up = u4 % 2
            w = []
            pidx = (half, u4 - 2) if u4 >= 2 else (half - 1, u4 - 2 + NSUBT)
            if (half, u4) != (0, 0) and (half, u4) != (0, 1):
                w.append(("sT", mm_last[pidx]))
            gl = 0
            gsem = "gc0" if up == 0 else "gc1"
            w2 = list(w)
            for part in range(8):
                eng = "sync" if part < 4 else "scalar"
                gl = P.add(eng, lambda e, u4=u4, up=up, part=part: e.dma_start(
                    out=gcol[:, up, part * 8:(part + 1) * 8, :],
                    in_=gat_h[part * 8:(part + 1) * 8, u4 // NSUB, :,
                              (u4 % NSUB) * 128:(u4 % NSUB + 1) * 128].rearrange(
                                  "a p f -> p a f")),
                    waits=(w if part < 4 else w2), inc=gsem, inc_by=16)
                if part < 4:
                    w = []
                else:
                    w2 = []
            w = []
            if (half, u4) != (0, 0) and (half, u4) != (0, 1):
                w.append(("sV", cmb_v.get(pidx, 0)))
            l4sem = "ld40" if up == 0 else "ld41"
            P.add("sync", lambda e, half=half, u4=u4, up=up: e.dma_start(
                out=xb4[:, up, :],
                in_=x[u4 * 128:(u4 + 1) * 128, half * 1024:(half + 1) * 1024]),
                waits=w, inc=l4sem, inc_by=16)
            yl = P.add("sync", lambda e, half=half, u4=u4, up=up: e.dma_start(
                out=yab4[:, up, :],
                in_=ya_h[u4, :, half * 1024:(half + 1) * 1024]),
                inc=l4sem, inc_by=16)
            first = True
            mmY = 0
            for os_ in range(2):
                for hc in range(NHC):
                    ww = [("wds", wdl[NHC - 1])]
                    if first:
                        ww.append((gsem, gl))
                        if (half, u4) != (0, 0) and (half, u4) != (0, 1):
                            ww.append(("sV", cmb_v.get(pidx, 0)))
                        first = False
                    mmY = P.add("tensor", lambda e, up=up, hc=hc, os_=os_: e.matmul(
                        pY[:, up, os_ * 512:(os_ + 1) * 512],
                        gcol[:, up, hc, :],
                        wdb[:, hc, os_ * 512:(os_ + 1) * 512],
                        start=(hc == 0), stop=(hc == NHC - 1)),
                        waits=ww, inc=("sT" if (hc == NHC - 1 and os_ == 1) else None))
            mm_last[(half, u4)] = mmY
            P.add("vector", lambda e, up=up: e.tensor_tensor(
                obA[:, up, :], pY[:, up, :], xb4[:, up, :], ALU.add),
                waits=[("sT", mmY), (l4sem, yl)], inc="sV")
            osem = "ow0" if up == 0 else "ow1"
            w = []
            if (half, u4) != (0, 0) and (half, u4) != (0, 1):
                w.append((osem, out_q.get(pidx, 0)))
            cmb_v[(half, u4)] = P.add("vector", lambda e, up=up: e.tensor_tensor(
                ob[:, up, :], obA[:, up, :], yab4[:, up, :], ALU.add),
                waits=w, inc="sV")
            out_q[(half, u4)] = P.add("gpsimd", lambda e, half=half, u4=u4, up=up: e.dma_start(
                out=y[u4 * 128:(u4 + 1) * 128, half * 1024:(half + 1) * 1024],
                in_=ob[:, up, :]),
                waits=[("sV", cmb_v[(half, u4)])], inc=osem, inc_by=16)

    P.wait_only("gpsimd", [("ow0", out_q[(1, NSUBT - 2)]), ("ow1", out_q[(1, NSUBT - 1)])])
    P.emit()
    ph.close(); per.close(); ctx.close()
    return nc, dbg


def _pack_inputs(L0_attn_x, Wq, Wk, Wv, Wo, Wg, Wu, Wd):
    f32 = np.float32
    xs = np.ascontiguousarray(np.asarray(L0_attn_x, dtype=f32).reshape(B, 4, T, D))
    wq_p = np.ascontiguousarray(
        np.asarray(Wq, f32).reshape(16, 128, 16, 128).transpose(3, 2, 0, 1))
    wk_p = np.ascontiguousarray(
        np.asarray(Wk, f32).reshape(4, 128, 16, 128).transpose(3, 2, 0, 1))
    wv_p = np.ascontiguousarray(
        np.asarray(Wv, f32).T.reshape(16, 128, 512).transpose(1, 0, 2))
    wo_p = np.ascontiguousarray(
        np.asarray(Wo, f32).T.reshape(16, 128, 2048).transpose(1, 0, 2))
    wg_t = np.asarray(Wg, f32).reshape(NHC, 128, 16, 128).transpose(0, 3, 2, 1)
    wu_t = np.asarray(Wu, f32).reshape(NHC, 128, 16, 128).transpose(0, 3, 2, 1)
    wgu_p = np.ascontiguousarray(np.stack([wg_t, wu_t], axis=2))
    wd_p = np.ascontiguousarray(
        np.asarray(Wd, f32).reshape(2, 1024, NHC, 128).transpose(0, 2, 3, 1))

    inv = 1.0 / (10000.0 ** (np.arange(0, HD, 2, dtype=np.float64) / HD))
    tril = np.ascontiguousarray(
        np.tile(np.triu(np.ones((128, 128), f32)), (1, 4)))
    identity = np.ascontiguousarray(np.eye(128, dtype=ml_dtypes.bfloat16))

    in_maps = []
    for core in range(8):
        b, c = divmod(core, 4)
        pos = np.arange(c * T, (c + 1) * T, dtype=np.float64)
        ang = np.outer(pos, inv)
        cosv = np.cos(ang).astype(f32)
        sinv = np.sin(ang).astype(f32)
        cos_fm = np.ascontiguousarray(np.concatenate([cosv, cosv], axis=1).T)
        sin_fm = np.ascontiguousarray(np.concatenate([-sinv, sinv], axis=1).T)
        pm = np.zeros((128, 8), f32)
        for r in range(8):
            rb_, rc_ = divmod(r, 4)
            if rb_ == b and rc_ < c:
                pm[:, r] = 1.0
        in_maps.append({
            "x": np.ascontiguousarray(xs[b, c]),
            "wq": wq_p, "wk": wk_p, "wv": wv_p, "wo": wo_p,
            "wgu": wgu_p, "wd": wd_p,
            "cosf": cos_fm, "sinf": sin_fm,
            "trilm": tril, "ident": identity,
            "pmask": pm,
        })
    return in_maps


def kernel(L0_attn_x, L0_attn_Wq, L0_attn_Wk, L0_attn_Wv, L0_attn_Wo,
           L0_ffn_w_gate, L0_ffn_w_up, L0_ffn_w_down):
    in_maps = _pack_inputs(L0_attn_x, L0_attn_Wq, L0_attn_Wk, L0_attn_Wv,
                           L0_attn_Wo, L0_ffn_w_gate, L0_ffn_w_up, L0_ffn_w_down)
    nc, _ = build_program(stop_after="P4", debug=False)
    res = run_bass_kernel_spmd(nc, in_maps, core_ids=list(range(8)))
    out = np.empty((B, S, D), dtype=np.float32)
    for core in range(8):
        b, c = divmod(core, 4)
        out[b, c * T:(c + 1) * T, :] = res.results[core]["y"]
    return out


# version 17
# speedup vs baseline: 1.1721x; 1.0098x over previous
import contextlib
import numpy as np
import ml_dtypes
import concourse.bass as bass
import concourse.mybir as mybir
from concourse.bass_utils import run_bass_kernel_spmd

F32 = mybir.dt.float32
BF16 = mybir.dt.bfloat16
AF = mybir.ActivationFunctionType
ALU = mybir.AluOpType
AX = mybir.AxisListType

B, S, D = 2, 16384, 2048
H, G, HD = 16, 4, 128
DH = 8192
T = S // 4
TC = 512
NCH = T // TC
NSUB = TC // 128
NSUBT = T // 128
NHC = DH // 128
EPS_LN = 1e-5
EPS_A = 1e-6
VW = 136

ENGINES = ("sync", "scalar", "vector", "tensor", "gpsimd")


class Plan:

    def __init__(self, nc, sems):
        self.nc = nc
        self.sems = sems
        self.cnt = {k: 0 for k in sems}
        self.ops = {e: [] for e in ENGINES}
        self.lastw = {e: {} for e in ENGINES}

    def add(self, eng, fn, waits=(), inc=None, inc_by=1):
        ws = []
        for s, v in waits:
            if v <= 0:
                continue
            if v > self.lastw[eng].get(s, -1):
                ws.append((s, v))
                self.lastw[eng][s] = v
        post = None
        if inc is not None:
            self.cnt[inc] += inc_by
            post = self.cnt[inc]
        self.ops[eng].append((fn, tuple(ws), inc, inc_by))
        return post

    def wait_only(self, eng, waits):
        self.add(eng, None, waits=waits)

    def emit(self):
        with self.nc.Block() as block:
            for eng in ENGINES:
                lst = self.ops[eng]
                if not lst:
                    continue

                def body(e, lst=lst):
                    for fn, ws, inc, by in lst:
                        for s, v in ws:
                            e.wait_ge(self.sems[s], v)
                        if fn is None:
                            continue
                        inst = fn(e)
                        if inc is not None:
                            inst.then_inc(self.sems[inc], by)

                getattr(block, eng)(body)
        self.ops = {e: [] for e in ENGINES}

    def barrier(self):
        snap = [(k, v) for k, v in self.cnt.items() if v > 0 and k != "sC"]
        for e in ENGINES:
            self.wait_only(e, snap)


def build_program(stop_after="P4", debug=False):
    nc = bass.Bass("TRN2", target_bir_lowering=False, debug=False)
    ctx = contextlib.ExitStack()

    x = nc.dram_tensor("x", [T, D], F32, kind="ExternalInput").ap()
    wq = nc.dram_tensor("wq", [128, 16, 16, 128], F32, kind="ExternalInput").ap()
    wk = nc.dram_tensor("wk", [128, 16, 4, 128], F32, kind="ExternalInput").ap()
    wv = nc.dram_tensor("wv", [128, 16, 512], F32, kind="ExternalInput").ap()
    wo = nc.dram_tensor("wo", [128, 16, 2048], F32, kind="ExternalInput").ap()
    wgu = nc.dram_tensor("wgu", [NHC, 128, 2, 16, 128], F32, kind="ExternalInput").ap()
    wd = nc.dram_tensor("wd", [2, NHC, 128, 1024], F32, kind="ExternalInput").ap()
    cosf = nc.dram_tensor("cosf", [128, T], F32, kind="ExternalInput").ap()
    sinf = nc.dram_tensor("sinf", [128, T], F32, kind="ExternalInput").ap()
    trilm = nc.dram_tensor("trilm", [128, 512], F32, kind="ExternalInput").ap()
    ident = nc.dram_tensor("ident", [128, 128], BF16, kind="ExternalInput").ap()
    pmask = nc.dram_tensor("pmask", [128, 8], F32, kind="ExternalInput").ap()
    y = nc.dram_tensor("y", [T, D], F32, kind="ExternalOutput").ap()

    xnfm_h = nc.dram_tensor("xnfm_h", [NCH, 128, 16, TC], BF16).ap()
    qf_h = nc.dram_tensor("qf_h", [NCH, 128, 16, TC], BF16).ap()
    kff_h = nc.dram_tensor("kff_h", [NCH, 128, 4, TC], BF16).ap()
    kft_h = nc.dram_tensor("kft_h", [NCH, 128, NSUB, 4, 128], BF16).ap()
    va_h = nc.dram_tensor("va_h", [NCH, 128, NSUB, 4, VW], BF16).ap()
    ya_h = nc.dram_tensor("ya_h", [NSUBT, 128, 2048], F32).ap()
    gat_h = nc.dram_tensor("gat_h", [NHC, NCH, 128, TC], BF16).ap()
    cc_in_t = nc.dram_tensor("cc_in", [4 * 128, VW], F32)
    cc_out_t = nc.dram_tensor("cc_out", [8 * 4 * 128, VW], F32)

    dbg = {}
    if debug:
        def dbg_out(name, shape, dt=F32):
            dbg[name] = nc.dram_tensor(name, shape, dt, kind="ExternalOutput").ap()
        if stop_after == "P1":
            dbg_out("d_xnfm", [NCH, 128, 16, TC], BF16)
            dbg_out("d_qf", [NCH, 128, 16, TC], BF16)
            dbg_out("d_kff", [NCH, 128, 4, TC], BF16)
            dbg_out("d_kft", [NCH, 128, NSUB, 4, 128], BF16)
            dbg_out("d_va", [NCH, 128, NSUB, 4, VW], BF16)
            dbg_out("d_sloc", [128, 4, VW], F32)
        if stop_after == "CC":
            dbg_out("d_sinit", [128, 4, VW], F32)
        if stop_after == "P2":
            dbg_out("d_ya", [NSUBT, 128, 2048], F32)
            dbg_out("d_sinit", [128, 4, VW], F32)
        if stop_after == "P3":
            dbg_out("d_gat", [NHC, NCH, 128, TC], BF16)

    sems = {}
    for sname in ("sT", "sV", "sS", "sP", "sC", "qS", "qA", "qG",
                  "cst", "wwq", "wwo", "wds", "glx",
                  "xl0", "xl1", "cs0", "cs1",
                  "qfA", "qfB", "kfS", "vaS", "ktG",
                  "qfw0", "qfw1", "wg0", "wg1", "wg2",
                  "gwa", "gwb", "gc0", "gc1", "ld40", "ld41",
                  "xnl0", "xnl1", "xnl2", "xnl3", "xnl4", "xnl5", "xnl6", "xnl7", "kfw",
                  "ow0", "ow1"):
        sems[sname] = ctx.enter_context(nc.semaphore(sname))
    P = Plan(nc, sems)

    per = contextlib.ExitStack()
    ident_sb = per.enter_context(nc.sbuf_tensor("ident_sb", [128, 128], BF16))
    tril_sb = per.enter_context(nc.sbuf_tensor("tril_sb", [128, 512], F32))
    pm_sb = per.enter_context(nc.sbuf_tensor("pm_sb", [128, 8], F32))
    S_sb = per.enter_context(nc.sbuf_tensor("S_sb", [128, 4, VW], F32))
    S_bf = per.enter_context(nc.sbuf_tensor("S_bf", [128, 4, VW], BF16))

    ph = contextlib.ExitStack()
    sb = lambda n, shp, dt: ph.enter_context(nc.sbuf_tensor(n, shp, dt))
    ps = lambda n, shp, dt: ph.enter_context(nc.psum_tensor(n, shp, dt))

    xsb = sb("xsb", [128, 2, D], F32)
    sqs = sb("sqs", [128, D], BF16)
    lnA = sb("lnA", [128, 2, 8], F32)
    xntm = sb("xntm", [128, 4, D], BF16)
    xnfm = sb("xnfm", [128, 2, 16, TC], BF16)
    cossb = sb("cossb", [128, 2, 2, TC], F32)
    wq_sb = sb("wq_sb", [128, 16, 16, 128], BF16)
    wk_sb = sb("wk_sb", [128, 16, 4, 128], BF16)
    wv_sb = sb("wv_sb", [128, 16, 512], BF16)
    qf_sb = sb("qf_sb", [128, 2, TC], BF16)
    kff_sb = sb("kff_sb", [128, 4, TC], BF16)
    kft_sb = sb("kft_sb", [128, NSUB, 4, 128], BF16)
    va_sb = sb("va_sb", [128, NSUB, 4, VW], BF16)
    S_loc = sb("S_loc", [128, 4, VW], F32)
    ra = sb("ra", [128, TC], F32)
    rb = sb("rb", [128, TC], F32)
    rc = sb("rc", [128, TC], F32)
    re = sb("re", [128, TC], F32)
    qev = sb("qev", [128, 2, TC], F32)

    psq = ps("psq", [128, 2, 512], F32)
    psv = ps("psv", [128, 2, 512], F32)
    pT = ps("pT", [128, 2, 1024], BF16)
    pstA = ps("pstA", [128, 512], F32)
    pstB = ps("pstB", [128, 512], F32)

    P.add("sync", lambda e: e.dma_start(out=ident_sb[:], in_=ident[:, :]), inc="cst", inc_by=16)
    P.add("sync", lambda e: e.dma_start(out=tril_sb[:], in_=trilm[:, :]), inc="cst", inc_by=16)
    P.add("sync", lambda e: e.dma_start(out=pm_sb[:], in_=pmask[:, :]), inc="cst", inc_by=16)
    c_consts = P.cnt["cst"]
    P.add("gpsimd", lambda e: e.dma_start(out=wq_sb[:], in_=wq[:, :, :, :]), inc="wwq", inc_by=16)
    P.add("gpsimd", lambda e: e.dma_start(out=wk_sb[:], in_=wk[:, :, :, :]), inc="wwq", inc_by=16)
    wv_ld = P.add("gpsimd", lambda e: e.dma_start(out=wv_sb[:], in_=wv[:, :, :]), inc="wwq", inc_by=16)
    P.add("vector", lambda e: e.memset(S_loc[:], 0.0), inc="sV")
    vz = P.add("vector", lambda e: e.memset(va_sb[:, :, :, 128:VW], 0.0), inc="sV")
    va_ones = P.add("vector", lambda e: e.memset(va_sb[:, :, :, 128:129], 1.0),
                    waits=[("sV", vz)], inc="sV")

    xload = [0] * NSUBT
    xn_done = [0] * NSUBT
    sq_done = [0] * NSUBT
    xnc_done = [0] * NCH
    xnfm_dma = [0] * NCH
    cos_ld = [0] * NCH
    cos_sem = [""] * NCH
    tr_last_of_sub = {}
    rope_prev = {}
    vmm_last = [0] * NCH
    kw_q = [0] * NCH
    kff_dma = [0] * NCH
    st_addv = {}
    psv_cp = {}
    qf_w = {}
    b2_v = {}
    tcopy = {}
    tr_cnt = 0
    hg_idx = 0
    rope_state = {"se": 0, "vc1": 0, "vc2": 0, "vb2": 0}

    def plan_rope_fmap(ci, slot_par, hg, final_fn, final_waits):
        cs = cossb[:, ci % 2, 0, :]
        sn = cossb[:, ci % 2, 1, :]
        mmv = P.cnt["sT"]
        pa2, pb2, _ = rope_prev.get(hg - 2, (0, 0, 0))
        s_ev = P.add("scalar", lambda e: e.copy(qev[:, slot_par, :], psq[:, slot_par, :]),
                     waits=[("sT", mmv), ("sP", pb2)], inc="sS")
        pw = [("sS", s_ev), (cos_sem[ci], cos_ld[ci])]
        if hg >= 1:
            pw.append(("sV", rope_state["vc1"]))
        va = P.add("gpsimd", lambda e: e.tensor_tensor(
            ra[:], qev[:, slot_par, :], cs, ALU.mult),
            waits=pw, inc="sP")
        P.add("vector", lambda e: e.tensor_tensor(
            rb[0:64, :], psq[64:128, slot_par, :], sn[0:64, :], ALU.mult),
            waits=[("sS", s_ev), (cos_sem[ci], cos_ld[ci]), ("sV", rope_state["vb2"])], inc="sV")
        b2 = P.add("vector", lambda e: e.tensor_tensor(
            rb[64:128, :], psq[0:64, slot_par, :], sn[64:128, :], ALU.mult),
            inc="sV")
        vc = P.add("vector", lambda e: e.tensor_tensor(
            rc[:], ra[:], rb[:, :], ALU.add),
            waits=[("sP", va)], inc="sV")
        vm = P.add("vector", lambda e: e.tensor_scalar(
            rb[:], rc[:], 0.0, None, ALU.min),
            waits=[("sS", rope_state["se"])], inc="sV")
        se = P.add("scalar", lambda e: e.activation(re[:], rb[:], AF.Exp),
                   waits=[("sV", vm)], inc="sS")
        ve = P.add("vector", lambda e_=None: None, waits=()) if False else None
        ve = P.add("vector", final_fn, waits=[("sS", se)] + list(final_waits), inc="sV")
        rope_state["se"] = se
        rope_state["vc2"] = rope_state.get("vc1", 0)
        rope_state["vc1"] = vc
        rope_state["vb2"] = b2
        rope_prev[hg] = (b2, va, s_ev)
        return s_ev, ve

    def plan_ln_sub(u):
        ci_, si_ = divmod(u, NSUB)
        p = u % 2
        w = []
        if u >= 2:
            w = [("sV", xn_done[u - 2]), ("sS", sq_done[u - 2])]
        xsem = "xl0" if p == 0 else "xl1"
        xload[u] = P.add("sync", lambda e, u=u, p=p: e.dma_start(
            out=xsb[:, p, :], in_=x[u * 128:(u + 1) * 128, :]),
            waits=w, inc=xsem, inc_by=16)
        P.add("scalar", lambda e, p=p: e.activation(
            sqs[:], xsb[:, p, :], AF.Identity, accum_out=lnA[:, p, 0:1]),
            waits=[(xsem, xload[u])], inc="sS")
        sq_done[u] = P.add("scalar", lambda e, p=p: e.activation(
            sqs[:], xsb[:, p, :], AF.Square, accum_out=lnA[:, p, 1:2]),
            waits=[(xsem, xload[u])], inc="sS")
        v1 = P.add("vector", lambda e, p=p: e.tensor_scalar(
            lnA[:, p, 2:3], lnA[:, p, 0:1], 1.0 / D, None, ALU.mult),
            waits=[("sS", sq_done[u])], inc="sV")
        v2 = P.add("vector", lambda e, p=p: e.tensor_scalar(
            lnA[:, p, 3:4], lnA[:, p, 1:2], 1.0 / D, None, ALU.mult),
            waits=[("sS", sq_done[u])], inc="sV")
        v3 = P.add("vector", lambda e, p=p: e.tensor_tensor(
            lnA[:, p, 7:8], lnA[:, p, 2:3], lnA[:, p, 2:3], ALU.mult),
            waits=[("sV", v1)], inc="sV")
        v4 = P.add("vector", lambda e, p=p: e.scalar_tensor_tensor(
            lnA[:, p, 4:5], lnA[:, p, 3:4], EPS_LN, lnA[:, p, 7:8],
            ALU.add, ALU.subtract), waits=[("sV", v3)], inc="sV")
        sl_ = P.add("scalar", lambda e, p=p: e.activation(
            lnA[:, p, 7:8], lnA[:, p, 4:5], AF.Ln),
            waits=[("sV", v4)], inc="sS")
        s2 = P.add("scalar", lambda e, p=p: e.activation(
            lnA[:, p, 5:6], lnA[:, p, 7:8], AF.Exp, scale=-0.5),
            waits=[("sS", sl_)], inc="sS")
        v5 = P.add("vector", lambda e, p=p: e.tensor_tensor(
            lnA[:, p, 6:7], lnA[:, p, 2:3], lnA[:, p, 5:6], ALU.mult),
            waits=[("sS", s2)], inc="sV")
        w = [("sV", v5)]
        if u >= 4:
            assert (u - 4) in tr_last_of_sub
            w.append(("sT", tr_last_of_sub[u - 4]))
        xn_done[u] = P.add("vector", lambda e, p=p, p4=u % 4: e.tensor_scalar(
            xntm[:, p4, :], xsb[:, p, :], lnA[:, p, 5:6], lnA[:, p, 6:7],
            ALU.mult, ALU.subtract), waits=w, inc="sV")

    pending_qf = []

    def flush_qf(n=None):
        while pending_qf and (n is None or len(pending_qf) > n):
            ci_, h_, qslot_, ve_ = pending_qf.pop(0)
            qf_w[ci_ * 16 + h_] = P.add(
                "scalar", lambda e, ci=ci_, h=h_, qslot=qslot_: e.dma_start(
                    out=qf_h[ci, :, h, :], in_=qf_sb[:, qslot, :]),
                waits=[("sV", ve_)], inc=("qfw0" if qslot_ == 0 else "qfw1"),
                inc_by=16)

    for si in range(NSUB):
        plan_ln_sub(si)

    for ci in range(NCH):
        cp = ci % 2
        w = []
        if ci >= 2:
            pa, pb, _ = rope_prev.get((ci - 2) * 20 + 19, (0, 0, 0))
            w = [("sV", pa), ("sP", pb)]
        csem = "cs0" if ci % 2 == 0 else "cs1"
        P.add("sync", lambda e, ci=ci: e.dma_start(
            out=cossb[:, ci % 2, 0, :], in_=cosf[:, ci * TC:(ci + 1) * TC]),
            waits=w, inc=csem, inc_by=16)
        cos_ld[ci] = P.add("sync", lambda e, ci=ci: e.dma_start(
            out=cossb[:, ci % 2, 1, :], in_=sinf[:, ci * TC:(ci + 1) * TC]),
            inc=csem, inc_by=16)
        cos_sem[ci] = csem

        for si in range(NSUB):
            u = ci * NSUB + si
            p = u % 2
            for dc in range(16):
                tp = tr_cnt % 2
                w = [("sV", xn_done[u]), ("cst", c_consts)]
                if tr_cnt >= 2:
                    w.append(("sS", tcopy[tr_cnt - 2]))
                tt = P.add("tensor", lambda e, p4=u % 4, dc=dc, tp=tp: e.transpose(
                    pT[:, tp, 0:128], xntm[:, p4, dc * 128:(dc + 1) * 128], ident_sb[:]),
                    waits=w, inc="sT")
                w2 = [("sT", tt)]
                if si == 0 and dc == 0 and ci >= 2:
                    w2.append(("qA", kw_q[ci - 1]))
                tcopy[tr_cnt] = P.add("scalar", lambda e, ci=ci, dc=dc, si=si, tp=tp: e.copy(
                    xnfm[:, ci % 2, dc, si * 128:(si + 1) * 128], pT[:, tp, 0:128]),
                    waits=w2, inc="sS")
                tr_cnt += 1
            tr_last_of_sub[u] = P.cnt["sT"]
        xnc_done[ci] = P.cnt["sS"]
        xnfm_dma[ci] = P.add("scalar", lambda e, ci=ci: e.dma_start(
            out=xnfm_h[ci, :, :, :], in_=xnfm[:, ci % 2, :, :]),
            waits=[("sS", xnc_done[ci])], inc="qA", inc_by=16)

        for h in range(16):
            qh_idx = ci * 16 + h
            slot = hg_idx % 2
            qslot = qh_idx % 2
            flush_qf(1)
            w = [("sS", xnc_done[ci]), ("wwq", wv_ld)]
            if hg_idx >= 2:
                sv2, pb2v, se2 = rope_prev.get(hg_idx - 2, (0, 0, 0))
                w.append(("sS", se2))
                w.append(("sV", sv2))
            for dc in range(16):
                ww = w if dc == 0 else ()
                P.add("tensor", lambda e, ci=ci, h=h, dc=dc, slot=slot: e.matmul(
                    psq[:, slot, :], wq_sb[:, dc, h, :], xnfm[:, ci % 2, dc, :],
                    start=(dc == 0), stop=(dc == 15)),
                    waits=ww, inc=("sT" if dc == 15 else None))
            fw = [(("qfw0" if qslot == 0 else "qfw1"), qf_w.get(qh_idx - 2, 0))] \
                if qh_idx >= 2 else []
            b2, ve = plan_rope_fmap(
                ci, slot, hg_idx,
                (lambda qslot: lambda e: e.scalar_tensor_tensor(
                    qf_sb[:, qslot, :], rc[:], 0.0, re[:], ALU.max, ALU.add))(qslot), fw)
            b2_v[hg_idx] = b2
            pending_qf.append((ci, h, qslot, ve))
            hg_idx += 1
            if ci + 1 < NCH and h in (4, 6, 8, 10):
                plan_ln_sub((ci + 1) * NSUB + (h - 4) // 2)

        for g in range(4):
            slot = hg_idx % 2
            flush_qf(1)
            w = [("sS", xnc_done[ci]), ("wwq", wv_ld)]
            if hg_idx >= 2:
                sv2, pb2v, se2 = rope_prev.get(hg_idx - 2, (0, 0, 0))
                w.append(("sS", se2))
                w.append(("sV", sv2))
            for dc in range(16):
                ww = w if dc == 0 else ()
                P.add("tensor", lambda e, ci=ci, g=g, dc=dc, slot=slot: e.matmul(
                    psq[:, slot, :], wk_sb[:, dc, g, :], xnfm[:, ci % 2, dc, :],
                    start=(dc == 0), stop=(dc == 15)),
                    waits=ww, inc=("sT" if dc == 15 else None))
            kwf = [("kfw", kff_dma[ci - 1])] if (g == 0 and ci > 0) else []
            b2, ve = plan_rope_fmap(
                ci, slot, hg_idx,
                (lambda g: lambda e: e.scalar_tensor_tensor(
                    kff_sb[:, g, :], rc[:], 0.0, re[:], ALU.max, ALU.add))(g), kwf)
            b2_v[hg_idx] = b2
            hg_idx += 1
        kff_done = P.cnt["sV"]
        flush_qf(0)
        for g in range(4):
            for si in range(NSUB):
                tp = tr_cnt % 2
                w = [("sV", kff_done)]
                if tr_cnt >= 2:
                    w.append(("sS", tcopy[tr_cnt - 2]))
                tt = P.add("tensor", lambda e, g=g, si=si, tp=tp: e.transpose(
                    pT[:, tp, 0:128], kff_sb[:, g, si * 128:(si + 1) * 128], ident_sb[:]),
                    waits=w, inc="sT")
                w2 = [("sT", tt)]
                if g == 0 and si == 0 and ci > 0:
                    w2.append(("qA", P.cnt["qA"]))
                tcopy[tr_cnt] = P.add("scalar", lambda e, g=g, si=si, tp=tp: e.copy(
                    kft_sb[:, si, g, :], pT[:, tp, 0:128]),
                    waits=w2, inc="sS")
                tr_cnt += 1
        kft_done = P.cnt["sS"]
        kff_dma[ci] = P.add("scalar", lambda e, ci=ci: e.dma_start(
            out=kff_h[ci, :, :, :], in_=kff_sb[:, :, :]),
            waits=[("sV", kff_done)], inc="kfw", inc_by=16)
        P.add("scalar", lambda e, ci=ci: e.dma_start(
            out=kft_h[ci, :, :, :, :], in_=kft_sb[:, :, :, :]),
            waits=[("sS", kft_done)], inc="qA", inc_by=16)

        for si in range(NSUB):
            gv = ci * NSUB + si
            vs = gv % 2
            w = [("sS", xnc_done[ci]), ("wwq", wv_ld), ("sV", va_ones)]
            if gv >= 2:
                w.append(("sS", psv_cp.get(gv - 2, 0)))
            for dc in range(16):
                ww = w if dc == 0 else ()
                P.add("tensor", lambda e, ci=ci, si=si, dc=dc, vs=vs: e.matmul(
                    psv[:, vs, :], xnfm[:, ci % 2, dc, si * 128:(si + 1) * 128],
                    wv_sb[:, dc, :], start=(dc == 0), stop=(dc == 15)),
                    waits=ww, inc=("sT" if dc == 15 else None))
            mv = P.cnt["sT"]
            last = 0
            for g in range(4):
                w2 = [("sT", mv)]
                if si == 0 and g == 0 and ci > 0:
                    w2.append(("qA", P.cnt["qA"]))
                last = P.add("scalar", lambda e, si=si, g=g, vs=vs: e.copy(
                    va_sb[:, si, g, 0:128], psv[:, vs, g * 128:(g + 1) * 128]),
                    waits=w2, inc="sS")
            psv_cp[gv] = last
        va_all = P.cnt["sS"]
        P.add("scalar", lambda e, ci=ci: e.dma_start(
            out=va_h[ci, :, :, :, :], in_=va_sb[:, :, :, :]),
            waits=[("sS", va_all)], inc="qA", inc_by=16)
        kw_q[ci] = P.cnt["qA"]

        for si in range(NSUB):
            for g in range(4):
                gi = ci * 16 + si * 4 + g
                pst = pstA if gi % 2 == 0 else pstB
                w = [("sS", va_all), ("sS", kft_done)]
                if gi >= 2:
                    w.append(("sV", st_addv.get(gi - 2, 0)))
                ms = P.add("tensor", lambda e, si=si, g=g, pst=pst: e.matmul(
                    pst[:, 0:129], kft_sb[:, si, g, :], va_sb[:, si, g, 0:129],
                    start=True, stop=True), waits=w, inc="sT")
                st_addv[gi] = P.add("vector", lambda e, g=g, pst=pst: e.tensor_tensor(
                    S_loc[:, g, 0:129], S_loc[:, g, 0:129], pst[:, 0:129], ALU.add),
                    waits=[("sT", ms)], inc="sV")
        vmm_last[ci] = P.cnt["sT"]

    p1_V = P.cnt["sV"]

    if debug and stop_after == "P1":
        w = [("sV", p1_V), ("sS", P.cnt["sS"]), ("qA", P.cnt["qA"])]
        for name, src in (("d_sloc", None),):
            pass
        P.add("gpsimd", lambda e: e.dma_start(out=dbg["d_sloc"][:, :, :], in_=S_loc[:]),
              waits=w, inc="qG", inc_by=16)
        for name, src in (("d_xnfm", xnfm_h), ("d_qf", qf_h), ("d_kff", kff_h)):
            P.add("gpsimd", lambda e, name=name, src=src: e.dma_start(
                out=dbg[name][:, :, :, :], in_=src[:, :, :, :]), inc="qG", inc_by=16)
        for name, src in (("d_kft", kft_h), ("d_va", va_h)):
            P.add("gpsimd", lambda e, name=name, src=src: e.dma_start(
                out=dbg[name][:, :, :, :, :], in_=src[:, :, :, :, :]), inc="qG", inc_by=16)
        P.wait_only("gpsimd", [("qG", P.cnt["qG"])])
        P.emit()
        ph.close(); per.close(); ctx.close()
        return nc, dbg

    P.add("gpsimd", lambda e: e.dma_start(
        out=cc_in_t.ap().rearrange("(g p) f -> p g f", p=128), in_=S_loc[:, :, :]),
        waits=[("sV", p1_V)], inc="qG", inc_by=16)
    cw = P.cnt["qG"]
    P.add("gpsimd", lambda e: e.collective_compute(
        "AllGather", ALU.bypass, replica_groups=[list(range(8))],
        ins=[cc_in_t.ap().opt()], outs=[cc_out_t.ap().opt()]),
        waits=[("qG", cw)], inc="sC")

    P.emit()
    ph.close()

    ph = contextlib.ExitStack()
    sb = lambda n, shp, dt: ph.enter_context(nc.sbuf_tensor(n, shp, dt))
    ps = lambda n, shp, dt: ph.enter_context(nc.psum_tensor(n, shp, dt))
    gath_sb = sb("gath_sb", [128, 8, 4, VW], F32)

    P.barrier()

    qbuf = sb("qbuf", [128, 2, 16, TC], BF16)
    kfb = sb("kfb", [128, 2, 4, TC], BF16)
    ktb = sb("ktb", [128, 2, NSUB, 4, 128], BF16)
    vab = sb("vab", [128, 2, NSUB, 4, VW], BF16)
    wo_sb = sb("wo_sb", [128, 16, 2048], BF16)
    abf = sb("abf", [128, 2, 4, 128], BF16)
    att_tm = sb("att_tm", [128, 2, 128], BF16)
    afm = sb("afm", [128, 2, 16, 128], BF16)
    st2 = sb("st2", [128, 2, 2], F32)
    y_sb = sb("y_sb", [128, 2, 2048], F32)

    pscr = ps("pscr", [128, 512], F32)
    pRp = ps("pRp", [128, 2, 512], F32)
    pTr = ps("pTr", [128, 1024], BF16)
    pW = ps("pW", [128, 2048], F32)

    msk_v = {}
    nrm_v = {}
    afc_s = {}
    yev_s = [0] * NSUBT
    yaw_q = [0] * NSUBT
    stadd_v = [0] * (NSUBT + 1)
    sbf_u = [0] * (NSUBT + 1)
    mmst_t = [0] * NSUBT
    wo_w = [0] * 16

    def p2_issue(cl):
        if cl >= NCH:
            return
        g2 = [("sT", mmst_t[(cl - 2) * NSUB + NSUB - 1])] if cl >= 2 else []
        P.add("sync", lambda e, cl=cl: e.dma_start(
            out=qbuf[:, cl % 2, 0:8, :], in_=qf_h[cl, :, 0:8, :]),
            waits=g2, inc="qfA", inc_by=16)
        if cl == 0:
            P.add("sync", lambda e: e.dma_start(
                out=gath_sb[:, :, :, :],
                in_=cc_out_t.ap().rearrange("(r g p) f -> p r g f", p=128, g=4)),
                waits=[("sC", P.cnt["sC"])], inc="glx", inc_by=16)
        P.add("sync", lambda e, cl=cl: e.dma_start(
            out=qbuf[:, cl % 2, 8:16, :], in_=qf_h[cl, :, 8:16, :]),
            inc="qfB", inc_by=16)
        P.add("scalar", lambda e, cl=cl: e.dma_start(
            out=kfb[:, cl % 2], in_=kff_h[cl]), waits=g2, inc="kfS", inc_by=16)
        P.add("scalar", lambda e, cl=cl: e.dma_start(
            out=vab[:, cl % 2], in_=va_h[cl]), inc="vaS", inc_by=16)
        P.add("gpsimd", lambda e, cl=cl: e.dma_start(
            out=ktb[:, cl % 2], in_=kft_h[cl]), waits=g2, inc="ktG", inc_by=16)

    for ci in range(NCH):
        if ci == 0:
            p2_issue(0)
            for h in range(16):
                wo_w[h] = P.add("gpsimd", lambda e, h=h: e.dma_start(
                    out=wo_sb[:, h, :], in_=wo[:, h, :]), inc="wwo", inc_by=16)
            p2_issue(1)
            pv = P.add("vector", lambda e: e.memset(S_sb[:], 0.0),
                       waits=[("glx", P.cnt["glx"])], inc="sV")
            for r in range(8):
                pv = P.add("vector", lambda e, r=r: e.scalar_tensor_tensor(
                    S_sb.ap().rearrange("p g f -> p (g f)"),
                    gath_sb[:, r, :, :].rearrange("p g f -> p (g f)"),
                    pm_sb[:, r:r + 1],
                    S_sb.ap().rearrange("p g f -> p (g f)"),
                    ALU.mult, ALU.add),
                    waits=[("cst", c_consts), ("sV", pv)], inc="sV")
            sbf_init = P.add("vector", lambda e: e.tensor_copy(S_bf[:], S_sb[:]),
                             waits=[("sV", pv)], inc="sV")
            stadd_v[0] = sbf_init
            sbf_u[0] = sbf_init
        else:
            p2_issue(ci + 1)

        for si in range(NSUB):
            u = ci * NSUB + si
            up = u % 2
            sl = slice(si * 128, (si + 1) * 128)
            mmW = 0

            def plan_tr_afc(h, u=u, up=up):
                hu = u * 16 + h
                hp = h % 2
                w = [("sV", nrm_v[hu])]
                if hu >= 1:
                    w.append(("sS", afc_s.get(hu - 1, 0)))
                tt = P.add("tensor", lambda e, hp=hp: e.transpose(
                    pTr[:, 0:128], att_tm[:, hp, :], ident_sb[:]),
                    waits=w, inc="sT")
                afc_s[hu] = P.add("scalar", lambda e, up=up, h=h: e.copy(
                    afm[:, up, h, :], pTr[:, 0:128]),
                    waits=[("sT", tt)], inc="sS")

            def plan_wo(h, u=u, up=up):
                nonlocal mmW
                w = [("wwo", wo_w[h]), ("sS", afc_s[u * 16 + h]), ("sV", stadd_v[u])]
                if h == 0 and u >= 1:
                    w.append(("sS", yev_s[u - 1]))
                first = True
                for os_ in range(4):
                    ww = w if first else ()
                    first = False
                    mmW = P.add("tensor", lambda e, up=up, h=h, os_=os_: e.matmul(
                        pW[:, os_ * 512:(os_ + 1) * 512], afm[:, up, h, :],
                        wo_sb[:, h, os_ * 512:(os_ + 1) * 512],
                        start=(h == 0), stop=(h == 15), skip_group_check=True),
                        waits=ww, inc=("sT" if (h == 15 and os_ == 3) else None))

            for h in range(16):
                g = h // 4
                hp = h % 2
                hu = u * 16 + h
                if h % 4 == 0:
                    w = [("qfA" if g < 2 else "qfB", 16 * (ci + 1)),
                         ("kfS", 16 * (ci + 1))]
                    qk = u * 4 + g
                    if qk >= 1:
                        w.append(("sV", msk_v.get(qk - 1, 0)))
                    sc = P.add("tensor", lambda e, ci=ci, g=g, sl=sl: e.matmul(
                        pscr[:, 0:512], kfb[:, ci % 2, g, sl],
                        qbuf[:, ci % 2, 4 * g:4 * g + 4, sl],
                        start=True, stop=True), waits=w, inc="sT")
                    msk_v[qk] = P.add("vector", lambda e, hp2=(g % 2): e.tensor_tensor(
                        abf[:, hp2, :, :], pscr.ap().rearrange("p (a f) -> p a f", a=4),
                        tril_sb.ap().rearrange("p (a f) -> p a f", a=4), ALU.mult),
                        waits=[("sT", sc)], inc="sV")
                w = [("sV", msk_v[u * 4 + g]), ("sV", sbf_u[u]),
                     ("vaS", 16 * (ci + 1)),
                     ("qfA" if h < 8 else "qfB", 16 * (ci + 1))]
                if hu >= 2:
                    w.append(("sV", nrm_v.get(hu - 2, 0)))
                P.add("tensor", lambda e, ci=ci, si=si, g=g, h=h, hp=hp: e.matmul(
                    pRp[:, hp, 0:129], abf[:, g % 2, h % 4, :],
                    vab[:, ci % 2, si, g, 0:129],
                    start=True, stop=False), waits=w)
                au = P.add("tensor", lambda e, ci=ci, h=h, g=g, hp=hp, sl=sl: e.matmul(
                    pRp[:, hp, 0:129], qbuf[:, ci % 2, h, sl], S_bf[:, g, 0:129],
                    start=False, stop=True), inc="sT")
                n1 = P.add("vector", lambda e, hp=hp: e.tensor_scalar(
                    st2[:, hp, 0:1], pRp[:, hp, 128:129], EPS_A, None, ALU.add),
                    waits=[("sT", au)], inc="sV")
                n2 = P.add("vector", lambda e, hp=hp: e.reciprocal(
                    st2[:, hp, 1:2], st2[:, hp, 0:1]), waits=[("sV", n1)], inc="sV")
                nrm_v[hu] = P.add("vector", lambda e, hp=hp: e.tensor_scalar(
                    att_tm[:, hp, :], pRp[:, hp, 0:128], st2[:, hp, 1:2], None, ALU.mult),
                    waits=[("sV", n2)], inc="sV")
                if h >= 1:
                    plan_tr_afc(h - 1)
                if h >= 2:
                    plan_wo(h - 2)
            plan_tr_afc(15)
            plan_wo(14)
            plan_wo(15)
            w = [("sT", mmW)]
            if u >= 2:
                w.append(("qA", P.cnt["qA"]))
            yev_s[u] = P.add("scalar", lambda e, up=up: e.copy(
                y_sb[:, up, :], pW[:]), waits=w, inc="sS")
            yaw_q[u] = P.add("scalar", lambda e, u=u, up=up: e.dma_start(
                out=ya_h[u], in_=y_sb[:, up, :]),
                waits=[("sS", yev_s[u])], inc="qA", inc_by=16)
            w = [("sS", yev_s[u]), ("ktG", 16 * (ci + 1)), ("vaS", 16 * (ci + 1))]
            mmst = 0
            for g in range(4):
                ww = w if g == 0 else ()
                mmst = P.add("tensor", lambda e, ci=ci, si=si, g=g: e.matmul(
                    pW[:, g * 512:g * 512 + 129], ktb[:, ci % 2, si, g, :],
                    vab[:, ci % 2, si, g, 0:129], start=True, stop=True),
                    waits=ww, inc=("sT" if g == 3 else None))
            mmst_t[u] = mmst
            for g in range(4):
                P.add("vector", lambda e, g=g: e.tensor_tensor(
                    S_sb[:, g, 0:129], S_sb[:, g, 0:129],
                    pW[:, g * 512:g * 512 + 129], ALU.add),
                    waits=[("sT", mmst)], inc="sV")
            stadd_v[u + 1] = P.cnt["sV"]
            sbf_u[u + 1] = P.add("vector", lambda e: e.tensor_copy(S_bf[:], S_sb[:]),
                                 waits=[("sV", stadd_v[u + 1])], inc="sV")

    if debug and stop_after == "P2":
        w = [("qA", P.cnt["qA"]), ("sV", P.cnt["sV"])]
        P.add("gpsimd", lambda e: e.dma_start(out=dbg["d_ya"][:, :, :], in_=ya_h[:, :, :]),
              waits=w, inc="qG", inc_by=16)
        P.add("gpsimd", lambda e: e.dma_start(out=dbg["d_sinit"][:, :, :], in_=S_sb[:]),
              inc="qG", inc_by=16)
        P.wait_only("gpsimd", [("qG", P.cnt["qG"])])
        P.emit()
        ph.close(); per.close(); ctx.close()
        return nc, dbg

    P.barrier()
    P.emit()
    ph.close()
    ph = contextlib.ExitStack()
    sb = lambda n, shp, dt: ph.enter_context(nc.sbuf_tensor(n, shp, dt))
    ps = lambda n, shp, dt: ph.enter_context(nc.psum_tensor(n, shp, dt))

    xnr = sb("xnr", [128, 16, NCH, TC], BF16)
    wgub = sb("wgub", [128, 3, 2, 16, 128], BF16)
    gsb = sb("gsb", [128, 2, NCH, TC], BF16)
    ssb3 = sb("ssb3", [128, 2, 512], F32)
    pg = ps("pg", [128, 2, 512], F32)
    pu = ps("pu", [128, 2, 512], F32)

    xnl = [0] * NCH
    for ci in range(NCH):
        xnl[ci] = P.add("sync", lambda e, ci=ci: e.dma_start(
            out=xnr[:, :, ci, :], in_=xnfm_h[ci]), inc=f"xnl{ci}", inc_by=16)

    wgl = [0] * NHC
    mmU_t = {}
    gmul_v = {}
    gw_q = [0] * NHC
    for hc in range(NHC):
        w = []
        if hc >= 3:
            w.append(("sT", mmU_t[(hc - 3, NCH - 1)]))
        wgl[hc] = P.add("gpsimd", lambda e, hc=hc: e.dma_start(
            out=wgub[:, hc % 3], in_=wgu[hc]), waits=w, inc=f"wg{hc % 3}", inc_by=16)
        for tt in range(NCH):
            tp = tt % 2
            pidx = (hc, tt - 2) if tt >= 2 else (hc - 1, tt - 2 + NCH)
            w = [(f"wg{hc % 3}", wgl[hc]), (f"xnl{tt}", xnl[tt])]
            if (hc, tt) != (0, 0):
                w.append(("sV", gmul_v.get(pidx, 0)))
            for dc in range(16):
                ww = w if dc == 0 else ()
                P.add("tensor", lambda e, hc=hc, tt=tt, dc=dc, tp=tp: e.matmul(
                    pg[:, tp, :], wgub[:, hc % 3, 0, dc, :], xnr[:, dc, tt, :],
                    start=(dc == 0), stop=(dc == 15)),
                    waits=ww, inc=("sT" if dc == 15 else None))
            mg = P.cnt["sT"]
            for dc in range(16):
                P.add("tensor", lambda e, hc=hc, tt=tt, dc=dc, tp=tp: e.matmul(
                    pu[:, tp, :], wgub[:, hc % 3, 1, dc, :], xnr[:, dc, tt, :],
                    start=(dc == 0), stop=(dc == 15)),
                    inc=("sT" if dc == 15 else None))
            mmU_t[(hc, tt)] = P.cnt["sT"]
            w2 = [("sT", mg)]
            if (hc, tt) != (0, 0):
                w2.append(("sV", gmul_v.get(pidx, 0)))
            sl3 = P.add("scalar", lambda e, tp=tp: e.activation(
                ssb3[:, tp, :], pg[:, tp, :], AF.Silu), waits=w2, inc="sS")
            w2 = [("sS", sl3), ("sT", mmU_t[(hc, tt)])]
            if hc >= 2 and tt == 0:
                w2.append((("gwa" if hc % 2 == 0 else "gwb"), gw_q[hc - 2]))
            gmul_v[(hc, tt)] = P.add("vector", lambda e, hc=hc, tt=tt, tp=tp: e.tensor_tensor(
                gsb[:, hc % 2, tt, :], ssb3[:, tp, :], pu[:, tp, :], ALU.mult),
                waits=w2, inc="sV")
        gw_q[hc] = P.add("scalar", lambda e, hc=hc: e.dma_start(
            out=gat_h[hc].rearrange("a p f -> p a f"), in_=gsb[:, hc % 2]),
            waits=[("sV", gmul_v[(hc, NCH - 1)])],
            inc=("gwa" if hc % 2 == 0 else "gwb"), inc_by=16)

    if debug and stop_after == "P3":
        P.add("gpsimd", lambda e: e.dma_start(
            out=dbg["d_gat"][:, :, :, :], in_=gat_h[:, :, :, :]),
            waits=[("qA", P.cnt["qA"])], inc="qG", inc_by=16)
        P.wait_only("gpsimd", [("qG", P.cnt["qG"])])
        P.emit()
        ph.close(); per.close(); ctx.close()
        return nc, dbg

    P.barrier()
    P.emit()
    ph.close()
    ph = contextlib.ExitStack()
    sb = lambda n, shp, dt: ph.enter_context(nc.sbuf_tensor(n, shp, dt))
    ps = lambda n, shp, dt: ph.enter_context(nc.psum_tensor(n, shp, dt))

    wdb = sb("wdb", [128, NHC, 1024], BF16)
    gcol = sb("gcol", [128, 2, NHC, 128], BF16)
    xb4 = sb("xb4", [128, 2, 1024], F32)
    yab4 = sb("yab4", [128, 2, 1024], F32)
    obA = sb("obA", [128, 2, 1024], F32)
    ob = sb("ob", [128, 2, 1024], F32)
    pY = ps("pY", [128, 2, 1024], F32)

    cmb_v = {}
    out_q = {}
    mm_last = {}
    for half in range(2):
        wdl = [0] * NHC
        for hc in range(NHC):
            w = []
            if half == 1 and hc == 0:
                w.append(("sT", mm_last[(0, NSUBT - 1)]))
            wdl[hc] = P.add("gpsimd", lambda e, half=half, hc=hc: e.dma_start(
                out=wdb[:, hc, :], in_=wd[half, hc]), waits=w, inc="wds", inc_by=16)
        for u4 in range(NSUBT):
            up = u4 % 2
            w = []
            pidx = (half, u4 - 2) if u4 >= 2 else (half - 1, u4 - 2 + NSUBT)
            if (half, u4) != (0, 0) and (half, u4) != (0, 1):
                w.append(("sT", mm_last[pidx]))
            gl = 0
            gsem = "gc0" if up == 0 else "gc1"
            w2 = list(w)
            for part in range(8):
                eng = "sync" if part < 4 else "scalar"
                gl = P.add(eng, lambda e, u4=u4, up=up, part=part: e.dma_start(
                    out=gcol[:, up, part * 8:(part + 1) * 8, :],
                    in_=gat_h[part * 8:(part + 1) * 8, u4 // NSUB, :,
                              (u4 % NSUB) * 128:(u4 % NSUB + 1) * 128].rearrange(
                                  "a p f -> p a f")),
                    waits=(w if part < 4 else w2), inc=gsem, inc_by=16)
                if part < 4:
                    w = []
                else:
                    w2 = []
            w = []
            if (half, u4) != (0, 0) and (half, u4) != (0, 1):
                w.append(("sV", cmb_v.get(pidx, 0)))
            l4sem = "ld40" if up == 0 else "ld41"
            P.add("sync", lambda e, half=half, u4=u4, up=up: e.dma_start(
                out=xb4[:, up, :],
                in_=x[u4 * 128:(u4 + 1) * 128, half * 1024:(half + 1) * 1024]),
                waits=w, inc=l4sem, inc_by=16)
            yl = P.add("sync", lambda e, half=half, u4=u4, up=up: e.dma_start(
                out=yab4[:, up, :],
                in_=ya_h[u4, :, half * 1024:(half + 1) * 1024]),
                inc=l4sem, inc_by=16)
            first = True
            mmY = 0
            for os_ in range(2):
                for hc in range(NHC):
                    ww = [("wds", wdl[NHC - 1])]
                    if first:
                        ww.append((gsem, gl))
                        if (half, u4) != (0, 0) and (half, u4) != (0, 1):
                            ww.append(("sV", cmb_v.get(pidx, 0)))
                        first = False
                    mmY = P.add("tensor", lambda e, up=up, hc=hc, os_=os_: e.matmul(
                        pY[:, up, os_ * 512:(os_ + 1) * 512],
                        gcol[:, up, hc, :],
                        wdb[:, hc, os_ * 512:(os_ + 1) * 512],
                        start=(hc == 0), stop=(hc == NHC - 1)),
                        waits=ww, inc=("sT" if (hc == NHC - 1 and os_ == 1) else None))
            mm_last[(half, u4)] = mmY
            P.add("vector", lambda e, up=up: e.tensor_tensor(
                obA[:, up, :], pY[:, up, :], xb4[:, up, :], ALU.add),
                waits=[("sT", mmY), (l4sem, yl)], inc="sV")
            osem = "ow0" if up == 0 else "ow1"
            w = []
            if (half, u4) != (0, 0) and (half, u4) != (0, 1):
                w.append((osem, out_q.get(pidx, 0)))
            cmb_v[(half, u4)] = P.add("vector", lambda e, up=up: e.tensor_tensor(
                ob[:, up, :], obA[:, up, :], yab4[:, up, :], ALU.add),
                waits=w, inc="sV")
            out_q[(half, u4)] = P.add("gpsimd", lambda e, half=half, u4=u4, up=up: e.dma_start(
                out=y[u4 * 128:(u4 + 1) * 128, half * 1024:(half + 1) * 1024],
                in_=ob[:, up, :]),
                waits=[("sV", cmb_v[(half, u4)])], inc=osem, inc_by=16)

    P.wait_only("gpsimd", [("ow0", out_q[(1, NSUBT - 2)]), ("ow1", out_q[(1, NSUBT - 1)])])
    P.emit()
    ph.close(); per.close(); ctx.close()
    return nc, dbg


def _pack_inputs(L0_attn_x, Wq, Wk, Wv, Wo, Wg, Wu, Wd):
    f32 = np.float32
    xs = np.ascontiguousarray(np.asarray(L0_attn_x, dtype=f32).reshape(B, 4, T, D))
    wq_p = np.ascontiguousarray(
        np.asarray(Wq, f32).reshape(16, 128, 16, 128).transpose(3, 2, 0, 1))
    wk_p = np.ascontiguousarray(
        np.asarray(Wk, f32).reshape(4, 128, 16, 128).transpose(3, 2, 0, 1))
    wv_p = np.ascontiguousarray(
        np.asarray(Wv, f32).T.reshape(16, 128, 512).transpose(1, 0, 2))
    wo_p = np.ascontiguousarray(
        np.asarray(Wo, f32).T.reshape(16, 128, 2048).transpose(1, 0, 2))
    wg_t = np.asarray(Wg, f32).reshape(NHC, 128, 16, 128).transpose(0, 3, 2, 1)
    wu_t = np.asarray(Wu, f32).reshape(NHC, 128, 16, 128).transpose(0, 3, 2, 1)
    wgu_p = np.ascontiguousarray(np.stack([wg_t, wu_t], axis=2))
    wd_p = np.ascontiguousarray(
        np.asarray(Wd, f32).reshape(2, 1024, NHC, 128).transpose(0, 2, 3, 1))

    inv = 1.0 / (10000.0 ** (np.arange(0, HD, 2, dtype=np.float64) / HD))
    tril = np.ascontiguousarray(
        np.tile(np.triu(np.ones((128, 128), f32)), (1, 4)))
    identity = np.ascontiguousarray(np.eye(128, dtype=ml_dtypes.bfloat16))

    in_maps = []
    for core in range(8):
        b, c = divmod(core, 4)
        pos = np.arange(c * T, (c + 1) * T, dtype=np.float64)
        ang = np.outer(pos, inv)
        cosv = np.cos(ang).astype(f32)
        sinv = np.sin(ang).astype(f32)
        cos_fm = np.ascontiguousarray(np.concatenate([cosv, cosv], axis=1).T)
        sin_fm = np.ascontiguousarray(np.concatenate([-sinv, sinv], axis=1).T)
        pm = np.zeros((128, 8), f32)
        for r in range(8):
            rb_, rc_ = divmod(r, 4)
            if rb_ == b and rc_ < c:
                pm[:, r] = 1.0
        in_maps.append({
            "x": np.ascontiguousarray(xs[b, c]),
            "wq": wq_p, "wk": wk_p, "wv": wv_p, "wo": wo_p,
            "wgu": wgu_p, "wd": wd_p,
            "cosf": cos_fm, "sinf": sin_fm,
            "trilm": tril, "ident": identity,
            "pmask": pm,
        })
    return in_maps


def kernel(L0_attn_x, L0_attn_Wq, L0_attn_Wk, L0_attn_Wv, L0_attn_Wo,
           L0_ffn_w_gate, L0_ffn_w_up, L0_ffn_w_down):
    in_maps = _pack_inputs(L0_attn_x, L0_attn_Wq, L0_attn_Wk, L0_attn_Wv,
                           L0_attn_Wo, L0_ffn_w_gate, L0_ffn_w_up, L0_ffn_w_down)
    nc, _ = build_program(stop_after="P4", debug=False)
    res = run_bass_kernel_spmd(nc, in_maps, core_ids=list(range(8)))
    out = np.empty((B, S, D), dtype=np.float32)
    for core in range(8):
        b, c = divmod(core, 4)
        out[b, c * T:(c + 1) * T, :] = res.results[core]["y"]
    return out


# version 18
# speedup vs baseline: 1.1754x; 1.0028x over previous
import contextlib
import numpy as np
import ml_dtypes
import concourse.bass as bass
import concourse.mybir as mybir
from concourse.bass_utils import run_bass_kernel_spmd

F32 = mybir.dt.float32
BF16 = mybir.dt.bfloat16
AF = mybir.ActivationFunctionType
ALU = mybir.AluOpType
AX = mybir.AxisListType

B, S, D = 2, 16384, 2048
H, G, HD = 16, 4, 128
DH = 8192
T = S // 4
TC = 512
NCH = T // TC
NSUB = TC // 128
NSUBT = T // 128
NHC = DH // 128
EPS_LN = 1e-5
EPS_A = 1e-6
VW = 136

ENGINES = ("sync", "scalar", "vector", "tensor", "gpsimd")


class Plan:

    def __init__(self, nc, sems):
        self.nc = nc
        self.sems = sems
        self.cnt = {k: 0 for k in sems}
        self.ops = {e: [] for e in ENGINES}
        self.lastw = {e: {} for e in ENGINES}

    def add(self, eng, fn, waits=(), inc=None, inc_by=1):
        ws = []
        for s, v in waits:
            if v <= 0:
                continue
            if v > self.lastw[eng].get(s, -1):
                ws.append((s, v))
                self.lastw[eng][s] = v
        post = None
        if inc is not None:
            self.cnt[inc] += inc_by
            post = self.cnt[inc]
        self.ops[eng].append((fn, tuple(ws), inc, inc_by))
        return post

    def wait_only(self, eng, waits):
        self.add(eng, None, waits=waits)

    def emit(self):
        with self.nc.Block() as block:
            for eng in ENGINES:
                lst = self.ops[eng]
                if not lst:
                    continue

                def body(e, lst=lst):
                    for fn, ws, inc, by in lst:
                        for s, v in ws:
                            e.wait_ge(self.sems[s], v)
                        if fn is None:
                            continue
                        inst = fn(e)
                        if inc is not None:
                            inst.then_inc(self.sems[inc], by)

                getattr(block, eng)(body)
        self.ops = {e: [] for e in ENGINES}

    def barrier(self):
        for eng, q, m in (("sync", "qS", "sC"), ("scalar", "qA", "sS"),
                          ("gpsimd", "qG", "sP")):
            if self.cnt[q] > 0:
                self.cnt[m] += 1
                self.add(eng, (lambda m: lambda e: e.sem_inc(self.sems[m], 1))(m),
                         waits=[(q, self.cnt[q])])
        snap = [(k, v) for k, v in self.cnt.items()
                if v > 0 and k in ("sT", "sV", "sS", "sP", "sC")]
        for e in ENGINES:
            self.wait_only(e, snap)


def build_program(stop_after="P4", debug=False):
    nc = bass.Bass("TRN2", target_bir_lowering=False, debug=False)
    ctx = contextlib.ExitStack()

    x = nc.dram_tensor("x", [T, D], F32, kind="ExternalInput").ap()
    wq = nc.dram_tensor("wq", [128, 16, 16, 128], F32, kind="ExternalInput").ap()
    wk = nc.dram_tensor("wk", [128, 16, 4, 128], F32, kind="ExternalInput").ap()
    wv = nc.dram_tensor("wv", [128, 16, 512], F32, kind="ExternalInput").ap()
    wo = nc.dram_tensor("wo", [128, 16, 2048], F32, kind="ExternalInput").ap()
    wgu = nc.dram_tensor("wgu", [NHC, 128, 2, 16, 128], F32, kind="ExternalInput").ap()
    wd = nc.dram_tensor("wd", [2, NHC, 128, 1024], F32, kind="ExternalInput").ap()
    cosf = nc.dram_tensor("cosf", [128, T], F32, kind="ExternalInput").ap()
    sinf = nc.dram_tensor("sinf", [128, T], F32, kind="ExternalInput").ap()
    trilm = nc.dram_tensor("trilm", [128, 512], F32, kind="ExternalInput").ap()
    ident = nc.dram_tensor("ident", [128, 128], BF16, kind="ExternalInput").ap()
    pmask = nc.dram_tensor("pmask", [128, 8], F32, kind="ExternalInput").ap()
    y = nc.dram_tensor("y", [T, D], F32, kind="ExternalOutput").ap()

    xnfm_h = nc.dram_tensor("xnfm_h", [NCH, 128, 16, TC], BF16).ap()
    qf_h = nc.dram_tensor("qf_h", [NCH, 128, 16, TC], BF16).ap()
    kff_h = nc.dram_tensor("kff_h", [NCH, 128, 4, TC], BF16).ap()
    kft_h = nc.dram_tensor("kft_h", [NCH, 128, NSUB, 4, 128], BF16).ap()
    va_h = nc.dram_tensor("va_h", [NCH, 128, NSUB, 4, VW], BF16).ap()
    ya_h = nc.dram_tensor("ya_h", [NSUBT, 128, 2048], F32).ap()
    gat_h = nc.dram_tensor("gat_h", [NHC, NCH, 128, TC], BF16).ap()
    wd_bf = nc.dram_tensor("wd_bf", [2, NHC, 128, 1024], BF16).ap()
    cc_in_t = nc.dram_tensor("cc_in", [4 * 128, VW], F32)
    cc_out_t = nc.dram_tensor("cc_out", [8 * 4 * 128, VW], F32)

    dbg = {}
    if debug:
        def dbg_out(name, shape, dt=F32):
            dbg[name] = nc.dram_tensor(name, shape, dt, kind="ExternalOutput").ap()
        if stop_after == "P1":
            dbg_out("d_xnfm", [NCH, 128, 16, TC], BF16)
            dbg_out("d_qf", [NCH, 128, 16, TC], BF16)
            dbg_out("d_kff", [NCH, 128, 4, TC], BF16)
            dbg_out("d_kft", [NCH, 128, NSUB, 4, 128], BF16)
            dbg_out("d_va", [NCH, 128, NSUB, 4, VW], BF16)
            dbg_out("d_sloc", [128, 4, VW], F32)
        if stop_after == "CC":
            dbg_out("d_sinit", [128, 4, VW], F32)
        if stop_after == "P2":
            dbg_out("d_ya", [NSUBT, 128, 2048], F32)
            dbg_out("d_sinit", [128, 4, VW], F32)
        if stop_after == "P3":
            dbg_out("d_gat", [NHC, NCH, 128, TC], BF16)

    sems = {}
    for sname in ("sT", "sV", "sS", "sP", "sC", "qS", "qA", "qG",
                  "cst", "wwq", "wwo", "wds", "glx", "wcl", "wcs",
                  "xl0", "xl1", "cs0", "cs1", "ls0", "ls1",
                  "qfw0", "qfw1", "wg0", "wg1", "wg2",
                  "gwa", "gwb", "gc0", "gc1", "ld40", "ld41",
                  "xnl0", "xnl1", "xnl2", "xnl3", "xnl4", "xnl5", "xnl6", "xnl7", "kfw",
                  "ow0", "ow1"):
        sems[sname] = ctx.enter_context(nc.semaphore(sname))
    P = Plan(nc, sems)

    per = contextlib.ExitStack()
    ident_sb = per.enter_context(nc.sbuf_tensor("ident_sb", [128, 128], BF16))
    tril_sb = per.enter_context(nc.sbuf_tensor("tril_sb", [128, 512], F32))
    pm_sb = per.enter_context(nc.sbuf_tensor("pm_sb", [128, 8], F32))
    S_sb = per.enter_context(nc.sbuf_tensor("S_sb", [128, 4, VW], F32))
    S_bf = per.enter_context(nc.sbuf_tensor("S_bf", [128, 4, VW], BF16))

    ph = contextlib.ExitStack()
    sb = lambda n, shp, dt: ph.enter_context(nc.sbuf_tensor(n, shp, dt))
    ps = lambda n, shp, dt: ph.enter_context(nc.psum_tensor(n, shp, dt))

    xsb = sb("xsb", [128, 2, D], F32)
    sqs = sb("sqs", [128, D], BF16)
    lnA = sb("lnA", [128, 2, 8], F32)
    xntm = sb("xntm", [128, 4, D], BF16)
    xnfm = sb("xnfm", [128, 2, 16, TC], BF16)
    cossb = sb("cossb", [128, 2, 2, TC], F32)
    wq_sb = sb("wq_sb", [128, 16, 16, 128], BF16)
    wk_sb = sb("wk_sb", [128, 16, 4, 128], BF16)
    wv_sb = sb("wv_sb", [128, 16, 512], BF16)
    qf_sb = sb("qf_sb", [128, 2, TC], BF16)
    kff_sb = sb("kff_sb", [128, 4, TC], BF16)
    kft_sb = sb("kft_sb", [128, NSUB, 4, 128], BF16)
    va_sb = sb("va_sb", [128, NSUB, 4, VW], BF16)
    S_loc = sb("S_loc", [128, 4, VW], F32)
    ra = sb("ra", [128, TC], F32)
    rb = sb("rb", [128, TC], F32)
    rc = sb("rc", [128, TC], F32)
    re = sb("re", [128, TC], F32)
    qev = sb("qev", [128, 2, TC], F32)

    psq = ps("psq", [128, 2, 512], F32)
    psv = ps("psv", [128, 2, 512], F32)
    pT = ps("pT", [128, 2, 1024], BF16)
    pstA = ps("pstA", [128, 512], F32)
    pstB = ps("pstB", [128, 512], F32)

    P.add("sync", lambda e: e.dma_start(out=ident_sb[:], in_=ident[:, :]), inc="cst", inc_by=16)
    P.add("sync", lambda e: e.dma_start(out=tril_sb[:], in_=trilm[:, :]), inc="cst", inc_by=16)
    P.add("sync", lambda e: e.dma_start(out=pm_sb[:], in_=pmask[:, :]), inc="cst", inc_by=16)
    c_consts = P.cnt["cst"]
    P.add("gpsimd", lambda e: e.dma_start(out=wq_sb[:], in_=wq[:, :, :, :]), inc="wwq", inc_by=16)
    P.add("gpsimd", lambda e: e.dma_start(out=wk_sb[:], in_=wk[:, :, :, :]), inc="wwq", inc_by=16)
    wv_ld = P.add("gpsimd", lambda e: e.dma_start(out=wv_sb[:], in_=wv[:, :, :]), inc="wwq", inc_by=16)
    P.add("vector", lambda e: e.memset(S_loc[:], 0.0), inc="sV")
    vz = P.add("vector", lambda e: e.memset(va_sb[:, :, :, 128:VW], 0.0), inc="sV")
    va_ones = P.add("vector", lambda e: e.memset(va_sb[:, :, :, 128:129], 1.0),
                    waits=[("sV", vz)], inc="sV")

    xload = [0] * NSUBT
    xn_done = [0] * NSUBT
    sq_done = [0] * NSUBT
    xnc_done = [0] * NCH
    xnfm_dma = [0] * NCH
    cos_ld = [0] * NCH
    cos_sem = [""] * NCH
    tr_last_of_sub = {}
    rope_prev = {}
    vmm_last = [0] * NCH
    kw_q = [0] * NCH
    kff_dma = [0] * NCH
    st_addv = {}
    psv_cp = {}
    qf_w = {}
    b2_v = {}
    tcopy = {}
    tr_cnt = 0
    hg_idx = 0
    rope_state = {"se": 0, "vc1": 0, "vc2": 0, "vb2": 0}

    def plan_rope_fmap(ci, slot_par, hg, final_fn, final_waits):
        cs = cossb[:, ci % 2, 0, :]
        sn = cossb[:, ci % 2, 1, :]
        mmv = P.cnt["sT"]
        pa2, pb2, _ = rope_prev.get(hg - 2, (0, 0, 0))
        s_ev = P.add("scalar", lambda e: e.copy(qev[:, slot_par, :], psq[:, slot_par, :]),
                     waits=[("sT", mmv), ("sP", pb2)], inc="sS")
        pw = [("sS", s_ev), (cos_sem[ci], cos_ld[ci])]
        if hg >= 1:
            pw.append(("sV", rope_state["vc1"]))
        va = P.add("gpsimd", lambda e: e.tensor_tensor(
            ra[:], qev[:, slot_par, :], cs, ALU.mult),
            waits=pw, inc="sP")
        P.add("vector", lambda e: e.tensor_tensor(
            rb[0:64, :], psq[64:128, slot_par, :], sn[0:64, :], ALU.mult),
            waits=[("sS", s_ev), (cos_sem[ci], cos_ld[ci]), ("sV", rope_state["vb2"])], inc="sV")
        b2 = P.add("vector", lambda e: e.tensor_tensor(
            rb[64:128, :], psq[0:64, slot_par, :], sn[64:128, :], ALU.mult),
            inc="sV")
        vc = P.add("vector", lambda e: e.tensor_tensor(
            rc[:], ra[:], rb[:, :], ALU.add),
            waits=[("sP", va)], inc="sV")
        vm = P.add("vector", lambda e: e.tensor_scalar(
            rb[:], rc[:], 0.0, None, ALU.min),
            waits=[("sS", rope_state["se"])], inc="sV")
        se = P.add("scalar", lambda e: e.activation(re[:], rb[:], AF.Exp),
                   waits=[("sV", vm)], inc="sS")
        ve = P.add("vector", lambda e_=None: None, waits=()) if False else None
        ve = P.add("vector", final_fn, waits=[("sS", se)] + list(final_waits), inc="sV")
        rope_state["se"] = se
        rope_state["vc2"] = rope_state.get("vc1", 0)
        rope_state["vc1"] = vc
        rope_state["vb2"] = b2
        rope_prev[hg] = (b2, va, s_ev)
        return s_ev, ve

    def plan_ln_sub(u):
        ci_, si_ = divmod(u, NSUB)
        p = u % 2
        w = []
        if u >= 2:
            w = [("sV", xn_done[u - 2]), ("sS", sq_done[u - 2])]
        xsem = "xl0" if p == 0 else "xl1"
        xload[u] = P.add("sync", lambda e, u=u, p=p: e.dma_start(
            out=xsb[:, p, :], in_=x[u * 128:(u + 1) * 128, :]),
            waits=w, inc=xsem, inc_by=16)
        P.add("scalar", lambda e, p=p: e.activation(
            sqs[:], xsb[:, p, :], AF.Identity, accum_out=lnA[:, p, 0:1]),
            waits=[(xsem, xload[u])], inc="sS")
        sq_done[u] = P.add("scalar", lambda e, p=p: e.activation(
            sqs[:], xsb[:, p, :], AF.Square, accum_out=lnA[:, p, 1:2]),
            waits=[(xsem, xload[u])], inc="sS")
        v1 = P.add("vector", lambda e, p=p: e.tensor_scalar(
            lnA[:, p, 2:3], lnA[:, p, 0:1], 1.0 / D, None, ALU.mult),
            waits=[("sS", sq_done[u])], inc="sV")
        v2 = P.add("vector", lambda e, p=p: e.tensor_scalar(
            lnA[:, p, 3:4], lnA[:, p, 1:2], 1.0 / D, None, ALU.mult),
            waits=[("sS", sq_done[u])], inc="sV")
        v3 = P.add("vector", lambda e, p=p: e.tensor_tensor(
            lnA[:, p, 7:8], lnA[:, p, 2:3], lnA[:, p, 2:3], ALU.mult),
            waits=[("sV", v1)], inc="sV")
        v4 = P.add("vector", lambda e, p=p: e.scalar_tensor_tensor(
            lnA[:, p, 4:5], lnA[:, p, 3:4], EPS_LN, lnA[:, p, 7:8],
            ALU.add, ALU.subtract), waits=[("sV", v3)], inc="sV")
        sl_ = P.add("scalar", lambda e, p=p: e.activation(
            lnA[:, p, 7:8], lnA[:, p, 4:5], AF.Ln),
            waits=[("sV", v4)], inc="sS")
        s2 = P.add("scalar", lambda e, p=p: e.activation(
            lnA[:, p, 5:6], lnA[:, p, 7:8], AF.Exp, scale=-0.5),
            waits=[("sS", sl_)], inc="sS")
        v5 = P.add("vector", lambda e, p=p: e.tensor_tensor(
            lnA[:, p, 6:7], lnA[:, p, 2:3], lnA[:, p, 5:6], ALU.mult),
            waits=[("sS", s2)], inc="sV")
        w = [("sV", v5)]
        if u >= 4:
            assert (u - 4) in tr_last_of_sub
            w.append(("sT", tr_last_of_sub[u - 4]))
        xn_done[u] = P.add("vector", lambda e, p=p, p4=u % 4: e.tensor_scalar(
            xntm[:, p4, :], xsb[:, p, :], lnA[:, p, 5:6], lnA[:, p, 6:7],
            ALU.mult, ALU.subtract), waits=w, inc="sV")

    pending_qf = []

    def flush_qf(n=None):
        while pending_qf and (n is None or len(pending_qf) > n):
            ci_, h_, qslot_, ve_ = pending_qf.pop(0)
            qf_w[ci_ * 16 + h_] = P.add(
                "scalar", lambda e, ci=ci_, h=h_, qslot=qslot_: e.dma_start(
                    out=qf_h[ci, :, h, :], in_=qf_sb[:, qslot, :]),
                waits=[("sV", ve_)], inc=("qfw0" if qslot_ == 0 else "qfw1"),
                inc_by=16)

    for si in range(NSUB):
        plan_ln_sub(si)

    for ci in range(NCH):
        cp = ci % 2
        w = []
        if ci >= 2:
            pa, pb, _ = rope_prev.get((ci - 2) * 20 + 19, (0, 0, 0))
            w = [("sV", pa), ("sP", pb)]
        csem = "cs0" if ci % 2 == 0 else "cs1"
        P.add("sync", lambda e, ci=ci: e.dma_start(
            out=cossb[:, ci % 2, 0, :], in_=cosf[:, ci * TC:(ci + 1) * TC]),
            waits=w, inc=csem, inc_by=16)
        cos_ld[ci] = P.add("sync", lambda e, ci=ci: e.dma_start(
            out=cossb[:, ci % 2, 1, :], in_=sinf[:, ci * TC:(ci + 1) * TC]),
            inc=csem, inc_by=16)
        cos_sem[ci] = csem

        for si in range(NSUB):
            u = ci * NSUB + si
            p = u % 2
            for dc in range(16):
                tp = tr_cnt % 2
                w = [("sV", xn_done[u]), ("cst", c_consts)]
                if tr_cnt >= 2:
                    w.append(("sS", tcopy[tr_cnt - 2]))
                tt = P.add("tensor", lambda e, p4=u % 4, dc=dc, tp=tp: e.transpose(
                    pT[:, tp, 0:128], xntm[:, p4, dc * 128:(dc + 1) * 128], ident_sb[:]),
                    waits=w, inc="sT")
                w2 = [("sT", tt)]
                if si == 0 and dc == 0 and ci >= 2:
                    w2.append(("qA", kw_q[ci - 1]))
                tcopy[tr_cnt] = P.add("scalar", lambda e, ci=ci, dc=dc, si=si, tp=tp: e.copy(
                    xnfm[:, ci % 2, dc, si * 128:(si + 1) * 128], pT[:, tp, 0:128]),
                    waits=w2, inc="sS")
                tr_cnt += 1
            tr_last_of_sub[u] = P.cnt["sT"]
        xnc_done[ci] = P.cnt["sS"]
        xnfm_dma[ci] = P.add("scalar", lambda e, ci=ci: e.dma_start(
            out=xnfm_h[ci, :, :, :], in_=xnfm[:, ci % 2, :, :]),
            waits=[("sS", xnc_done[ci])], inc="qA", inc_by=16)

        for h in range(16):
            qh_idx = ci * 16 + h
            slot = hg_idx % 2
            qslot = qh_idx % 2
            flush_qf(1)
            w = [("sS", xnc_done[ci]), ("wwq", wv_ld)]
            if hg_idx >= 2:
                sv2, pb2v, se2 = rope_prev.get(hg_idx - 2, (0, 0, 0))
                w.append(("sS", se2))
                w.append(("sV", sv2))
            for dc in range(16):
                ww = w if dc == 0 else ()
                P.add("tensor", lambda e, ci=ci, h=h, dc=dc, slot=slot: e.matmul(
                    psq[:, slot, :], wq_sb[:, dc, h, :], xnfm[:, ci % 2, dc, :],
                    start=(dc == 0), stop=(dc == 15)),
                    waits=ww, inc=("sT" if dc == 15 else None))
            fw = [(("qfw0" if qslot == 0 else "qfw1"), qf_w.get(qh_idx - 2, 0))] \
                if qh_idx >= 2 else []
            b2, ve = plan_rope_fmap(
                ci, slot, hg_idx,
                (lambda qslot: lambda e: e.scalar_tensor_tensor(
                    qf_sb[:, qslot, :], rc[:], 0.0, re[:], ALU.max, ALU.add))(qslot), fw)
            b2_v[hg_idx] = b2
            pending_qf.append((ci, h, qslot, ve))
            hg_idx += 1
            if ci + 1 < NCH and h in (4, 6, 8, 10):
                plan_ln_sub((ci + 1) * NSUB + (h - 4) // 2)

        for g in range(4):
            slot = hg_idx % 2
            flush_qf(1)
            w = [("sS", xnc_done[ci]), ("wwq", wv_ld)]
            if hg_idx >= 2:
                sv2, pb2v, se2 = rope_prev.get(hg_idx - 2, (0, 0, 0))
                w.append(("sS", se2))
                w.append(("sV", sv2))
            for dc in range(16):
                ww = w if dc == 0 else ()
                P.add("tensor", lambda e, ci=ci, g=g, dc=dc, slot=slot: e.matmul(
                    psq[:, slot, :], wk_sb[:, dc, g, :], xnfm[:, ci % 2, dc, :],
                    start=(dc == 0), stop=(dc == 15)),
                    waits=ww, inc=("sT" if dc == 15 else None))
            kwf = [("kfw", kff_dma[ci - 1])] if (g == 0 and ci > 0) else []
            b2, ve = plan_rope_fmap(
                ci, slot, hg_idx,
                (lambda g: lambda e: e.scalar_tensor_tensor(
                    kff_sb[:, g, :], rc[:], 0.0, re[:], ALU.max, ALU.add))(g), kwf)
            b2_v[hg_idx] = b2
            hg_idx += 1
        kff_done = P.cnt["sV"]
        flush_qf(0)
        for g in range(4):
            for si in range(NSUB):
                tp = tr_cnt % 2
                w = [("sV", kff_done)]
                if tr_cnt >= 2:
                    w.append(("sS", tcopy[tr_cnt - 2]))
                tt = P.add("tensor", lambda e, g=g, si=si, tp=tp: e.transpose(
                    pT[:, tp, 0:128], kff_sb[:, g, si * 128:(si + 1) * 128], ident_sb[:]),
                    waits=w, inc="sT")
                w2 = [("sT", tt)]
                if g == 0 and si == 0 and ci > 0:
                    w2.append(("qA", P.cnt["qA"]))
                tcopy[tr_cnt] = P.add("scalar", lambda e, g=g, si=si, tp=tp: e.copy(
                    kft_sb[:, si, g, :], pT[:, tp, 0:128]),
                    waits=w2, inc="sS")
                tr_cnt += 1
        kft_done = P.cnt["sS"]
        kff_dma[ci] = P.add("scalar", lambda e, ci=ci: e.dma_start(
            out=kff_h[ci, :, :, :], in_=kff_sb[:, :, :]),
            waits=[("sV", kff_done)], inc="kfw", inc_by=16)
        P.add("scalar", lambda e, ci=ci: e.dma_start(
            out=kft_h[ci, :, :, :, :], in_=kft_sb[:, :, :, :]),
            waits=[("sS", kft_done)], inc="qA", inc_by=16)

        for si in range(NSUB):
            gv = ci * NSUB + si
            vs = gv % 2
            w = [("sS", xnc_done[ci]), ("wwq", wv_ld), ("sV", va_ones)]
            if gv >= 2:
                w.append(("sS", psv_cp.get(gv - 2, 0)))
            for dc in range(16):
                ww = w if dc == 0 else ()
                P.add("tensor", lambda e, ci=ci, si=si, dc=dc, vs=vs: e.matmul(
                    psv[:, vs, :], xnfm[:, ci % 2, dc, si * 128:(si + 1) * 128],
                    wv_sb[:, dc, :], start=(dc == 0), stop=(dc == 15)),
                    waits=ww, inc=("sT" if dc == 15 else None))
            mv = P.cnt["sT"]
            last = 0
            for g in range(4):
                w2 = [("sT", mv)]
                if si == 0 and g == 0 and ci > 0:
                    w2.append(("qA", P.cnt["qA"]))
                last = P.add("scalar", lambda e, si=si, g=g, vs=vs: e.copy(
                    va_sb[:, si, g, 0:128], psv[:, vs, g * 128:(g + 1) * 128]),
                    waits=w2, inc="sS")
            psv_cp[gv] = last
        va_all = P.cnt["sS"]
        P.add("scalar", lambda e, ci=ci: e.dma_start(
            out=va_h[ci, :, :, :, :], in_=va_sb[:, :, :, :]),
            waits=[("sS", va_all)], inc="qA", inc_by=16)
        kw_q[ci] = P.cnt["qA"]

        for si in range(NSUB):
            for g in range(4):
                gi = ci * 16 + si * 4 + g
                pst = pstA if gi % 2 == 0 else pstB
                w = [("sS", va_all), ("sS", kft_done)]
                if gi >= 2:
                    w.append(("sV", st_addv.get(gi - 2, 0)))
                ms = P.add("tensor", lambda e, si=si, g=g, pst=pst: e.matmul(
                    pst[:, 0:129], kft_sb[:, si, g, :], va_sb[:, si, g, 0:129],
                    start=True, stop=True), waits=w, inc="sT")
                st_addv[gi] = P.add("vector", lambda e, g=g, pst=pst: e.tensor_tensor(
                    S_loc[:, g, 0:129], S_loc[:, g, 0:129], pst[:, 0:129], ALU.add),
                    waits=[("sT", ms)], inc="sV")
        vmm_last[ci] = P.cnt["sT"]

    p1_V = P.cnt["sV"]

    if debug and stop_after == "P1":
        w = [("sV", p1_V), ("sS", P.cnt["sS"]), ("qA", P.cnt["qA"])]
        for name, src in (("d_sloc", None),):
            pass
        P.add("gpsimd", lambda e: e.dma_start(out=dbg["d_sloc"][:, :, :], in_=S_loc[:]),
              waits=w, inc="qG", inc_by=16)
        for name, src in (("d_xnfm", xnfm_h), ("d_qf", qf_h), ("d_kff", kff_h)):
            P.add("gpsimd", lambda e, name=name, src=src: e.dma_start(
                out=dbg[name][:, :, :, :], in_=src[:, :, :, :]), inc="qG", inc_by=16)
        for name, src in (("d_kft", kft_h), ("d_va", va_h)):
            P.add("gpsimd", lambda e, name=name, src=src: e.dma_start(
                out=dbg[name][:, :, :, :, :], in_=src[:, :, :, :, :]), inc="qG", inc_by=16)
        P.wait_only("gpsimd", [("qG", P.cnt["qG"])])
        P.emit()
        ph.close(); per.close(); ctx.close()
        return nc, dbg

    P.add("gpsimd", lambda e: e.dma_start(
        out=cc_in_t.ap().rearrange("(g p) f -> p g f", p=128), in_=S_loc[:, :, :]),
        waits=[("sV", p1_V)], inc="qG", inc_by=16)
    cw = P.cnt["qG"]
    P.add("gpsimd", lambda e: e.collective_compute(
        "AllGather", ALU.bypass, replica_groups=[list(range(8))],
        ins=[cc_in_t.ap().opt()], outs=[cc_out_t.ap().opt()]),
        waits=[("qG", cw)], inc="sC")

    P.emit()
    ph.close()

    ph = contextlib.ExitStack()
    sb = lambda n, shp, dt: ph.enter_context(nc.sbuf_tensor(n, shp, dt))
    ps = lambda n, shp, dt: ph.enter_context(nc.psum_tensor(n, shp, dt))
    gath_sb = sb("gath_sb", [128, 8, 4, VW], F32)

    P.barrier()
    gl = P.add("gpsimd", lambda e: e.dma_start(
        out=gath_sb[:, :, :, :],
        in_=cc_out_t.ap().rearrange("(r g p) f -> p r g f", p=128, g=4)),
        waits=[("sC", P.cnt["sC"])], inc="glx", inc_by=16)
    pv = P.add("vector", lambda e: e.memset(S_sb[:], 0.0), waits=[("glx", gl)], inc="sV")
    for r in range(8):
        pv = P.add("vector", lambda e, r=r: e.scalar_tensor_tensor(
            S_sb.ap().rearrange("p g f -> p (g f)"),
            gath_sb[:, r, :, :].rearrange("p g f -> p (g f)"),
            pm_sb[:, r:r + 1],
            S_sb.ap().rearrange("p g f -> p (g f)"),
            ALU.mult, ALU.add),
            waits=[("cst", c_consts), ("sV", pv)], inc="sV")
    sbf_init = P.add("vector", lambda e: e.tensor_copy(S_bf[:], S_sb[:]),
                     waits=[("sV", pv)], inc="sV")

    if debug and stop_after == "CC":
        P.add("gpsimd", lambda e: e.dma_start(out=dbg["d_sinit"][:, :, :], in_=S_sb[:]),
              waits=[("sV", sbf_init)], inc="qG", inc_by=16)
        P.wait_only("gpsimd", [("qG", P.cnt["qG"])])
        P.emit()
        ph.close(); per.close(); ctx.close()
        return nc, dbg

    qbuf = sb("qbuf", [128, 2, 16, TC], BF16)
    kfb = sb("kfb", [128, 2, 4, TC], BF16)
    ktb = sb("ktb", [128, 2, NSUB, 4, 128], BF16)
    vab = sb("vab", [128, 2, NSUB, 4, VW], BF16)
    wo_sb = sb("wo_sb", [128, 16, 2048], BF16)
    abf = sb("abf", [128, 2, 4, 128], BF16)
    att_tm = sb("att_tm", [128, 2, 128], BF16)
    afm = sb("afm", [128, 2, 16, 128], BF16)
    st2 = sb("st2", [128, 2, 2], F32)
    y_sb = sb("y_sb", [128, 2, 2048], F32)

    pscr = ps("pscr", [128, 512], F32)
    pRp = ps("pRp", [128, 2, 512], F32)
    pTr = ps("pTr", [128, 1024], BF16)
    pW = ps("pW", [128, 2048], F32)

    wo_ld = P.add("gpsimd", lambda e: e.dma_start(out=wo_sb[:], in_=wo[:, :, :]),
                  inc="wwo", inc_by=16)

    ld2 = [0] * NCH
    ld2_sem = [""] * NCH
    msk_v = {}
    nrm_v = {}
    afc_s = {}
    yev_s = [0] * NSUBT
    yaw_q = [0] * NSUBT
    stadd_v = [0] * (NSUBT + 1)
    stadd_v[0] = sbf_init
    sbf_u = [0] * (NSUBT + 1)
    sbf_u[0] = sbf_init
    mmst_t = [0] * NSUBT

    for ci in range(NCH):
        w = []
        if ci >= 2:
            w.append(("sT", mmst_t[(ci - 2) * NSUB + NSUB - 1]))
        lsem = "ls0" if ci % 2 == 0 else "ls1"
        for fn in (
            lambda e, ci=ci: e.dma_start(out=qbuf[:, ci % 2], in_=qf_h[ci]),
            lambda e, ci=ci: e.dma_start(out=kfb[:, ci % 2], in_=kff_h[ci]),
            lambda e, ci=ci: e.dma_start(out=ktb[:, ci % 2], in_=kft_h[ci]),
            lambda e, ci=ci: e.dma_start(out=vab[:, ci % 2], in_=va_h[ci]),
        ):
            ld2[ci] = P.add("sync", fn, waits=w, inc=lsem, inc_by=16)
            w = []
        ld2_sem[ci] = lsem

        for si in range(NSUB):
            u = ci * NSUB + si
            up = u % 2
            sl = slice(si * 128, (si + 1) * 128)
            mmW = 0

            def plan_tr_afc(h, u=u, up=up):
                hu = u * 16 + h
                hp = h % 2
                w = [("sV", nrm_v[hu])]
                if hu >= 1:
                    w.append(("sS", afc_s.get(hu - 1, 0)))
                tt = P.add("tensor", lambda e, hp=hp: e.transpose(
                    pTr[:, 0:128], att_tm[:, hp, :], ident_sb[:]),
                    waits=w, inc="sT")
                afc_s[hu] = P.add("scalar", lambda e, up=up, h=h: e.copy(
                    afm[:, up, h, :], pTr[:, 0:128]),
                    waits=[("sT", tt)], inc="sS")

            def plan_wo(h, u=u, up=up):
                nonlocal mmW
                w = [("wwo", wo_ld), ("sS", afc_s[u * 16 + h]), ("sV", stadd_v[u])]
                if h == 0 and u >= 1:
                    w.append(("sS", yev_s[u - 1]))
                first = True
                for os_ in range(4):
                    ww = w if first else ()
                    first = False
                    mmW = P.add("tensor", lambda e, up=up, h=h, os_=os_: e.matmul(
                        pW[:, os_ * 512:(os_ + 1) * 512], afm[:, up, h, :],
                        wo_sb[:, h, os_ * 512:(os_ + 1) * 512],
                        start=(h == 0), stop=(h == 15), skip_group_check=True),
                        waits=ww, inc=("sT" if (h == 15 and os_ == 3) else None))

            for h in range(16):
                g = h // 4
                hp = h % 2
                hu = u * 16 + h
                if h % 4 == 0:
                    w = [(ld2_sem[ci], ld2[ci])]
                    qk = u * 4 + g
                    if qk >= 1:
                        w.append(("sV", msk_v.get(qk - 1, 0)))
                    sc = P.add("tensor", lambda e, ci=ci, g=g, sl=sl: e.matmul(
                        pscr[:, 0:512], kfb[:, ci % 2, g, sl],
                        qbuf[:, ci % 2, 4 * g:4 * g + 4, sl],
                        start=True, stop=True), waits=w, inc="sT")
                    msk_v[qk] = P.add("vector", lambda e, hp2=(g % 2): e.tensor_tensor(
                        abf[:, hp2, :, :], pscr.ap().rearrange("p (a f) -> p a f", a=4),
                        tril_sb.ap().rearrange("p (a f) -> p a f", a=4), ALU.mult),
                        waits=[("sT", sc)], inc="sV")
                w = [("sV", msk_v[u * 4 + g]), ("sV", sbf_u[u])]
                if hu >= 2:
                    w.append(("sV", nrm_v.get(hu - 2, 0)))
                P.add("tensor", lambda e, ci=ci, si=si, g=g, h=h, hp=hp: e.matmul(
                    pRp[:, hp, 0:129], abf[:, g % 2, h % 4, :],
                    vab[:, ci % 2, si, g, 0:129],
                    start=True, stop=False), waits=w)
                au = P.add("tensor", lambda e, ci=ci, h=h, g=g, hp=hp, sl=sl: e.matmul(
                    pRp[:, hp, 0:129], qbuf[:, ci % 2, h, sl], S_bf[:, g, 0:129],
                    start=False, stop=True), inc="sT")
                n1 = P.add("vector", lambda e, hp=hp: e.tensor_scalar(
                    st2[:, hp, 0:1], pRp[:, hp, 128:129], EPS_A, None, ALU.add),
                    waits=[("sT", au)], inc="sV")
                n2 = P.add("vector", lambda e, hp=hp: e.reciprocal(
                    st2[:, hp, 1:2], st2[:, hp, 0:1]), waits=[("sV", n1)], inc="sV")
                nrm_v[hu] = P.add("vector", lambda e, hp=hp: e.tensor_scalar(
                    att_tm[:, hp, :], pRp[:, hp, 0:128], st2[:, hp, 1:2], None, ALU.mult),
                    waits=[("sV", n2)], inc="sV")
                if h >= 1:
                    plan_tr_afc(h - 1)
                if h >= 2:
                    plan_wo(h - 2)
            plan_tr_afc(15)
            plan_wo(14)
            plan_wo(15)
            w = [("sT", mmW)]
            if u >= 2:
                w.append(("qA", P.cnt["qA"]))
            yev_s[u] = P.add("scalar", lambda e, up=up: e.copy(
                y_sb[:, up, :], pW[:]), waits=w, inc="sS")
            yaw_q[u] = P.add("scalar", lambda e, u=u, up=up: e.dma_start(
                out=ya_h[u], in_=y_sb[:, up, :]),
                waits=[("sS", yev_s[u])], inc="qA", inc_by=16)
            w = [("sS", yev_s[u]), (ld2_sem[ci], ld2[ci])]
            mmst = 0
            for g in range(4):
                ww = w if g == 0 else ()
                mmst = P.add("tensor", lambda e, ci=ci, si=si, g=g: e.matmul(
                    pW[:, g * 512:g * 512 + 129], ktb[:, ci % 2, si, g, :],
                    vab[:, ci % 2, si, g, 0:129], start=True, stop=True),
                    waits=ww, inc=("sT" if g == 3 else None))
            mmst_t[u] = mmst
            for g in range(4):
                P.add("vector", lambda e, g=g: e.tensor_tensor(
                    S_sb[:, g, 0:129], S_sb[:, g, 0:129],
                    pW[:, g * 512:g * 512 + 129], ALU.add),
                    waits=[("sT", mmst)], inc="sV")
            stadd_v[u + 1] = P.cnt["sV"]
            sbf_u[u + 1] = P.add("vector", lambda e: e.tensor_copy(S_bf[:], S_sb[:]),
                                 waits=[("sV", stadd_v[u + 1])], inc="sV")

    if debug and stop_after == "P2":
        w = [("qA", P.cnt["qA"]), ("sV", P.cnt["sV"])]
        P.add("gpsimd", lambda e: e.dma_start(out=dbg["d_ya"][:, :, :], in_=ya_h[:, :, :]),
              waits=w, inc="qG", inc_by=16)
        P.add("gpsimd", lambda e: e.dma_start(out=dbg["d_sinit"][:, :, :], in_=S_sb[:]),
              inc="qG", inc_by=16)
        P.wait_only("gpsimd", [("qG", P.cnt["qG"])])
        P.emit()
        ph.close(); per.close(); ctx.close()
        return nc, dbg

    P.barrier()
    P.emit()
    ph.close()
    ph = contextlib.ExitStack()
    sb = lambda n, shp, dt: ph.enter_context(nc.sbuf_tensor(n, shp, dt))
    ps = lambda n, shp, dt: ph.enter_context(nc.psum_tensor(n, shp, dt))

    xnr = sb("xnr", [128, 16, NCH, TC], BF16)
    wgub = sb("wgub", [128, 3, 2, 16, 128], BF16)
    gsb = sb("gsb", [128, 2, NCH, TC], BF16)
    ssb3 = sb("ssb3", [128, 2, 512], F32)
    wstg = sb("wstg", [128, 8, 1024], BF16)
    pg = ps("pg", [128, 2, 512], F32)
    pu = ps("pu", [128, 2, 512], F32)

    xnl = [0] * NCH
    for ci in range(NCH):
        xnl[ci] = P.add("sync", lambda e, ci=ci: e.dma_start(
            out=xnr[:, :, ci, :], in_=xnfm_h[ci]), inc=f"xnl{ci}", inc_by=16)

    wgl = [0] * NHC
    mmU_t = {}
    gmul_v = {}
    gw_q = [0] * NHC
    for hc in range(NHC):
        w = []
        if hc >= 3:
            w.append(("sT", mmU_t[(hc - 3, NCH - 1)]))
        wgl[hc] = P.add("gpsimd", lambda e, hc=hc: e.dma_start(
            out=wgub[:, hc % 3], in_=wgu[hc]), waits=w, inc=f"wg{hc % 3}", inc_by=16)
        for j in (0, 1):
            r = hc * 2 + j
            lw = [("wcs", 16 * (r - 7))] if r >= 8 else []
            P.add("gpsimd", lambda e, r=r: e.dma_start(
                out=wstg[:, r % 8, :], in_=wd[r // NHC, r % NHC]),
                waits=lw, inc="wcl", inc_by=16)
            P.add("sync", lambda e, r=r: e.dma_start(
                out=wd_bf[r // NHC, r % NHC], in_=wstg[:, r % 8, :]),
                waits=[("wcl", 16 * (r + 1))], inc="wcs", inc_by=16)
        for tt in range(NCH):
            tp = tt % 2
            pidx = (hc, tt - 2) if tt >= 2 else (hc - 1, tt - 2 + NCH)
            w = [(f"wg{hc % 3}", wgl[hc]), (f"xnl{tt}", xnl[tt])]
            if (hc, tt) != (0, 0):
                w.append(("sV", gmul_v.get(pidx, 0)))
            for dc in range(16):
                ww = w if dc == 0 else ()
                P.add("tensor", lambda e, hc=hc, tt=tt, dc=dc, tp=tp: e.matmul(
                    pg[:, tp, :], wgub[:, hc % 3, 0, dc, :], xnr[:, dc, tt, :],
                    start=(dc == 0), stop=(dc == 15)),
                    waits=ww, inc=("sT" if dc == 15 else None))
            mg = P.cnt["sT"]
            for dc in range(16):
                P.add("tensor", lambda e, hc=hc, tt=tt, dc=dc, tp=tp: e.matmul(
                    pu[:, tp, :], wgub[:, hc % 3, 1, dc, :], xnr[:, dc, tt, :],
                    start=(dc == 0), stop=(dc == 15)),
                    inc=("sT" if dc == 15 else None))
            mmU_t[(hc, tt)] = P.cnt["sT"]
            w2 = [("sT", mg)]
            if (hc, tt) != (0, 0):
                w2.append(("sV", gmul_v.get(pidx, 0)))
            sl3 = P.add("scalar", lambda e, tp=tp: e.activation(
                ssb3[:, tp, :], pg[:, tp, :], AF.Silu), waits=w2, inc="sS")
            w2 = [("sS", sl3), ("sT", mmU_t[(hc, tt)])]
            if hc >= 2 and tt == 0:
                w2.append((("gwa" if hc % 2 == 0 else "gwb"), gw_q[hc - 2]))
            gmul_v[(hc, tt)] = P.add("vector", lambda e, hc=hc, tt=tt, tp=tp: e.tensor_tensor(
                gsb[:, hc % 2, tt, :], ssb3[:, tp, :], pu[:, tp, :], ALU.mult),
                waits=w2, inc="sV")
        gw_q[hc] = P.add("scalar", lambda e, hc=hc: e.dma_start(
            out=gat_h[hc].rearrange("a p f -> p a f"), in_=gsb[:, hc % 2]),
            waits=[("sV", gmul_v[(hc, NCH - 1)])],
            inc=("gwa" if hc % 2 == 0 else "gwb"), inc_by=16)

    if debug and stop_after == "P3":
        P.add("gpsimd", lambda e: e.dma_start(
            out=dbg["d_gat"][:, :, :, :], in_=gat_h[:, :, :, :]),
            waits=[("qA", P.cnt["qA"])], inc="qG", inc_by=16)
        P.wait_only("gpsimd", [("qG", P.cnt["qG"])])
        P.emit()
        ph.close(); per.close(); ctx.close()
        return nc, dbg

    P.barrier()
    P.emit()
    ph.close()
    ph = contextlib.ExitStack()
    sb = lambda n, shp, dt: ph.enter_context(nc.sbuf_tensor(n, shp, dt))
    ps = lambda n, shp, dt: ph.enter_context(nc.psum_tensor(n, shp, dt))

    wdb = sb("wdb", [128, NHC, 1024], BF16)
    gcol = sb("gcol", [128, 2, NHC, 128], BF16)
    xb4 = sb("xb4", [128, 2, 1024], F32)
    yab4 = sb("yab4", [128, 2, 1024], F32)
    obA = sb("obA", [128, 2, 1024], F32)
    ob = sb("ob", [128, 2, 1024], F32)
    pY = ps("pY", [128, 2, 1024], F32)

    cmb_v = {}
    out_q = {}
    mm_last = {}
    for half in range(2):
        wdl = [0] * NHC
        for hc in range(NHC):
            eng = ("sync", "scalar", "gpsimd")[hc % 3]
            w = []
            if hc < 3:
                w.append(("wcs", 16 * NHC * (half + 1)))
                if half == 1:
                    w.append(("sT", mm_last[(0, NSUBT - 1)]))
            wdl[hc] = P.add(eng, lambda e, half=half, hc=hc: e.dma_start(
                out=wdb[:, hc, :], in_=wd_bf[half, hc]), waits=w, inc="wds", inc_by=16)
        for u4 in range(NSUBT):
            up = u4 % 2
            w = []
            pidx = (half, u4 - 2) if u4 >= 2 else (half - 1, u4 - 2 + NSUBT)
            if (half, u4) != (0, 0) and (half, u4) != (0, 1):
                w.append(("sT", mm_last[pidx]))
            gl = 0
            gsem = "gc0" if up == 0 else "gc1"
            w2 = list(w)
            for part in range(8):
                eng = "sync" if part < 4 else "scalar"
                gl = P.add(eng, lambda e, u4=u4, up=up, part=part: e.dma_start(
                    out=gcol[:, up, part * 8:(part + 1) * 8, :],
                    in_=gat_h[part * 8:(part + 1) * 8, u4 // NSUB, :,
                              (u4 % NSUB) * 128:(u4 % NSUB + 1) * 128].rearrange(
                                  "a p f -> p a f")),
                    waits=(w if part < 4 else w2), inc=gsem, inc_by=16)
                if part < 4:
                    w = []
                else:
                    w2 = []
            w = []
            if (half, u4) != (0, 0) and (half, u4) != (0, 1):
                w.append(("sV", cmb_v.get(pidx, 0)))
            l4sem = "ld40" if up == 0 else "ld41"
            P.add("sync", lambda e, half=half, u4=u4, up=up: e.dma_start(
                out=xb4[:, up, :],
                in_=x[u4 * 128:(u4 + 1) * 128, half * 1024:(half + 1) * 1024]),
                waits=w, inc=l4sem, inc_by=16)
            yl = P.add("sync", lambda e, half=half, u4=u4, up=up: e.dma_start(
                out=yab4[:, up, :],
                in_=ya_h[u4, :, half * 1024:(half + 1) * 1024]),
                inc=l4sem, inc_by=16)
            first = True
            mmY = 0
            for os_ in range(2):
                for hc in range(NHC):
                    ww = [("wds", wdl[NHC - 1])]
                    if first:
                        ww.append((gsem, gl))
                        if (half, u4) != (0, 0) and (half, u4) != (0, 1):
                            ww.append(("sV", cmb_v.get(pidx, 0)))
                        first = False
                    mmY = P.add("tensor", lambda e, up=up, hc=hc, os_=os_: e.matmul(
                        pY[:, up, os_ * 512:(os_ + 1) * 512],
                        gcol[:, up, hc, :],
                        wdb[:, hc, os_ * 512:(os_ + 1) * 512],
                        start=(hc == 0), stop=(hc == NHC - 1)),
                        waits=ww, inc=("sT" if (hc == NHC - 1 and os_ == 1) else None))
            mm_last[(half, u4)] = mmY
            P.add("vector", lambda e, up=up: e.tensor_tensor(
                obA[:, up, :], pY[:, up, :], xb4[:, up, :], ALU.add),
                waits=[("sT", mmY), (l4sem, yl)], inc="sV")
            osem = "ow0" if up == 0 else "ow1"
            w = []
            if (half, u4) != (0, 0) and (half, u4) != (0, 1):
                w.append((osem, out_q.get(pidx, 0)))
            cmb_v[(half, u4)] = P.add("vector", lambda e, up=up: e.tensor_tensor(
                ob[:, up, :], obA[:, up, :], yab4[:, up, :], ALU.add),
                waits=w, inc="sV")
            out_q[(half, u4)] = P.add("gpsimd", lambda e, half=half, u4=u4, up=up: e.dma_start(
                out=y[u4 * 128:(u4 + 1) * 128, half * 1024:(half + 1) * 1024],
                in_=ob[:, up, :]),
                waits=[("sV", cmb_v[(half, u4)])], inc=osem, inc_by=16)

    P.wait_only("gpsimd", [("ow0", out_q[(1, NSUBT - 2)]), ("ow1", out_q[(1, NSUBT - 1)])])
    P.emit()
    ph.close(); per.close(); ctx.close()
    return nc, dbg


def _pack_inputs(L0_attn_x, Wq, Wk, Wv, Wo, Wg, Wu, Wd):
    f32 = np.float32
    xs = np.ascontiguousarray(np.asarray(L0_attn_x, dtype=f32).reshape(B, 4, T, D))
    wq_p = np.ascontiguousarray(
        np.asarray(Wq, f32).reshape(16, 128, 16, 128).transpose(3, 2, 0, 1))
    wk_p = np.ascontiguousarray(
        np.asarray(Wk, f32).reshape(4, 128, 16, 128).transpose(3, 2, 0, 1))
    wv_p = np.ascontiguousarray(
        np.asarray(Wv, f32).T.reshape(16, 128, 512).transpose(1, 0, 2))
    wo_p = np.ascontiguousarray(
        np.asarray(Wo, f32).T.reshape(16, 128, 2048).transpose(1, 0, 2))
    wg_t = np.asarray(Wg, f32).reshape(NHC, 128, 16, 128).transpose(0, 3, 2, 1)
    wu_t = np.asarray(Wu, f32).reshape(NHC, 128, 16, 128).transpose(0, 3, 2, 1)
    wgu_p = np.ascontiguousarray(np.stack([wg_t, wu_t], axis=2))
    wd_p = np.ascontiguousarray(
        np.asarray(Wd, f32).reshape(2, 1024, NHC, 128).transpose(0, 2, 3, 1))

    inv = 1.0 / (10000.0 ** (np.arange(0, HD, 2, dtype=np.float64) / HD))
    tril = np.ascontiguousarray(
        np.tile(np.triu(np.ones((128, 128), f32)), (1, 4)))
    identity = np.ascontiguousarray(np.eye(128, dtype=ml_dtypes.bfloat16))

    in_maps = []
    for core in range(8):
        b, c = divmod(core, 4)
        pos = np.arange(c * T, (c + 1) * T, dtype=np.float64)
        ang = np.outer(pos, inv)
        cosv = np.cos(ang).astype(f32)
        sinv = np.sin(ang).astype(f32)
        cos_fm = np.ascontiguousarray(np.concatenate([cosv, cosv], axis=1).T)
        sin_fm = np.ascontiguousarray(np.concatenate([-sinv, sinv], axis=1).T)
        pm = np.zeros((128, 8), f32)
        for r in range(8):
            rb_, rc_ = divmod(r, 4)
            if rb_ == b and rc_ < c:
                pm[:, r] = 1.0
        in_maps.append({
            "x": np.ascontiguousarray(xs[b, c]),
            "wq": wq_p, "wk": wk_p, "wv": wv_p, "wo": wo_p,
            "wgu": wgu_p, "wd": wd_p,
            "cosf": cos_fm, "sinf": sin_fm,
            "trilm": tril, "ident": identity,
            "pmask": pm,
        })
    return in_maps


def kernel(L0_attn_x, L0_attn_Wq, L0_attn_Wk, L0_attn_Wv, L0_attn_Wo,
           L0_ffn_w_gate, L0_ffn_w_up, L0_ffn_w_down):
    in_maps = _pack_inputs(L0_attn_x, L0_attn_Wq, L0_attn_Wk, L0_attn_Wv,
                           L0_attn_Wo, L0_ffn_w_gate, L0_ffn_w_up, L0_ffn_w_down)
    nc, _ = build_program(stop_after="P4", debug=False)
    res = run_bass_kernel_spmd(nc, in_maps, core_ids=list(range(8)))
    out = np.empty((B, S, D), dtype=np.float32)
    for core in range(8):
        b, c = divmod(core, 4)
        out[b, c * T:(c + 1) * T, :] = res.results[core]["y"]
    return out
